# version 04
# baseline (speedup 1.0000x reference)
import math
import numpy as np
from contextlib import ExitStack

import concourse.bass as bass
import concourse.mybir as mybir
from concourse.bass_utils import run_bass_kernel_spmd

F32 = mybir.dt.float32
BF16 = mybir.dt.bfloat16
U8 = mybir.dt.uint8
AF = mybir.ActivationFunctionType
ALU = mybir.AluOpType

B, S, D, H, DH, FF = 4, 4096, 2048, 16, 128, 8192
_PH5_DOWN = True
TOK = 2048
TFULL = 4096
NDC = D // 128
NCTRL = 130156
STEP64 = (NCTRL - 1) / (D * FF - 1)
NREC = 256
MAGIC = 12582912.0
TWO_PI = float(2.0 * np.pi)
LN_EPS = 1e-5

_bfd = None
def _bf16(x):
    global _bfd
    if _bfd is None:
        import ml_dtypes
        _bfd = ml_dtypes.bfloat16
    return np.asarray(x, np.float64).astype(_bfd).astype(np.float64)


_HOSTC = None

def _host_consts():
    global _HOSTC
    if _HOSTC is not None:
        return _HOSTC
    d = {}
    halos = np.zeros((3, 5, 128, 1024), np.int64)
    f0col = np.zeros((3, 128, 1024), np.float32)
    betacol = np.zeros((3, 128, 1024), np.float32)
    nbrows = np.zeros((3, NREC, 2, 512), np.float32)
    for w, (O, I) in enumerate([(FF, D), (FF, D), (D, FF)]):
        nic = I // 128
        rec = np.arange(NREC)
        ic = rec % nic
        ob = rec // nic
        j = np.arange(512)
        n0 = (ob[:, None] * 512 + j[None, :]).astype(np.int64) * I + (ic[:, None] * 128)
        t0 = n0.astype(np.float64) * STEP64
        sA = np.minimum(np.floor(t0).astype(np.int64), NCTRL - 2)
        f0 = t0 - sA
        beta = (1.0 - f0) / STEP64
        pp = (rec // 2)[:, None].repeat(512, 1)
        xx = (rec % 2)[:, None] * 512 + j[None, :]
        for a in range(5):
            hid = np.clip(sA - 1 + a, 0, NCTRL - 1)
            halos[w, a, pp, xx] = hid
        f0col[w, pp, xx] = f0.astype(np.float32)
        betacol[w, pp, xx] = beta.astype(np.float32)
        nbh = _bf16(-beta)
        nbl = _bf16(-beta - nbh)
        nbrows[w, :, 0, :] = nbh.astype(np.float32)
        nbrows[w, :, 1, :] = nbl.astype(np.float32)
    d["halo_idx"] = halos
    d["f0col"] = f0col
    d["betacol"] = betacol
    d["nbrows"] = nbrows
    pv = np.arange(128, dtype=np.float64)
    basis = np.stack([pv**m for m in range(4)], axis=0)
    bh = _bf16(basis)
    bl = _bf16(basis - bh)
    cb = np.zeros((12, 128), np.float32)
    for m in range(4):
        cb[3 * m + 0] = bh[m]
        cb[3 * m + 1] = bl[m]
        cb[3 * m + 2] = bh[m]
    d["cbasis"] = cb
    ab = np.zeros((5, 128), np.float32)
    ab[0] = 1.0
    ab[1] = 1.0
    ab[2] = bh[1]
    ab[3] = bl[1]
    ab[4] = bh[1]
    d["abasis"] = ab
    d["ones2"] = np.ones((2, 128), np.float32)
    d["iotacol"] = np.arange(128, dtype=np.float32)[:, None].copy()
    d["onescol"] = np.ones((128, 1), np.float32)
    d["ones1f"] = np.ones((1, 128), np.float32)
    d["ones128"] = np.ones((128, 128), np.float32)
    d["epscol"] = np.full((128, 1), 1e-5, np.float32)
    _HOSTC = d
    return d


def _core_in_map(c, inp, hc):
    b, half = c // 2, c % 2
    x = np.asarray(inp["L0_attn_x"])
    xb = x[b]
    own = xb[half * TOK:(half + 1) * TOK]
    oth = xb[(1 - half) * TOK:(2 - half) * TOK]
    xT = np.ascontiguousarray(np.concatenate([own, oth], axis=0).T)
    m = {"xT": xT}
    for nm, key in [("wqT", "L0_attn_Wq"), ("wkT", "L0_attn_Wk"),
                    ("wvT", "L0_attn_Wv"), ("woT", "L0_attn_Wo")]:
        m[nm] = np.ascontiguousarray(np.asarray(inp[key]).T)
    for nm, key in [("bq2d", "L0_attn_bq"), ("bk2d", "L0_attn_bk"),
                    ("bo2d", "L0_attn_bo"), ("hel2d", "L0_attn_helix_rotation")]:
        m[nm] = np.ascontiguousarray(np.asarray(inp[key]).reshape(H, DH).T)
    m["bv_row"] = np.ascontiguousarray(np.asarray(inp["L0_attn_bv"]).reshape(1, D))
    m["spiral_row"] = np.ascontiguousarray(np.asarray(inp["L0_attn_spiral_weight"]).reshape(1, H))
    cps = [np.asarray(inp["L0_ffn_gate_cp"]), np.asarray(inp["L0_ffn_up_cp"]),
           np.asarray(inp["L0_ffn_down_cp"])]
    halo = np.zeros((3, 5, 128, 1024), np.float32)
    for w in range(3):
        halo[w] = np.take(cps[w], hc["halo_idx"][w])
    m["halo"] = halo
    for nm in ["f0col", "betacol", "nbrows", "cbasis", "abasis", "ones2",
               "iotacol", "onescol", "ones1f", "ones128", "epscol"]:
        m[nm] = hc[nm]
    return m


class _Ctx:
    pass


class _S:
    def __init__(self, h):
        self.h = h
        self.n = 0

    def inc(self, instr, k=1):
        instr.then_inc(self.h, k)
        self.n += k
        return self.n


def _build(dump=(), upto=9):
    nc = bass.Bass("TRN2", target_bir_lowering=False, debug=False)
    t = {}

    def dram(name, shape, dtype, kind="Internal"):
        if name in dump:
            kind = "ExternalOutput"
        t[name] = nc.dram_tensor(name, list(shape), dtype, kind=kind)

    def din(name, shape, dtype=F32):
        t[name] = nc.dram_tensor(name, list(shape), dtype, kind="ExternalInput")

    din("xT", [D, TFULL])
    for nm in ["wqT", "wkT", "wvT", "woT"]:
        din(nm, [D, D])
    for nm in ["bq2d", "bk2d", "bo2d", "hel2d"]:
        din(nm, [128, H])
    din("bv_row", [1, D])
    din("spiral_row", [1, H])
    din("halo", [3, 5, 128, 1024])
    din("f0col", [3, 128, 1024])
    din("betacol", [3, 128, 1024])
    din("nbrows", [3, NREC, 2, 512])
    din("cbasis", [12, 128])
    din("abasis", [5, 128])
    din("ones2", [2, 128])
    din("iotacol", [128, 1])
    din("onescol", [128, 1])
    din("ones1f", [1, 128])
    din("ones128", [128, 128])
    din("epscol", [128, 1])

    dram("recsC", [3, NREC, 12, 512], BF16)
    dram("recsA", [3, NREC, 5, 512], BF16)
    dram("wgg", [64, 16, 128, 128], BF16)
    dram("wgu", [64, 16, 128, 128], BF16)
    dram("wgd", [16, 64, 128, 128], BF16)
    dram("KT", [D, TFULL], BF16)
    dram("Vt", [H, TFULL // 128, 128, 128], BF16)
    dram("QT", [D, TOK], BF16)
    dram("z1", [D, TOK], F32)
    dram("h1f", [D, TOK], F32)
    dram("z2", [D, TOK], F32)
    dram("dbgt", [128, 160], F32)
    dram("outT", [D, TOK], F32, kind="ExternalOutput")

    ctx = _Ctx()
    ctx.nc = nc
    ctx.t = t

    with ExitStack() as es:
        E = es.enter_context
        sbd = {}

        def sbuf(name, shape, dtype):
            sbd[name] = E(nc.sbuf_tensor(name, list(shape), dtype))

        sbuf("s_cb", [12, 128], BF16)
        sbuf("s_ab", [5, 128], BF16)
        sbuf("s_ones2", [2, 128], BF16)
        sbuf("s_iota", [128, 1], F32)
        sbuf("s_onescol", [128, 1], F32)
        sbuf("s_ones1f", [1, 128], F32)
        sbuf("s_onesall", [128, 128], F32)
        sbuf("s_eps", [128, 1], F32)
        sbuf("s_ones1b", [1, 128], BF16)
        sbuf("s_hel", [128, H], F32)
        sbuf("s_bq", [128, H], F32)
        sbuf("s_bk", [128, H], F32)
        sbuf("s_bo", [128, H], F32)
        sbuf("s_sp", [1, H], F32)
        for nm in ["tb_r", "tb_sin", "tb_s2", "tb_sq2", "tb_cos", "tb_qb",
                   "tb_ks", "tb_kb", "tb_t1", "tb_sb"]:
            sbuf(nm, [128, H], F32)
        sbuf("tb_sr", [1, H], F32)
        sbuf("rowsA", [128, TOK], F32)
        sbuf("rowsB", [128, TOK], F32)

        sbuf("arA", [128, 65536], U8)
        sbuf("arB", [128, 65536], U8)
        sbuf("arC", [128, 49152], U8)

        def carve(ar, off, dtype, n, rearr=None, **kw):
            esz = 2 if dtype == BF16 else 4
            ap = sbd[ar][:, off:off + n * esz].bitcast(dtype)
            if rearr:
                ap = ap.rearrange(rearr, **kw)
            return ap

        ctx.carve = carve
        ctx.sb = sbd

        pb = [E(nc.psum_tensor(f"pb{i}", [128, 512], F32)) for i in range(6)]
        psS1 = E(nc.psum_tensor("psS1", [1, 512], F32))
        psS2 = E(nc.psum_tensor("psS2", [1, 512], F32))
        ctx.pb, ctx.psS1, ctx.psS2 = pb, psS1, psS2

        sems = {}
        for nm in ["cl", "cg", "tabv", "tabs", "tmm",
                   "p1l", "p1v", "p1s",
                   "wls", "wln", "wmm", "wact", "wve", "wst",
                   "q_xl", "q_wl", "q_mm", "q_ev", "q_st",
                   "a_kv", "a_q", "a_s", "a_x", "a_v", "a_d", "a_f",
                   "o_wl", "o_mm", "o_ev", "o_xl", "o_z", "o_sq", "o_st",
                   "o_sc", "o_s2",
                   "l_zl", "l_ap", "l_hb",
                   "f_wl", "f_mm", "f_sil", "f_dwl", "f_dmm", "f_ev",
                   "f_hl", "f_z2", "f_sq", "f_st", "f_sc", "f_s2", "f_mu",
                   "g_zl", "g_ap", "g_st", "g_bc", "g_cp", "vsf", "asf"]:
            sems[nm] = _S(E(nc.semaphore(nm)))
        ctx.S = sems

        with nc.Block() as block:
            phases = [_ph0, _ph1_prep, _ph1_main, _ph2_qkv, _ph3_attn,
                      _ph4_wo_ln1, _ph5_ffn, _ph6_out]
            for pi, ph in enumerate(phases):
                if pi <= upto:
                    ph(ctx, block)
            if upto < len(phases) - 1:
                def _tail(g):
                    for nm in ["p1s", "wst", "q_st", "o_st", "f_st", "g_st"]:
                        if sems[nm].n:
                            g.wait_ge(sems[nm].h, sems[nm].n)
                block.gpsimd(_tail)

    return nc, ctx


def _vs(ctx, v, instr):
    n = ctx.S["vsf"].inc(instr)
    v.wait_ge(ctx.S["vsf"].h, n)


def _as(ctx, s, instr):
    n = ctx.S["asf"].inc(instr)
    s.wait_ge(ctx.S["asf"].h, n)


def _ph0(ctx, block):
    t, sb, S, pb = ctx.t, ctx.sb, ctx.S, ctx.pb
    s_bv = ctx.carve("arA", 49152, BF16, D)[0:1, :]
    ctx.s_bv = s_bv

    def g_loads(g):
        for nm, dst in [("cbasis", "s_cb"), ("abasis", "s_ab"), ("ones2", "s_ones2")]:
            S["cg"].inc(g.dma_start(out=sb[dst][:], in_=t[nm][:, :]), 16)
        S["cg"].inc(g.dma_start(out=sb["s_ones1b"][:], in_=t["ones1f"][:, :]), 16)
        S["cg"].inc(g.dma_start(out=s_bv, in_=t["bv_row"][:, :]), 16)
    block.gpsimd(g_loads)

    def s_loads(sy):
        for nm, dst in [("iotacol", "s_iota"), ("onescol", "s_onescol"),
                        ("ones1f", "s_ones1f"), ("ones128", "s_onesall"), ("epscol", "s_eps"), ("hel2d", "s_hel"), ("bq2d", "s_bq"),
                        ("bk2d", "s_bk"), ("bo2d", "s_bo"), ("spiral_row", "s_sp")]:
            S["cl"].inc(sy.dma_start(out=sb[dst][:], in_=t[nm][:, :]), 16)
    block.sync(s_loads)

    def v1(v):
        v.wait_ge(S["cl"].h, S["cl"].n)
        BIG = 1e30
        _vs(ctx, v, v.tensor_scalar(sb["tb_t1"][:], sb["s_hel"][:], float(np.pi), BIG,
                                    op0=ALU.subtract, op1=ALU.mult))
        _vs(ctx, v, v.tensor_scalar(sb["tb_t1"][:], sb["tb_t1"][:], 0.0, 1.0,
                                    op0=ALU.max, op1=ALU.min))
        _vs(ctx, v, v.tensor_scalar(sb["tb_s2"][:], sb["s_hel"][:], float(-np.pi), -BIG,
                                    op0=ALU.subtract, op1=ALU.mult))
        _vs(ctx, v, v.tensor_scalar(sb["tb_s2"][:], sb["tb_s2"][:], 0.0, 1.0,
                                    op0=ALU.max, op1=ALU.min))
        _vs(ctx, v, v.tensor_tensor(sb["tb_t1"][:], sb["tb_t1"][:], sb["tb_s2"][:], op=ALU.subtract))
        _vs(ctx, v, v.scalar_tensor_tensor(sb["tb_r"][:], sb["tb_t1"][:], -TWO_PI, sb["s_hel"][:],
                                           op0=ALU.mult, op1=ALU.add))
        S["tabv"].inc(v.tensor_scalar(sb["tb_r"][:], sb["tb_r"][:], float(np.pi),
                                      float(-np.pi), op0=ALU.min, op1=ALU.max))
    block.vector(v1)

    def a1(s):
        s.wait_ge(S["tabv"].h, S["tabv"].n)
        _as(ctx, s, s.activation(sb["tb_sin"][:], sb["tb_r"][:], AF.Sin, bias=0.0, scale=1.0))
        _as(ctx, s, s.activation(sb["tb_s2"][:], sb["tb_r"][:], AF.Sin, bias=0.0, scale=0.5))
        S["tabs"].inc(s.activation(sb["tb_sq2"][:], sb["tb_s2"][:], AF.Square,
                                   bias=0.0, scale=1.0))
    block.scalar(a1)

    def v2(v):
        v.wait_ge(S["tabs"].h, S["tabs"].n)
        _vs(ctx, v, v.tensor_scalar(sb["tb_cos"][:], sb["tb_sq2"][:], -2.0, 1.0, op0=ALU.mult, op1=ALU.add))
        _vs(ctx, v, v.tensor_tensor(sb["tb_qb"][:], sb["tb_cos"][:], sb["s_bq"][:], op=ALU.mult))
        S["tabv"].inc(v.tensor_scalar(sb["tb_sr"][:], sb["s_sp"][:],
                                      float(1.0 / math.sqrt(DH)), 0.0,
                                      op0=ALU.mult, op1=ALU.add))
    block.vector(v2)

    def t1(te):
        te.wait_ge(S["tabv"].h, S["tabv"].n)
        S["tmm"].inc(te.matmul(pb[0][:, 0:H], sb["s_ones1f"][:], sb["tb_sr"][:],
                               start=True, stop=True))
    block.tensor(t1)

    def a2(s):
        s.wait_ge(S["tmm"].h, S["tmm"].n)
        S["tabs"].inc(s.copy(sb["tb_sb"][:], pb[0][:, 0:H]))
    block.scalar(a2)

    def v3(v):
        v.wait_ge(S["tabs"].h, S["tabs"].n)
        _vs(ctx, v, v.tensor_tensor(sb["tb_ks"][:], sb["tb_sin"][:], sb["tb_sb"][:], op=ALU.mult))
        S["tabv"].inc(v.tensor_tensor(sb["tb_kb"][:], sb["tb_ks"][:], sb["s_bk"][:],
                                      op=ALU.mult))
    block.vector(v3)

    def gdbg(g):
        g.wait_ge(S["tabv"].h, S["tabv"].n)
        for i, nm in enumerate(["tb_cos", "tb_qb", "tb_ks", "tb_kb", "s_hel", "tb_r", "tb_sin", "tb_s2", "tb_sq2", "tb_sb"]):
            S["cg"].inc(g.dma_start(out=t["dbgt"][:, i * H:(i + 1) * H], in_=sb[nm][:]), 16)
    block.gpsimd(gdbg)


def _ph1_prep(ctx, block):
    t, sb, S, carve = ctx.t, ctx.sb, ctx.S, ctx.carve
    d = float(STEP64)
    d2, d3 = d * d, d * d * d

    IN = [carve("arB", i * 4096, F32, 1024) for i in range(7)]
    TMP = [carve("arB", (7 + i) * 4096, F32, 1024) for i in range(8)]
    OUT = [carve("arC", i * 2048, BF16, 1024) for i in range(12)]

    for w in range(3):
        base_l = S["p1l"].n

        def loads(sy, w=w):
            if w > 0:
                sy.wait_ge(S["p1v"].h, 6 * w)
            for a in range(5):
                S["p1l"].inc(sy.dma_start(out=IN[a], in_=t["halo"][w, a]), 16)
            S["p1l"].inc(sy.dma_start(out=IN[5], in_=t["f0col"][w]), 16)
            S["p1l"].inc(sy.dma_start(out=IN[6], in_=t["betacol"][w]), 16)
        block.sync(loads)

        stores = []

        def vops(v, w=w, base_l=base_l, stores=stores):
            v.wait_ge(S["p1l"].h, base_l + 7 * 16)
            if w > 0:
                v.wait_ge(S["p1s"].h, 17 * 16 * w)
            h0, h1, h2, h3, h4, f0, bet = IN
            Bc, C, Dc, Pp, Q, t1, t2, _x = TMP
            v.tensor_tensor(t1, h2, h0, op=ALU.subtract)
            v.tensor_scalar(Bc, t1, 0.5, 0.0, op0=ALU.mult, op1=ALU.add)
            v.scalar_tensor_tensor(t1, h1, -2.5, h0, op0=ALU.mult, op1=ALU.add)
            v.scalar_tensor_tensor(t1, h3, -0.5, t1, op0=ALU.mult, op1=ALU.add)
            v.scalar_tensor_tensor(C, h2, 2.0, t1, op0=ALU.mult, op1=ALU.add)
            v.scalar_tensor_tensor(t1, h1, -3.0, h0, op0=ALU.mult, op1=ALU.add)
            v.scalar_tensor_tensor(t1, h2, 3.0, t1, op0=ALU.mult, op1=ALU.add)
            v.scalar_tensor_tensor(t1, h3, -1.0, t1, op0=ALU.mult, op1=ALU.add)
            v.tensor_scalar(Dc, t1, -0.5, 0.0, op0=ALU.mult, op1=ALU.add)
            v.scalar_tensor_tensor(t1, h2, -2.5, h1, op0=ALU.mult, op1=ALU.add)
            v.scalar_tensor_tensor(t1, h4, -0.5, t1, op0=ALU.mult, op1=ALU.add)
            v.scalar_tensor_tensor(t1, h3, 2.0, t1, op0=ALU.mult, op1=ALU.add)
            v.tensor_tensor(t1, t1, C, op=ALU.subtract)
            v.scalar_tensor_tensor(Pp, Dc, -3.0, t1, op0=ALU.mult, op1=ALU.add)
            v.scalar_tensor_tensor(t1, h2, -3.0, h1, op0=ALU.mult, op1=ALU.add)
            v.scalar_tensor_tensor(t1, h3, 3.0, t1, op0=ALU.mult, op1=ALU.add)
            v.scalar_tensor_tensor(t1, h4, -1.0, t1, op0=ALU.mult, op1=ALU.add)
            v.tensor_scalar(t1, t1, -0.5, 0.0, op0=ALU.mult, op1=ALU.add)
            v.tensor_tensor(Q, t1, Dc, op=ALU.subtract)

            slot = [0]

            def put(g_ap, dests):
                s_hi = OUT[slot[0] % 12]; slot[0] += 1
                s_lo = OUT[slot[0] % 12]; slot[0] += 1
                v.tensor_copy(s_hi, g_ap)
                last = v.tensor_tensor(s_lo, g_ap, s_hi, op=ALU.subtract)
                for dd, which in dests:
                    stores.append((dd, s_hi if which == "hi" else s_lo))
                return last

            v.tensor_tensor(t2, f0, Dc, op=ALU.mult)
            v.tensor_tensor(t2, t2, C, op=ALU.add)
            v.tensor_tensor(t2, t2, f0, op=ALU.mult)
            v.tensor_tensor(t2, t2, Bc, op=ALU.add)
            v.tensor_tensor(t2, t2, f0, op=ALU.mult)
            v.tensor_tensor(t2, t2, h1, op=ALU.add)
            S["p1v"].inc(put(t2, [(("recsC", 0), "hi"), (("recsC", 1), "hi"),
                                  (("recsC", 2), "lo")]))
            v.tensor_scalar(t1, Dc, 3.0, 0.0, op0=ALU.mult, op1=ALU.add)
            v.tensor_tensor(t2, t1, f0, op=ALU.mult)
            v.scalar_tensor_tensor(t2, C, 2.0, t2, op0=ALU.mult, op1=ALU.add)
            v.tensor_tensor(t2, t2, f0, op=ALU.mult)
            v.tensor_tensor(t2, t2, Bc, op=ALU.add)
            v.tensor_scalar(t2, t2, d, 0.0, op0=ALU.mult, op1=ALU.add)
            S["p1v"].inc(put(t2, [(("recsC", 3), "hi"), (("recsC", 4), "hi"),
                                  (("recsC", 5), "lo")]))
            v.tensor_tensor(t2, t1, f0, op=ALU.mult)
            v.tensor_tensor(t2, t2, C, op=ALU.add)
            v.tensor_scalar(t2, t2, d2, 0.0, op0=ALU.mult, op1=ALU.add)
            S["p1v"].inc(put(t2, [(("recsC", 6), "hi"), (("recsC", 7), "hi"),
                                  (("recsC", 8), "lo")]))
            v.tensor_scalar(t2, Dc, d3, 0.0, op0=ALU.mult, op1=ALU.add)
            S["p1v"].inc(put(t2, [(("recsC", 9), "hi"), (("recsC", 10), "hi"),
                                  (("recsC", 11), "lo")]))
            v.tensor_tensor(t1, Q, bet, op=ALU.mult)
            v.tensor_scalar(t2, Pp, d2, 0.0, op0=ALU.mult, op1=ALU.add)
            v.scalar_tensor_tensor(t2, t1, -d3, t2, op0=ALU.mult, op1=ALU.add)
            S["p1v"].inc(put(t2, [(("recsA", 0), "hi"), (("recsA", 1), "lo")]))
            v.tensor_scalar(t2, Q, d3, 0.0, op0=ALU.mult, op1=ALU.add)
            S["p1v"].inc(put(t2, [(("recsA", 2), "hi"), (("recsA", 3), "hi"),
                                  (("recsA", 4), "lo")]))
        block.vector(vops)

        pv_base = S["p1v"].n - 6

        def gstores(g, w=w, stores=stores, pv_base=pv_base):
            per_group = [3, 3, 3, 3, 2, 3]
            done = 0
            for gi, cnt in enumerate(per_group):
                g.wait_ge(S["p1v"].h, pv_base + gi + 1)
                for _ in range(cnt):
                    (recname, row), src = stores[done]
                    done += 1
                    dst = ctx.t[recname][w][:, row, :].rearrange("(p k) j -> p k j", p=128)
                    S["p1s"].inc(g.dma_start(
                        out=dst, in_=src.rearrange("p (k j) -> p k j", k=2)), 16)
        block.gpsimd(gstores)


def _ph1_main(ctx, block):
    t, sb, S, pb = ctx.t, ctx.sb, ctx.S, ctx.pb
    carve = ctx.carve

    crb = [carve("arC", i * 4096, BF16, 2048, "p (r j) -> p r j", r=4)[0:12]
           for i in range(2)]
    arb = [carve("arC", 8192 + i * 4096, BF16, 2048, "p (r j) -> p r j", r=4)[0:5]
           for i in range(2)]
    nbb = [carve("arC", 16384 + i * 4096, BF16, 2048, "p (r j) -> p r j", r=4)[0:2]
           for i in range(2)]
    sR = [carve("arC", 24576 + i * 2048, F32, 512) for i in range(2)]
    sS = [carve("arC", 28672 + i * 2048, F32, 512) for i in range(2)]
    sSA = [carve("arC", 32768 + i * 2048, F32, 512) for i in range(2)]
    stg = [carve("arC", 36864 + i * 1024, BF16, 512) for i in range(3)]

    NB = 64
    RPB = 4
    base = dict(wls=S["wls"].n, wln=S["wln"].n, wmm=S["wmm"].n, wact=S["wact"].n,
                wve=S["wve"].n, wst=S["wst"].n, p1s=S["p1s"].n, tabs=S["tabs"].n,
                cg=S["cg"].n, p1v=S["p1v"].n)

    def syncs(sy):
        sy.wait_ge(S["p1s"].h, base["p1s"])
        for w in range(3):
            for b in range(NB):
                gb = w * NB + b
                if gb >= 2:
                    sy.wait_ge(S["wmm"].h, base["wmm"] + (gb - 1) * RPB)
                sl = gb % 2
                S["wls"].inc(sy.dma_start(
                    out=crb[sl], in_=t["recsC"][w, b * RPB:(b + 1) * RPB].rearrange("r a j -> a r j")), 16)
                S["wls"].inc(sy.dma_start(
                    out=arb[sl], in_=t["recsA"][w, b * RPB:(b + 1) * RPB].rearrange("r a j -> a r j")), 16)
    block.sync(syncs)

    def tes(te):
        te.wait_ge(S["tabs"].h, base["tabs"])
        te.wait_ge(S["cg"].h, base["cg"])
        te.wait_ge(S["p1v"].h, base["p1v"])
        for w in range(3):
            for b in range(NB):
                gb = w * NB + b
                sl = gb % 2
                te.wait_ge(S["wls"].h, base["wls"] + 32 * (gb + 1))
                te.wait_ge(S["wln"].h, base["wln"] + 16 * (gb + 1))
                for i in range(RPB):
                    r = gb * RPB + i
                    if r >= 2:
                        te.wait_ge(S["wve"].h, base["wve"] + 2 * (r - 1))
                        te.wait_ge(S["wact"].h, base["wact"] + 2 * (r - 1))
                    te.matmul(pb[r % 2][:, :], sb["s_cb"][:], crb[sl][:, i, :],
                              start=True, stop=True)
                    te.matmul(pb[2 + r % 2][:, :], sb["s_ab"][:], arb[sl][:, i, :],
                              start=True, stop=True)
                    S["wmm"].inc(te.matmul(pb[4 + r % 2][:, :], sb["s_ones2"][:],
                                           nbb[sl][:, i, :], start=True, stop=True))
    block.tensor(tes)

    def acts(s):
        for r in range(3 * NB * RPB):
            s.wait_ge(S["wmm"].h, base["wmm"] + r + 1)
            if r >= 2:
                s.wait_ge(S["wve"].h, base["wve"] + 2 * (r - 2) + 1)
            s.activation(sR[r % 2], pb[4 + r % 2][:, :], AF.Relu,
                         bias=sb["s_iota"][:, 0:1], scale=1.0)
            S["wact"].inc(s.activation(sS[r % 2], sR[r % 2], AF.Square,
                                       bias=0.0, scale=1.0), 2)
    block.scalar(acts)

    def ves(v):
        for r in range(3 * NB * RPB):
            v.wait_ge(S["wact"].h, base["wact"] + 2 * (r + 1))
            v.wait_ge(S["wmm"].h, base["wmm"] + r + 1)
            if r >= 3:
                v.wait_ge(S["wst"].h, base["wst"] + 16 * (r - 2))
            v.tensor_tensor(sSA[r % 2], sS[r % 2], pb[2 + r % 2][:, :], op=ALU.mult)
            S["wve"].inc(v.tensor_tensor(stg[r % 3], sSA[r % 2], pb[r % 2][:, :],
                                         op=ALU.add), 2)
    block.vector(ves)

    def gps(g):
        for w in range(3):
            nic = [16, 16, 64][w]
            wt = [t["wgg"], t["wgu"], t["wgd"]][w]
            for b in range(NB):
                gb = w * NB + b
                sl = gb % 2
                if gb >= 2:
                    g.wait_ge(S["wmm"].h, base["wmm"] + (gb - 1) * RPB)
                S["wln"].inc(g.dma_start(
                    out=nbb[sl], in_=t["nbrows"][w, b * RPB:(b + 1) * RPB].rearrange("r a j -> a r j")), 16)
                for i in range(RPB):
                    r = gb * RPB + i
                    rec = b * RPB + i
                    ic, ob = rec % nic, rec // nic
                    g.wait_ge(S["wve"].h, base["wve"] + 2 * (r + 1))
                    dst = wt[4 * ob:4 * ob + 4, ic].rearrange("k p j -> p k j")
                    S["wst"].inc(g.dma_start(
                        out=dst, in_=stg[r % 3].rearrange("p (k j) -> p k j", k=4)), 16)
    block.gpsimd(gps)


def _ph2_qkv(ctx, block):
    t, sb, S, pb = ctx.t, ctx.sb, ctx.S, ctx.pb
    carve = ctx.carve
    s_bv = ctx.s_bv

    vxT = carve("arB", 0, BF16, 16 * 2048, "p (c j) -> p c j", c=16)
    vws = [carve("arC", i * 4096, BF16, 16 * 128, "p (c j) -> p c j", c=16) for i in range(2)]
    vvw = [carve("arC", 8192 + i * 16384, BF16, 16 * 512, "p (c j) -> p c j", c=16) for i in range(2)]
    stg = [carve("arC", 40960 + i * 1024, BF16, 512) for i in range(3)]

    groups = []
    slabs = []
    for th in range(2):
        for os in range(16):
            slabs.append(("kq", "wkT", th, os))
            for qb in range(4):
                groups.append(dict(kind="K", th=th, os=os, qb=qb, slab=len(slabs) - 1))
        for ob in range(4):
            slabs.append(("v", "wvT", th, ob))
            for tt in range(16):
                groups.append(dict(kind="V", th=th, ob=ob, tt=tt, slab=len(slabs) - 1))
        if th == 0:
            for os in range(16):
                slabs.append(("kq", "wqT", th, os))
                for qb in range(4):
                    groups.append(dict(kind="Q", th=th, os=os, qb=qb, slab=len(slabs) - 1))
    slab_last = {}
    for gi, g in enumerate(groups):
        slab_last[g["slab"]] = gi
    kq_ct = v_ct = 0
    slab_slot = {}
    for si, (knd, _, _, _) in enumerate(slabs):
        if knd == "kq":
            slab_slot[si] = kq_ct % 2
            kq_ct += 1
        else:
            slab_slot[si] = v_ct % 2
            v_ct += 1
    prev_same = {}
    last_of = {}
    for si, (knd, _, _, _) in enumerate(slabs):
        key = (knd, slab_slot[si])
        prev_same[si] = last_of.get(key)
        last_of[key] = si
    th0_last = max(gi for gi, g in enumerate(groups) if g["th"] == 0)

    base = dict(q_xl=S["q_xl"].n, q_wl=S["q_wl"].n, q_mm=S["q_mm"].n,
                q_ev=S["q_ev"].n, q_st=S["q_st"].n, wst=S["wst"].n,
                cg=S["cg"].n, wve=S["wve"].n, wact=S["wact"].n, tabv=S["tabv"].n)

    def gps(g):
        g.wait_ge(S["wve"].h, base["wve"])
        g.wait_ge(S["wst"].h, base["wst"])
        S["q_xl"].inc(g.dma_start(
            out=vxT, in_=t["xT"][:, 0:2048].rearrange("(c p) j -> p c j", p=128)), 16)
        for si, (knd, wname, th, idx) in enumerate(slabs):
            if th == 1 and slabs[si - 1][2] == 0:
                g.wait_ge(S["q_mm"].h, base["q_mm"] + th0_last + 1)
                S["q_xl"].inc(g.dma_start(
                    out=vxT, in_=t["xT"][:, 2048:4096].rearrange("(c p) j -> p c j", p=128)), 16)
            ps = prev_same[si]
            if ps is not None:
                g.wait_ge(S["q_mm"].h, base["q_mm"] + slab_last[ps] + 1)
            if knd == "kq":
                dst = vws[slab_slot[si]]
                src = t[wname][:, idx * 128:(idx + 1) * 128].rearrange("(c p) j -> p c j", p=128)
            else:
                dst = vvw[slab_slot[si]]
                src = t[wname][:, idx * 512:(idx + 1) * 512].rearrange("(c p) j -> p c j", p=128)
            S["q_wl"].inc(g.dma_start(out=dst, in_=src), 16)
    block.gpsimd(gps)

    def tes(te):
        te.wait_ge(S["cg"].h, base["cg"])
        te.wait_ge(S["tabv"].h, base["tabv"])
        te.wait_ge(S["wve"].h, base["wve"])
        te.wait_ge(S["wact"].h, base["wact"])
        for gi, gr in enumerate(groups):
            need_xl = 16 * (1 if gr["th"] == 0 else 2)
            te.wait_ge(S["q_xl"].h, base["q_xl"] + need_xl)
            te.wait_ge(S["q_wl"].h, base["q_wl"] + 16 * (gr["slab"] + 1))
            if gi >= 2:
                te.wait_ge(S["q_ev"].h, base["q_ev"] + gi - 1)
            p = pb[gi % 2]
            if gr["kind"] in ("K", "Q"):
                sl = slab_slot[gr["slab"]]
                for dc in range(NDC):
                    mm = te.matmul(p[:, :], vws[sl][:, dc, :],
                                   vxT[:, dc, gr["qb"] * 512:(gr["qb"] + 1) * 512],
                                   start=(dc == 0), stop=(dc == NDC - 1))
                S["q_mm"].inc(mm)
            else:
                sl = slab_slot[gr["slab"]]
                tt = gr["tt"]
                for dc in range(NDC):
                    te.matmul(p[:, :], vxT[:, dc, tt * 128:(tt + 1) * 128],
                              vvw[sl][:, dc, :], start=(dc == 0), stop=False)
                S["q_mm"].inc(te.matmul(
                    p[:, :], sb["s_ones1b"][:],
                    s_bv[0:1, gr["ob"] * 512:(gr["ob"] + 1) * 512],
                    start=False, stop=True))
    block.tensor(tes)

    def acts(s):
        for gi, gr in enumerate(groups):
            s.wait_ge(S["q_mm"].h, base["q_mm"] + gi + 1)
            if gi >= 3:
                s.wait_ge(S["q_st"].h, base["q_st"] + 16 * (gi - 2))
            p = pb[gi % 2]
            dst = stg[gi % 3]
            if gr["kind"] == "K":
                os = gr["os"]
                S["q_ev"].inc(s.activation(dst, p[:, :], AF.Identity,
                                           bias=sb["tb_kb"][:, os:os + 1],
                                           scale=sb["tb_ks"][:, os:os + 1]))
            elif gr["kind"] == "Q":
                os = gr["os"]
                S["q_ev"].inc(s.activation(dst, p[:, :], AF.Identity,
                                           bias=sb["tb_qb"][:, os:os + 1],
                                           scale=sb["tb_cos"][:, os:os + 1]))
            else:
                S["q_ev"].inc(s.copy(dst, p[:, :]))
    block.scalar(acts)

    def syncs(sy):
        for gi, gr in enumerate(groups):
            sy.wait_ge(S["q_ev"].h, base["q_ev"] + gi + 1)
            src = stg[gi % 3]
            if gr["kind"] == "K":
                dst = t["KT"][gr["os"] * 128:(gr["os"] + 1) * 128,
                              gr["th"] * 2048 + gr["qb"] * 512:
                              gr["th"] * 2048 + (gr["qb"] + 1) * 512]
                S["q_st"].inc(sy.dma_start(out=dst, in_=src), 16)
            elif gr["kind"] == "Q":
                dst = t["QT"][gr["os"] * 128:(gr["os"] + 1) * 128,
                              gr["qb"] * 512:(gr["qb"] + 1) * 512]
                S["q_st"].inc(sy.dma_start(out=dst, in_=src), 16)
            else:
                tc = gr["th"] * 16 + gr["tt"]
                dst = t["Vt"][gr["ob"] * 4:(gr["ob"] + 1) * 4, tc].rearrange("h p j -> p h j")
                S["q_st"].inc(sy.dma_start(out=dst, in_=src.rearrange("p (h j) -> p h j", h=4)), 16)
    block.sync(syncs)


def _ph3_attn(ctx, block):
    t, sb, S, pb = ctx.t, ctx.sb, ctx.S, ctx.pb
    psS1 = ctx.psS1
    carve = ctx.carve

    vK = [carve("arB", i * 8192, BF16, 4096) for i in range(2)]
    vV = [carve("arB", 16384 + i * 8192, BF16, 32 * 128, "p (c j) -> p c j", c=32) for i in range(2)]
    vQ = [carve("arB", 32768 + i * 1024, BF16, 512) for i in range(2)]
    vprob = [carve("arB", 34816 + i * 1024, BF16, 512) for i in range(3)]
    dacc = [carve("arB", 38912 + i * 2048, F32, 512) for i in range(2)]
    vDrb = carve("arB", 43008, F32, 512)
    r_dr = ctx.sb["rowsB"][0:1, 0:512]
    normT = carve("arA", 0, BF16, 16 * 2048, "p (h j) -> p h j", h=16)

    NC = 32
    base = dict(a_kv=S["a_kv"].n, a_q=S["a_q"].n, a_s=S["a_s"].n, a_x=S["a_x"].n,
                a_v=S["a_v"].n, a_d=S["a_d"].n, a_f=S["a_f"].n,
                q_mm=S["q_mm"].n, q_st=S["q_st"].n, q_ev=S["q_ev"].n)

    def syncs(sy):
        sy.wait_ge(S["q_st"].h, base["q_st"])
        for h in range(H):
            if h >= 2:
                sy.wait_ge(S["a_v"].h, base["a_v"] + 128 * (h - 1))
            S["a_kv"].inc(sy.dma_start(out=vK[h % 2], in_=t["KT"][h * 128:(h + 1) * 128, :]), 16)
            S["a_kv"].inc(sy.dma_start(out=vV[h % 2], in_=t["Vt"][h].rearrange("c p j -> p c j")), 16)
            for qb in range(4):
                gq = h * 4 + qb
                if gq >= 2:
                    sy.wait_ge(S["a_s"].h, base["a_s"] + 32 * (gq - 1))
                S["a_q"].inc(sy.dma_start(
                    out=vQ[gq % 2],
                    in_=t["QT"][h * 128:(h + 1) * 128, qb * 512:(qb + 1) * 512]), 16)
    block.sync(syncs)

    def tes(te):
        te.wait_ge(S["q_mm"].h, base["q_mm"])
        te.wait_ge(S["q_ev"].h, base["q_ev"])
        for h in range(H):
            te.wait_ge(S["a_kv"].h, base["a_kv"] + 32 * (h + 1))
            for qb in range(4):
                gq = h * 4 + qb
                te.wait_ge(S["a_q"].h, base["a_q"] + 16 * (gq + 1))
                for c in range(NC):
                    wv = 33 * gq + c - 1
                    if wv >= 1:
                        te.wait_ge(S["a_x"].h, base["a_x"] + wv)
                    S["a_s"].inc(te.matmul(pb[c % 2][:, :],
                                           vK[h % 2][:, c * 128:(c + 1) * 128],
                                           vQ[gq % 2], start=True, stop=True))
                    cc = c - 1
                    if cc >= 0:
                        te.wait_ge(S["a_x"].h, base["a_x"] + 33 * gq + cc + 1)
                        if cc == 0 and gq >= 1:
                            te.wait_ge(S["a_d"].h, base["a_d"] + 34 * gq)
                        S["a_v"].inc(te.matmul(pb[2][:, :], vV[h % 2][:, cc, :],
                                               vprob[cc % 3], start=(cc == 0), stop=False))
                cc = NC - 1
                te.wait_ge(S["a_x"].h, base["a_x"] + 33 * gq + cc + 1)
                S["a_v"].inc(te.matmul(pb[2][:, :], vV[h % 2][:, cc, :],
                                       vprob[cc % 3], start=False, stop=True))
                te.wait_ge(S["a_d"].h, base["a_d"] + 34 * gq + 32)
                S["a_f"].inc(te.matmul(psS1[:, :], sb["s_onescol"][:],
                                       dacc[(NC - 1) % 2], start=True, stop=True))
                te.wait_ge(S["a_d"].h, base["a_d"] + 34 * gq + 33)
                S["a_f"].inc(te.matmul(pb[4][:, :], sb["s_ones1f"][:], r_dr,
                                       start=True, stop=True))
    block.tensor(tes)

    def acts(s):
        for h in range(H):
            for qb in range(4):
                gq = h * 4 + qb
                for c in range(NC):
                    s.wait_ge(S["a_s"].h, base["a_s"] + 32 * gq + c + 1)
                    wv = 32 * gq + c - 2
                    if wv >= 1:
                        s.wait_ge(S["a_v"].h, base["a_v"] + wv)
                    wd = 34 * gq + c - 2
                    if wd >= 1:
                        s.wait_ge(S["a_d"].h, base["a_d"] + wd)
                    S["a_x"].inc(s.activation(vprob[c % 3], pb[c % 2][:, :], AF.Exp,
                                              bias=0.0, scale=1.0))
                s.wait_ge(S["a_f"].h, base["a_f"] + 2 * gq + 2)
                S["a_x"].inc(s.copy(vDrb, pb[4][:, :]))
    block.scalar(acts)

    def ves(v):
        for h in range(H):
            for qb in range(4):
                gq = h * 4 + qb
                for c in range(NC):
                    v.wait_ge(S["a_x"].h, base["a_x"] + 33 * gq + c + 1)
                    if c == 0:
                        S["a_d"].inc(v.tensor_copy(dacc[0], vprob[0]))
                    else:
                        S["a_d"].inc(v.tensor_tensor(dacc[c % 2], dacc[(c - 1) % 2],
                                                     vprob[c % 3], op=ALU.add))
                v.wait_ge(S["a_f"].h, base["a_f"] + 2 * gq + 1)
                S["a_d"].inc(v.reciprocal(r_dr, psS1[0:1, :]))
                v.wait_ge(S["a_x"].h, base["a_x"] + 33 * (gq + 1))
                v.wait_ge(S["a_v"].h, base["a_v"] + 32 * (gq + 1))
                S["a_d"].inc(v.tensor_tensor(normT[:, h, qb * 512:(qb + 1) * 512],
                                             pb[2][:, :], vDrb, op=ALU.mult))
    block.vector(ves)


def _ph4_wo_ln1(ctx, block):
    t, sb, S, pb = ctx.t, ctx.sb, ctx.S, ctx.pb
    psS1, psS2 = ctx.psS1, ctx.psS2
    carve = ctx.carve

    normT = carve("arA", 0, BF16, 16 * 2048, "p (h j) -> p h j", h=16)
    h1T = carve("arA", 0, BF16, 16 * 2048, "p (h j) -> p h j", h=16)
    vWo = carve("arB", 0, BF16, 16 * 2048, "p (c j) -> p c j", c=16)
    vz = [carve("arC", i * 2048, F32, 512) for i in range(2)]
    vxf = [carve("arC", 4096 + i * 2048, F32, 512) for i in range(2)]
    vz1 = [carve("arC", 8192 + i * 2048, F32, 512) for i in range(2)]
    vsq = [carve("arC", 12288 + i * 2048, F32, 512) for i in range(2)]
    vzl = [carve("arC", 16384 + i * 2048, F32, 512) for i in range(2)]
    vh1 = [carve("arC", 20480 + i * 2048, F32, 512) for i in range(2)]
    vRR = carve("arC", 24576, F32, 512)
    vMM = carve("arC", 26624, F32, 512)
    r_sz = sb["rowsA"][0:1, :]
    r_sq = sb["rowsA"][32:33, :]
    r_r = sb["rowsB"][0:1, :]
    r_tmp = sb["rowsB"][32:33, :]

    base = dict(o_wl=S["o_wl"].n, o_mm=S["o_mm"].n, o_ev=S["o_ev"].n, o_xl=S["o_xl"].n,
                o_z=S["o_z"].n, o_sq=S["o_sq"].n, o_st=S["o_st"].n, o_sc=S["o_sc"].n,
                o_s2=S["o_s2"].n, a_d=S["a_d"].n, a_x=S["a_x"].n, a_v=S["a_v"].n,
                l_zl=S["l_zl"].n, l_ap=S["l_ap"].n, l_hb=S["l_hb"].n,
                g_bc=S["g_bc"].n, g_cp=S["g_cp"].n)

    def gps(g):
        g.wait_ge(S["a_d"].h, base["a_d"])
        S["o_wl"].inc(g.dma_start(
            out=vWo, in_=t["woT"].ap().rearrange("(c p) o -> p c o", p=128)), 16)
        for qb in range(4):
            for ot in range(16):
                gi = qb * 16 + ot
                g.wait_ge(S["o_z"].h, base["o_z"] + gi + 1)
                S["o_st"].inc(g.dma_start(
                    out=t["z1"][ot * 128:(ot + 1) * 128, qb * 512:(qb + 1) * 512],
                    in_=vz1[ot % 2]), 16)
        for qb in range(4):
            for dc in range(16):
                ai = qb * 16 + dc
                g.wait_ge(S["l_ap"].h, base["l_ap"] + 2 * ai + 1)
                S["o_st"].inc(g.dma_start(
                    out=t["h1f"][dc * 128:(dc + 1) * 128, qb * 512:(qb + 1) * 512],
                    in_=vh1[dc % 2]), 16)
    block.gpsimd(gps)

    def syncs(sy):
        for qb in range(4):
            for ot in range(16):
                gi = qb * 16 + ot
                if gi >= 2:
                    sy.wait_ge(S["o_z"].h, base["o_z"] + gi - 1)
                S["o_xl"].inc(sy.dma_start(
                    out=vxf[ot % 2],
                    in_=t["xT"][ot * 128:(ot + 1) * 128, qb * 512:(qb + 1) * 512]), 16)
        for qb in range(4):
            for dc in range(16):
                ai = qb * 16 + dc
                if ai == 0:
                    sy.wait_ge(S["o_st"].h, base["o_st"] + 16 * 64)
                if ai >= 2:
                    sy.wait_ge(S["l_ap"].h, base["l_ap"] + 2 * (ai - 1))
                S["l_zl"].inc(sy.dma_start(
                    out=vzl[dc % 2],
                    in_=t["z1"][dc * 128:(dc + 1) * 128, qb * 512:(qb + 1) * 512]), 16)
    block.sync(syncs)

    def tes(te):
        te.wait_ge(S["o_wl"].h, base["o_wl"] + 16)
        te.wait_ge(S["a_d"].h, base["a_d"])
        te.wait_ge(S["a_x"].h, base["a_x"])
        for qb in range(4):
            for ot in range(16):
                gi = qb * 16 + ot
                if gi >= 2:
                    te.wait_ge(S["o_ev"].h, base["o_ev"] + gi - 1)
                p = pb[ot % 2]
                for dc in range(NDC):
                    mm = te.matmul(p[:, :], vWo[:, dc, ot * 128:(ot + 1) * 128],
                                   normT[:, dc, qb * 512:(qb + 1) * 512],
                                   start=(dc == 0), stop=(dc == NDC - 1))
                S["o_mm"].inc(mm)
                po = ot - 1
                if po >= 0:
                    gp = qb * 16 + po
                    te.wait_ge(S["o_z"].h, base["o_z"] + gp + 1)
                    te.wait_ge(S["o_sq"].h, base["o_sq"] + gp + 1)
                    if po == 0:
                        te.wait_ge(S["o_sc"].h, base["o_sc"] + 2 * qb)
                    te.matmul(psS1[:, :], sb["s_onescol"][:], vz1[po % 2],
                              start=(po == 0), stop=False)
                    S["o_s2"].inc(te.matmul(psS2[:, :], sb["s_onescol"][:], vsq[po % 2],
                                            start=(po == 0), stop=False))
            po = 15
            gp = qb * 16 + po
            te.wait_ge(S["o_z"].h, base["o_z"] + gp + 1)
            te.wait_ge(S["o_sq"].h, base["o_sq"] + gp + 1)
            te.matmul(psS1[:, :], sb["s_onescol"][:], vz1[po % 2], start=False, stop=True)
            S["o_s2"].inc(te.matmul(psS2[:, :], sb["s_onescol"][:], vsq[po % 2],
                                    start=False, stop=True))
    block.tensor(tes)

    def acts(s):
        for qb in range(4):
            for ot in range(16):
                gi = qb * 16 + ot
                s.wait_ge(S["o_mm"].h, base["o_mm"] + gi + 1)
                if gi >= 2:
                    s.wait_ge(S["o_z"].h, base["o_z"] + gi - 1)
                S["o_ev"].inc(s.activation(vz[ot % 2], pb[ot % 2][:, :], AF.Identity,
                                           bias=sb["s_bo"][:, ot:ot + 1], scale=1.0))
                s.wait_ge(S["o_z"].h, base["o_z"] + gi + 1)
                if gi >= 2:
                    s.wait_ge(S["o_s2"].h, base["o_s2"] + gi - 1)
                S["o_sq"].inc(s.activation(vsq[ot % 2], vz1[ot % 2], AF.Square,
                                           bias=0.0, scale=1.0))
            s.wait_ge(S["o_s2"].h, base["o_s2"] + 16 * (qb + 1))
            s.copy(r_sz[0:1, qb * 512:(qb + 1) * 512], psS1[0:1, :])
            S["o_sc"].inc(s.copy(r_sq[0:1, qb * 512:(qb + 1) * 512], psS2[0:1, :]), 2)
    block.scalar(acts)

    def ves(v):
        for qb in range(4):
            for ot in range(16):
                gi = qb * 16 + ot
                v.wait_ge(S["o_ev"].h, base["o_ev"] + gi + 1)
                v.wait_ge(S["o_xl"].h, base["o_xl"] + 16 * (gi + 1))
                if gi >= 2:
                    v.wait_ge(S["o_sq"].h, base["o_sq"] + gi - 1)
                    v.wait_ge(S["o_st"].h, base["o_st"] + 16 * (gi - 1))
                    v.wait_ge(S["o_s2"].h, base["o_s2"] + gi - 1)
                S["o_z"].inc(v.tensor_tensor(vz1[ot % 2], vz[ot % 2], vxf[ot % 2],
                                             op=ALU.add))
        v.wait_ge(S["o_sc"].h, base["o_sc"] + 8)
        _vs(ctx, v, v.tensor_scalar(r_sz, r_sz, float(1.0 / D), 0.0, op0=ALU.mult, op1=ALU.add))
        _vs(ctx, v, v.tensor_tensor(r_tmp, r_sz, r_sz, op=ALU.mult))
        S["o_z"].inc(v.scalar_tensor_tensor(r_sq, r_sq, float(1.0 / D), r_tmp,
                                            op0=ALU.mult, op1=ALU.subtract))
    block.vector(ves)

    def acts2(s):
        s.wait_ge(S["o_z"].h, S["o_z"].n)
        S["o_sc"].inc(s.activation(r_tmp, r_sq, AF.Sqrt, bias=sb["s_eps"][0:1, 0:1], scale=1.0))
    block.scalar(acts2)

    def ves2(v):
        v.wait_ge(S["o_sc"].h, S["o_sc"].n)
        _vs(ctx, v, v.reciprocal(r_r, r_tmp))
        S["l_hb"].inc(v.tensor_tensor(r_sz, r_sz, r_r, op=ALU.mult))
    block.vector(ves2)

    lb = dict(l_hb=S["l_hb"].n, l_zl=base["l_zl"], l_ap=base["l_ap"],
              g_bc=base["g_bc"], g_cp=base["g_cp"], o_st=base["o_st"])

    def tes2(te):
        te.wait_ge(S["l_hb"].h, lb["l_hb"])
        for qb in range(4):
            if qb >= 1:
                te.wait_ge(S["l_ap"].h, lb["l_ap"] + 32 * qb)
            te.matmul(pb[4][:, :], sb["s_ones1f"][:],
                      r_r[0:1, qb * 512:(qb + 1) * 512], start=True, stop=True)
            S["g_bc"].inc(te.matmul(pb[5][:, :], sb["s_ones1f"][:],
                                    r_sz[0:1, qb * 512:(qb + 1) * 512],
                                    start=True, stop=True))
    block.tensor(tes2)

    def acts3(s):
        for qb in range(4):
            s.wait_ge(S["g_bc"].h, lb["g_bc"] + qb + 1)
            if qb >= 1:
                s.wait_ge(S["l_ap"].h, lb["l_ap"] + 32 * qb)
            s.copy(vRR, pb[4][:, :])
            S["g_cp"].inc(s.copy(vMM, pb[5][:, :]), 2)
    block.scalar(acts3)

    def ves3(v):
        for qb in range(4):
            v.wait_ge(S["g_cp"].h, lb["g_cp"] + 2 * (qb + 1))
            for dc in range(16):
                ai = qb * 16 + dc
                v.wait_ge(S["l_zl"].h, lb["l_zl"] + 16 * (ai + 1))
                if ai >= 2:
                    v.wait_ge(S["o_st"].h, lb["o_st"] + 16 * 64 + 16 * (ai - 1))
                v.tensor_tensor(vh1[dc % 2], vzl[dc % 2], vRR, op=ALU.mult)
                S["l_ap"].inc(v.tensor_tensor(vh1[dc % 2], vh1[dc % 2], vMM,
                                              op=ALU.subtract))
                S["l_ap"].inc(v.tensor_copy(h1T[:, dc, qb * 512:(qb + 1) * 512],
                                            vh1[dc % 2]))
    block.vector(ves3)


def _ph5_ffn(ctx, block):
    t, sb, S, pb = ctx.t, ctx.sb, ctx.S, ctx.pb
    psS1, psS2 = ctx.psS1, ctx.psS2
    carve = ctx.carve

    h1T = carve("arA", 0, BF16, 16 * 2048, "p (h j) -> p h j", h=16)
    interT = carve("arB", 0, BF16, 64 * 512, "p (c j) -> p c j", c=64)
    gsl = [carve("arC", i * 4096, BF16, 16 * 128, "p (c j) -> p c j", c=16) for i in range(2)]
    usl = [carve("arC", 8192 + i * 4096, BF16, 16 * 128, "p (c j) -> p c j", c=16) for i in range(2)]
    dsl = [carve("arC", 16384 + i * 4096, BF16, 16 * 128, "p (c j) -> p c j", c=16) for i in range(2)]
    vsil = [carve("arC", 24576 + i * 1024, BF16, 512) for i in range(2)]
    vz = [carve("arC", 26624 + i * 2048, F32, 512) for i in range(2)]
    vz2 = [carve("arC", 30720 + i * 2048, F32, 512) for i in range(2)]
    vsq = [carve("arC", 34816 + i * 2048, F32, 512) for i in range(2)]
    vhl = [carve("arC", 38912 + i * 2048, F32, 512) for i in range(2)]
    r_sz = sb["rowsA"][0:1, :]
    r_sq = sb["rowsA"][32:33, :]
    r_r = sb["rowsB"][0:1, :]
    r_tmp = sb["rowsB"][32:33, :]

    base = dict(f_wl=S["f_wl"].n, f_mm=S["f_mm"].n, f_sil=S["f_sil"].n,
                f_mu=S["f_mu"].n,
                f_dwl=S["f_dwl"].n, f_dmm=S["f_dmm"].n, f_ev=S["f_ev"].n,
                f_hl=S["f_hl"].n, f_z2=S["f_z2"].n, f_sq=S["f_sq"].n,
                f_st=S["f_st"].n, f_sc=S["f_sc"].n, f_s2=S["f_s2"].n,
                l_ap=S["l_ap"].n, wst=S["wst"].n, o_st=S["o_st"].n,
                o_z=S["o_z"].n, o_s2=S["o_s2"].n, o_sc=S["o_sc"].n,
                g_bc=S["g_bc"].n, g_cp=S["g_cp"].n, l_hb=S["l_hb"].n)

    def syncs(sy):
        sy.wait_ge(S["wst"].h, base["wst"])
        sy.wait_ge(S["o_z"].h, base["o_z"])
        sy.wait_ge(S["o_s2"].h, base["o_s2"])
        sy.wait_ge(S["o_st"].h, base["o_st"])
        for tb in range(4):
            for it in range(64):
                si = tb * 64 + it
                if si >= 2:
                    sy.wait_ge(S["f_mm"].h, base["f_mm"] + 2 * (si - 1))
                S["f_wl"].inc(sy.dma_start(
                    out=gsl[it % 2], in_=t["wgg"][it].rearrange("c p j -> p c j")), 16)
                S["f_wl"].inc(sy.dma_start(
                    out=usl[it % 2], in_=t["wgu"][it].rearrange("c p j -> p c j")), 16)
            for ot in (range(16) if _PH5_DOWN else []):
                hi = tb * 16 + ot
                for qq in range(4):
                    dq = ot * 4 + qq
                    di = tb * 64 + dq
                    if di == 0:
                        sy.wait_ge(S["l_ap"].h, base["l_ap"])
                        sy.wait_ge(S["g_cp"].h, base["g_cp"])
                    if di >= 2:
                        sy.wait_ge(S["f_dmm"].h, base["f_dmm"] + di - 1)
                    S["f_dwl"].inc(sy.dma_start(
                        out=dsl[dq % 2],
                        in_=t["wgd"][ot, qq * 16:(qq + 1) * 16].rearrange("c p j -> p c j")), 16)
                if hi >= 2:
                    sy.wait_ge(S["f_z2"].h, base["f_z2"] + hi - 1)
                S["f_hl"].inc(sy.dma_start(
                    out=vhl[ot % 2],
                    in_=t["h1f"][ot * 128:(ot + 1) * 128, tb * 512:(tb + 1) * 512]), 16)
    block.sync(syncs)

    def tes(te):
        te.wait_ge(S["l_ap"].h, base["l_ap"])
        te.wait_ge(S["g_cp"].h, base["g_cp"])
        for tb in range(4):
            for it in range(64):
                si = tb * 64 + it
                te.wait_ge(S["f_wl"].h, base["f_wl"] + 32 * (si + 1))
                if _PH5_DOWN and tb >= 1 and it == 0:
                    te.wait_ge(S["f_dmm"].h, base["f_dmm"] + 64 * tb)
                if si >= 2:
                    te.wait_ge(S["f_sil"].h, base["f_sil"] + si - 1)
                    te.wait_ge(S["f_mu"].h, base["f_mu"] + si - 1)
                for dc in range(NDC):
                    mm = te.matmul(pb[2 * (si % 2)][:, :], gsl[it % 2][:, dc, :],
                                   h1T[:, dc, tb * 512:(tb + 1) * 512],
                                   start=(dc == 0), stop=(dc == NDC - 1))
                S["f_mm"].inc(mm)
                for dc in range(NDC):
                    mm = te.matmul(pb[1 + 2 * (si % 2)][:, :], usl[it % 2][:, dc, :],
                                   h1T[:, dc, tb * 512:(tb + 1) * 512],
                                   start=(dc == 0), stop=(dc == NDC - 1))
                S["f_mm"].inc(mm)
            for ot in (range(16) if _PH5_DOWN else []):
                hi = tb * 16 + ot
                te.wait_ge(S["f_mu"].h, base["f_mu"] + 64 * (tb + 1))
                if hi >= 2:
                    te.wait_ge(S["f_ev"].h, base["f_ev"] + hi - 1)
                p = pb[4 + ot % 2]
                for qq in range(4):
                    di = tb * 64 + ot * 4 + qq
                    te.wait_ge(S["f_dwl"].h, base["f_dwl"] + 16 * (di + 1))
                    for c in range(16):
                        ic = qq * 16 + c
                        mm = te.matmul(p[:, :], dsl[(ot * 4 + qq) % 2][:, c, :],
                                       interT[:, ic, :],
                                       start=(ic == 0), stop=(ic == 63))
                    S["f_dmm"].inc(mm)
                po = ot - 1
                if po >= 0:
                    hp = tb * 16 + po
                    te.wait_ge(S["f_z2"].h, base["f_z2"] + hp + 1)
                    te.wait_ge(S["f_sq"].h, base["f_sq"] + hp + 1)
                    if po == 0:
                        te.wait_ge(S["f_sc"].h, base["f_sc"] + 2 * tb)
                    te.matmul(psS1[:, :], sb["s_onescol"][:], vz2[po % 2],
                              start=(po == 0), stop=False)
                    S["f_s2"].inc(te.matmul(psS2[:, :], sb["s_onescol"][:], vsq[po % 2],
                                            start=(po == 0), stop=False))
            if _PH5_DOWN:
                po = 15
                hp = tb * 16 + po
                te.wait_ge(S["f_z2"].h, base["f_z2"] + hp + 1)
                te.wait_ge(S["f_sq"].h, base["f_sq"] + hp + 1)
                te.matmul(psS1[:, :], sb["s_onescol"][:], vz2[po % 2], start=False, stop=True)
                S["f_s2"].inc(te.matmul(psS2[:, :], sb["s_onescol"][:], vsq[po % 2],
                                        start=False, stop=True))
    block.tensor(tes)

    def acts(s):
        for tb in range(4):
            for it in range(64):
                si = tb * 64 + it
                s.wait_ge(S["f_mm"].h, base["f_mm"] + 2 * si + 1)
                if si >= 2:
                    s.wait_ge(S["f_mu"].h, base["f_mu"] + si - 1)
                S["f_sil"].inc(s.activation(vsil[si % 2], pb[2 * (si % 2)][:, :],
                                            AF.Silu, bias=0.0, scale=1.0))
            for ot in (range(16) if _PH5_DOWN else []):
                hi = tb * 16 + ot
                s.wait_ge(S["f_dmm"].h, base["f_dmm"] + 4 * (hi + 1))
                if hi >= 2:
                    s.wait_ge(S["f_z2"].h, base["f_z2"] + hi - 1)
                S["f_ev"].inc(s.activation(vz[ot % 2], pb[4 + ot % 2][:, :], AF.Copy,
                                           bias=0.0, scale=1.0))
                s.wait_ge(S["f_z2"].h, base["f_z2"] + hi + 1)
                if hi >= 2:
                    s.wait_ge(S["f_s2"].h, base["f_s2"] + hi - 1)
                S["f_sq"].inc(s.activation(vsq[ot % 2], vz2[ot % 2], AF.Square,
                                           bias=0.0, scale=1.0))
            if _PH5_DOWN:
                s.wait_ge(S["f_s2"].h, base["f_s2"] + 16 * (tb + 1))
                if tb == 0:
                    s.wait_ge(S["g_bc"].h, base["g_bc"])
                s.copy(r_sz[0:1, tb * 512:(tb + 1) * 512], psS1[0:1, :])
                S["f_sc"].inc(s.copy(r_sq[0:1, tb * 512:(tb + 1) * 512], psS2[0:1, :]), 2)
    block.scalar(acts)

    def ves(v):
        for tb in range(4):
            for it in range(64):
                si = tb * 64 + it
                v.wait_ge(S["f_sil"].h, base["f_sil"] + si + 1)
                v.wait_ge(S["f_mm"].h, base["f_mm"] + 2 * si + 2)
                if _PH5_DOWN and tb >= 1 and it == 0:
                    v.wait_ge(S["f_dmm"].h, base["f_dmm"] + 64 * tb)
                S["f_mu"].inc(v.tensor_tensor(interT[:, it, :], vsil[si % 2],
                                               pb[1 + 2 * (si % 2)][:, :], op=ALU.mult))
            for ot in (range(16) if _PH5_DOWN else []):
                hi = tb * 16 + ot
                v.wait_ge(S["f_ev"].h, base["f_ev"] + hi + 1)
                v.wait_ge(S["f_hl"].h, base["f_hl"] + 16 * (hi + 1))
                if hi >= 2:
                    v.wait_ge(S["f_sq"].h, base["f_sq"] + hi - 1)
                    v.wait_ge(S["f_st"].h, base["f_st"] + 16 * (hi - 1))
                    v.wait_ge(S["f_s2"].h, base["f_s2"] + hi - 1)
                S["f_z2"].inc(v.tensor_tensor(vz2[ot % 2], vz[ot % 2], vhl[ot % 2],
                                              op=ALU.add))
        if not _PH5_DOWN:
            return
        v.wait_ge(S["f_sc"].h, base["f_sc"] + 8)
        _vs(ctx, v, v.tensor_scalar(r_sz, r_sz, float(1.0 / D), 0.0, op0=ALU.mult, op1=ALU.add))
        _vs(ctx, v, v.tensor_tensor(r_tmp, r_sz, r_sz, op=ALU.mult))
        S["f_z2"].inc(v.scalar_tensor_tensor(r_sq, r_sq, float(1.0 / D), r_tmp,
                                             op0=ALU.mult, op1=ALU.subtract))
    block.vector(ves)

    def gps(g):
        for tb in (range(4) if _PH5_DOWN else []):
            for ot in range(16):
                hi = tb * 16 + ot
                g.wait_ge(S["f_z2"].h, base["f_z2"] + hi + 1)
                S["f_st"].inc(g.dma_start(
                    out=t["z2"][ot * 128:(ot + 1) * 128, tb * 512:(tb + 1) * 512],
                    in_=vz2[ot % 2]), 16)
    block.gpsimd(gps)

    def acts2(s):
        if not _PH5_DOWN:
            return
        s.wait_ge(S["f_z2"].h, S["f_z2"].n)
        S["f_sc"].inc(s.activation(r_tmp, r_sq, AF.Sqrt, bias=sb["s_eps"][0:1, 0:1], scale=1.0))
    block.scalar(acts2)

    def ves2(v):
        if not _PH5_DOWN:
            return
        v.wait_ge(S["f_sc"].h, S["f_sc"].n)
        _vs(ctx, v, v.reciprocal(r_r, r_tmp))
        S["l_hb"].inc(v.tensor_tensor(r_sz, r_sz, r_r, op=ALU.mult))
    block.vector(ves2)


def _ph6_out(ctx, block):
    t, sb, S, pb = ctx.t, ctx.sb, ctx.S, ctx.pb
    carve = ctx.carve

    vzl = [carve("arC", i * 2048, F32, 512) for i in range(2)]
    vout = [carve("arC", 4096 + i * 2048, F32, 512) for i in range(2)]
    vRR = carve("arC", 8192, F32, 512)
    vMM = carve("arC", 10240, F32, 512)
    r_sz = sb["rowsA"][0:1, :]
    r_r = sb["rowsB"][0:1, :]

    base = dict(l_hb=S["l_hb"].n, g_zl=S["g_zl"].n, g_ap=S["g_ap"].n,
                g_st=S["g_st"].n, g_bc=S["g_bc"].n, g_cp=S["g_cp"].n,
                f_st=S["f_st"].n, f_mm=S["f_mm"].n, f_ev=S["f_ev"].n)

    def tes(te):
        te.wait_ge(S["l_hb"].h, base["l_hb"])
        te.wait_ge(S["f_ev"].h, base["f_ev"])
        for qb in range(4):
            if qb >= 1:
                te.wait_ge(S["g_ap"].h, base["g_ap"] + 32 * qb)
            te.matmul(pb[4][:, :], sb["s_ones1f"][:],
                      r_r[0:1, qb * 512:(qb + 1) * 512], start=True, stop=True)
            S["g_bc"].inc(te.matmul(pb[5][:, :], sb["s_ones1f"][:],
                                    r_sz[0:1, qb * 512:(qb + 1) * 512],
                                    start=True, stop=True))
    block.tensor(tes)

    def acts(s):
        for qb in range(4):
            s.wait_ge(S["g_bc"].h, base["g_bc"] + qb + 1)
            if qb >= 1:
                s.wait_ge(S["g_ap"].h, base["g_ap"] + 32 * qb)
            s.copy(vRR, pb[4][:, :])
            S["g_cp"].inc(s.copy(vMM, pb[5][:, :]), 2)
    block.scalar(acts)

    def syncs(sy):
        sy.wait_ge(S["f_st"].h, base["f_st"])
        sy.wait_ge(S["f_mm"].h, base["f_mm"])
        for qb in range(4):
            for dc in range(16):
                ai = qb * 16 + dc
                if ai >= 2:
                    sy.wait_ge(S["g_ap"].h, base["g_ap"] + 2 * (ai - 1))
                S["g_zl"].inc(sy.dma_start(
                    out=vzl[dc % 2],
                    in_=t["z2"][dc * 128:(dc + 1) * 128, qb * 512:(qb + 1) * 512]), 16)
    block.sync(syncs)

    def ves(v):
        for qb in range(4):
            v.wait_ge(S["g_cp"].h, base["g_cp"] + 2 * (qb + 1))
            for dc in range(16):
                ai = qb * 16 + dc
                v.wait_ge(S["g_zl"].h, base["g_zl"] + 16 * (ai + 1))
                if ai >= 2:
                    v.wait_ge(S["g_st"].h, base["g_st"] + 16 * (ai - 1))
                v.tensor_tensor(vout[dc % 2], vzl[dc % 2], vRR, op=ALU.mult)
                S["g_ap"].inc(v.tensor_tensor(vout[dc % 2], vout[dc % 2], vMM,
                                              op=ALU.subtract), 2)
    block.vector(ves)

    def gps(g):
        for qb in range(4):
            for dc in range(16):
                ai = qb * 16 + dc
                g.wait_ge(S["g_ap"].h, base["g_ap"] + 2 * (ai + 1))
                S["g_st"].inc(g.dma_start(
                    out=t["outT"][dc * 128:(dc + 1) * 128, qb * 512:(qb + 1) * 512],
                    in_=vout[dc % 2]), 16)
        g.wait_ge(S["g_st"].h, base["g_st"] + 16 * 64)
    block.gpsimd(gps)


def kernel(**inputs):
    hc = _host_consts()
    in_maps = [_core_in_map(c, inputs, hc) for c in range(8)]
    nc, _ = _build()
    res = run_bass_kernel_spmd(nc, in_maps, core_ids=list(range(8)))
    out = np.zeros((B, S, D), np.float32)
    for c in range(8):
        b, half = c // 2, c % 2
        out[b, half * TOK:(half + 1) * TOK, :] = np.asarray(res.results[c]["outT"]).T
    return out


# version 05
# speedup vs baseline: 1.1645x; 1.1645x over previous
import math
import numpy as np
from contextlib import ExitStack

import concourse.bass as bass
import concourse.mybir as mybir
from concourse.bass_utils import run_bass_kernel_spmd

F32 = mybir.dt.float32
BF16 = mybir.dt.bfloat16
U8 = mybir.dt.uint8
AF = mybir.ActivationFunctionType
ALU = mybir.AluOpType

B, S, D, H, DH, FF = 4, 4096, 2048, 16, 128, 8192
_PH5_DOWN = True
TOK = 2048
TFULL = 4096
NDC = D // 128
NCTRL = 130156
STEP64 = (NCTRL - 1) / (D * FF - 1)
NREC = 256
MAGIC = 12582912.0
TWO_PI = float(2.0 * np.pi)
LN_EPS = 1e-5

_bfd = None
def _bf16(x):
    global _bfd
    if _bfd is None:
        import ml_dtypes
        _bfd = ml_dtypes.bfloat16
    return np.asarray(x, np.float64).astype(_bfd).astype(np.float64)


_HOSTC = None

def _host_consts():
    global _HOSTC
    if _HOSTC is not None:
        return _HOSTC
    d = {}
    halos = np.zeros((3, 5, 128, 1024), np.int64)
    f0col = np.zeros((3, 128, 1024), np.float32)
    betacol = np.zeros((3, 128, 1024), np.float32)
    nbrows = np.zeros((3, NREC, 2, 512), np.float32)
    for w, (O, I) in enumerate([(FF, D), (FF, D), (D, FF)]):
        nic = I // 128
        rec = np.arange(NREC)
        ic = rec % nic
        ob = rec // nic
        j = np.arange(512)
        n0 = (ob[:, None] * 512 + j[None, :]).astype(np.int64) * I + (ic[:, None] * 128)
        t0 = n0.astype(np.float64) * STEP64
        sA = np.minimum(np.floor(t0).astype(np.int64), NCTRL - 2)
        f0 = t0 - sA
        beta = (1.0 - f0) / STEP64
        pp = (rec // 2)[:, None].repeat(512, 1)
        xx = (rec % 2)[:, None] * 512 + j[None, :]
        for a in range(5):
            hid = np.clip(sA - 1 + a, 0, NCTRL - 1)
            halos[w, a, pp, xx] = hid
        f0col[w, pp, xx] = f0.astype(np.float32)
        betacol[w, pp, xx] = beta.astype(np.float32)
        nbh = _bf16(-beta)
        nbl = _bf16(-beta - nbh)
        nbrows[w, :, 0, :] = nbh.astype(np.float32)
        nbrows[w, :, 1, :] = nbl.astype(np.float32)
    d["halo_idx"] = halos
    d["f0col"] = f0col
    d["betacol"] = betacol
    d["nbrows"] = nbrows
    pv = np.arange(128, dtype=np.float64)
    basis = np.stack([pv**m for m in range(4)], axis=0)
    bh = _bf16(basis)
    bl = _bf16(basis - bh)
    cb = np.zeros((12, 128), np.float32)
    for m in range(4):
        cb[3 * m + 0] = bh[m]
        cb[3 * m + 1] = bl[m]
        cb[3 * m + 2] = bh[m]
    d["cbasis"] = cb
    ab = np.zeros((5, 128), np.float32)
    ab[0] = 1.0
    ab[1] = 1.0
    ab[2] = bh[1]
    ab[3] = bl[1]
    ab[4] = bh[1]
    d["abasis"] = ab
    d["ones2"] = np.ones((2, 128), np.float32)
    d["iotacol"] = np.arange(128, dtype=np.float32)[:, None].copy()
    d["onescol"] = np.ones((128, 1), np.float32)
    d["ones1f"] = np.ones((1, 128), np.float32)
    d["ones128"] = np.ones((128, 128), np.float32)
    d["epscol"] = np.full((128, 1), 1e-5, np.float32)
    _HOSTC = d
    return d


def _core_in_map(c, inp, hc):
    b, half = c // 2, c % 2
    x = np.asarray(inp["L0_attn_x"])
    xb = x[b]
    own = xb[half * TOK:(half + 1) * TOK]
    oth = xb[(1 - half) * TOK:(2 - half) * TOK]
    xT = np.ascontiguousarray(np.concatenate([own, oth], axis=0).T)
    m = {"xT": xT}
    for nm, key in [("wqT", "L0_attn_Wq"), ("wkT", "L0_attn_Wk"),
                    ("wvT", "L0_attn_Wv"), ("woT", "L0_attn_Wo")]:
        m[nm] = np.ascontiguousarray(np.asarray(inp[key]).T)
    for nm, key in [("bq2d", "L0_attn_bq"), ("bk2d", "L0_attn_bk"),
                    ("bo2d", "L0_attn_bo"), ("hel2d", "L0_attn_helix_rotation")]:
        m[nm] = np.ascontiguousarray(np.asarray(inp[key]).reshape(H, DH).T)
    m["bv_row"] = np.ascontiguousarray(np.asarray(inp["L0_attn_bv"]).reshape(1, D))
    m["spiral_row"] = np.ascontiguousarray(np.asarray(inp["L0_attn_spiral_weight"]).reshape(1, H))
    cps = [np.asarray(inp["L0_ffn_gate_cp"]), np.asarray(inp["L0_ffn_up_cp"]),
           np.asarray(inp["L0_ffn_down_cp"])]
    halo = np.zeros((3, 5, 128, 1024), np.float32)
    for w in range(3):
        halo[w] = np.take(cps[w], hc["halo_idx"][w])
    m["halo"] = halo
    import ml_dtypes
    for nm in ["f0col", "betacol", "cbasis", "abasis", "ones2",
               "iotacol", "onescol", "ones1f", "ones128", "epscol"]:
        m[nm] = hc[nm]
    m["nbrows"] = hc["nbrows"].astype(ml_dtypes.bfloat16)
    return m


class _Ctx:
    pass


class _S:
    def __init__(self, h):
        self.h = h
        self.n = 0

    def inc(self, instr, k=1):
        instr.then_inc(self.h, k)
        self.n += k
        return self.n


def _build(dump=(), upto=9):
    nc = bass.Bass("TRN2", target_bir_lowering=False, debug=False)
    t = {}

    def dram(name, shape, dtype, kind="Internal"):
        if name in dump:
            kind = "ExternalOutput"
        t[name] = nc.dram_tensor(name, list(shape), dtype, kind=kind)

    def din(name, shape, dtype=F32):
        t[name] = nc.dram_tensor(name, list(shape), dtype, kind="ExternalInput")

    din("xT", [D, TFULL])
    for nm in ["wqT", "wkT", "wvT", "woT"]:
        din(nm, [D, D])
    for nm in ["bq2d", "bk2d", "bo2d", "hel2d"]:
        din(nm, [128, H])
    din("bv_row", [1, D])
    din("spiral_row", [1, H])
    din("halo", [3, 5, 128, 1024])
    din("f0col", [3, 128, 1024])
    din("betacol", [3, 128, 1024])
    din("nbrows", [3, NREC, 2, 512], BF16)
    din("cbasis", [12, 128])
    din("abasis", [5, 128])
    din("ones2", [2, 128])
    din("iotacol", [128, 1])
    din("onescol", [128, 1])
    din("ones1f", [1, 128])
    din("ones128", [128, 128])
    din("epscol", [128, 1])

    dram("recsC", [3, NREC, 12, 512], BF16)
    dram("recsA", [3, NREC, 5, 512], BF16)
    dram("wgg", [64, 16, 128, 128], BF16)
    dram("wgu", [64, 16, 128, 128], BF16)
    dram("wgd", [16, 64, 128, 128], BF16)
    dram("KT", [D, TFULL], BF16)
    dram("Vt", [H, TFULL // 128, 128, 128], BF16)
    dram("QT", [D, TOK], BF16)
    dram("z1", [D, TOK], F32)
    dram("h1f", [D, TOK], F32)
    dram("z2", [D, TOK], F32)
    dram("dbgt", [128, 160], F32)
    dram("outT", [D, TOK], F32, kind="ExternalOutput")

    ctx = _Ctx()
    ctx.nc = nc
    ctx.t = t

    with ExitStack() as es:
        E = es.enter_context
        sbd = {}

        def sbuf(name, shape, dtype):
            sbd[name] = E(nc.sbuf_tensor(name, list(shape), dtype))

        sbuf("s_cb", [12, 128], BF16)
        sbuf("s_ab", [5, 128], BF16)
        sbuf("s_ones2", [2, 128], BF16)
        sbuf("s_iota", [128, 1], F32)
        sbuf("s_onescol", [128, 1], F32)
        sbuf("s_ones1f", [1, 128], F32)
        sbuf("s_onesall", [128, 128], F32)
        sbuf("s_eps", [128, 1], F32)
        sbuf("s_ones1b", [1, 128], BF16)
        sbuf("s_hel", [128, H], F32)
        sbuf("s_bq", [128, H], F32)
        sbuf("s_bk", [128, H], F32)
        sbuf("s_bo", [128, H], F32)
        sbuf("s_sp", [1, H], F32)
        for nm in ["tb_r", "tb_sin", "tb_s2", "tb_sq2", "tb_cos", "tb_qb",
                   "tb_ks", "tb_kb", "tb_t1", "tb_sb"]:
            sbuf(nm, [128, H], F32)
        sbuf("tb_sr", [1, H], F32)
        sbuf("rowsA", [128, TOK], F32)
        sbuf("rowsB", [128, TOK], F32)

        sbuf("arA", [128, 65536], U8)
        sbuf("arB", [128, 65536], U8)
        sbuf("arC", [128, 49152], U8)

        def carve(ar, off, dtype, n, rearr=None, **kw):
            esz = 2 if dtype == BF16 else 4
            ap = sbd[ar][:, off:off + n * esz].bitcast(dtype)
            if rearr:
                ap = ap.rearrange(rearr, **kw)
            return ap

        ctx.carve = carve
        ctx.sb = sbd

        pb = [E(nc.psum_tensor(f"pb{i}", [128, 512], F32)) for i in range(6)]
        psS1 = E(nc.psum_tensor("psS1", [1, 512], F32))
        psS2 = E(nc.psum_tensor("psS2", [1, 512], F32))
        ctx.pb, ctx.psS1, ctx.psS2 = pb, psS1, psS2

        sems = {}
        for nm in ["cl", "cg", "tabv", "tabs", "tmm",
                   "p1l", "p1v", "p1s",
                   "wls", "wln", "wmm", "wact", "wve", "wst",
                   "q_xl", "q_wl", "q_mm", "q_ev", "q_st",
                   "a_kv", "a_q", "a_s", "a_x", "a_v", "a_d", "a_f",
                   "o_wl", "o_mm", "o_ev", "o_xl", "o_z", "o_sq", "o_st",
                   "o_sc", "o_s2",
                   "l_zl", "l_ap", "l_hb",
                   "f_wl", "f_mm", "f_sil", "f_dwl", "f_dmm", "f_ev",
                   "f_hl", "f_z2", "f_sq", "f_st", "f_sc", "f_s2", "f_mu",
                   "g_zl", "g_ap", "g_st", "g_bc", "g_cp", "vsf", "asf"]:
            sems[nm] = _S(E(nc.semaphore(nm)))
        ctx.S = sems

        with nc.Block() as block:
            phases = [_ph0, _ph1_prep, _ph1_main, _ph2_qkv, _ph3_attn,
                      _ph4_wo_ln1, _ph5_ffn, _ph6_out]
            for pi, ph in enumerate(phases):
                if pi <= upto:
                    ph(ctx, block)
            if upto < len(phases) - 1:
                def _tail(g):
                    for nm in ["p1s", "wst", "q_st", "o_st", "f_st", "g_st"]:
                        if sems[nm].n:
                            g.wait_ge(sems[nm].h, sems[nm].n)
                block.gpsimd(_tail)

    return nc, ctx


def _vs(ctx, v, instr):
    n = ctx.S["vsf"].inc(instr)
    v.wait_ge(ctx.S["vsf"].h, n)


def _as(ctx, s, instr):
    n = ctx.S["asf"].inc(instr)
    s.wait_ge(ctx.S["asf"].h, n)


def _ph0(ctx, block):
    t, sb, S, pb = ctx.t, ctx.sb, ctx.S, ctx.pb
    s_bv = ctx.carve("arA", 49152, BF16, D)[0:1, :]
    ctx.s_bv = s_bv

    def g_loads(g):
        for nm, dst in [("cbasis", "s_cb"), ("abasis", "s_ab"), ("ones2", "s_ones2")]:
            S["cg"].inc(g.dma_start(out=sb[dst][:], in_=t[nm][:, :]), 16)
        S["cg"].inc(g.dma_start(out=sb["s_ones1b"][:], in_=t["ones1f"][:, :]), 16)
        S["cg"].inc(g.dma_start(out=s_bv, in_=t["bv_row"][:, :]), 16)
    block.gpsimd(g_loads)

    def s_loads(sy):
        for nm, dst in [("iotacol", "s_iota"), ("onescol", "s_onescol"),
                        ("ones1f", "s_ones1f"), ("ones128", "s_onesall"), ("epscol", "s_eps"), ("hel2d", "s_hel"), ("bq2d", "s_bq"),
                        ("bk2d", "s_bk"), ("bo2d", "s_bo"), ("spiral_row", "s_sp")]:
            S["cl"].inc(sy.dma_start(out=sb[dst][:], in_=t[nm][:, :]), 16)
    block.sync(s_loads)

    def v1(v):
        v.wait_ge(S["cl"].h, S["cl"].n)
        BIG = 1e30
        _vs(ctx, v, v.tensor_scalar(sb["tb_t1"][:], sb["s_hel"][:], float(np.pi), BIG,
                                    op0=ALU.subtract, op1=ALU.mult))
        _vs(ctx, v, v.tensor_scalar(sb["tb_t1"][:], sb["tb_t1"][:], 0.0, 1.0,
                                    op0=ALU.max, op1=ALU.min))
        _vs(ctx, v, v.tensor_scalar(sb["tb_s2"][:], sb["s_hel"][:], float(-np.pi), -BIG,
                                    op0=ALU.subtract, op1=ALU.mult))
        _vs(ctx, v, v.tensor_scalar(sb["tb_s2"][:], sb["tb_s2"][:], 0.0, 1.0,
                                    op0=ALU.max, op1=ALU.min))
        _vs(ctx, v, v.tensor_tensor(sb["tb_t1"][:], sb["tb_t1"][:], sb["tb_s2"][:], op=ALU.subtract))
        _vs(ctx, v, v.scalar_tensor_tensor(sb["tb_r"][:], sb["tb_t1"][:], -TWO_PI, sb["s_hel"][:],
                                           op0=ALU.mult, op1=ALU.add))
        S["tabv"].inc(v.tensor_scalar(sb["tb_r"][:], sb["tb_r"][:], float(np.pi),
                                      float(-np.pi), op0=ALU.min, op1=ALU.max))
    block.vector(v1)

    def a1(s):
        s.wait_ge(S["tabv"].h, S["tabv"].n)
        _as(ctx, s, s.activation(sb["tb_sin"][:], sb["tb_r"][:], AF.Sin, bias=0.0, scale=1.0))
        _as(ctx, s, s.activation(sb["tb_s2"][:], sb["tb_r"][:], AF.Sin, bias=0.0, scale=0.5))
        S["tabs"].inc(s.activation(sb["tb_sq2"][:], sb["tb_s2"][:], AF.Square,
                                   bias=0.0, scale=1.0))
    block.scalar(a1)

    def v2(v):
        v.wait_ge(S["tabs"].h, S["tabs"].n)
        _vs(ctx, v, v.tensor_scalar(sb["tb_cos"][:], sb["tb_sq2"][:], -2.0, 1.0, op0=ALU.mult, op1=ALU.add))
        _vs(ctx, v, v.tensor_tensor(sb["tb_qb"][:], sb["tb_cos"][:], sb["s_bq"][:], op=ALU.mult))
        S["tabv"].inc(v.tensor_scalar(sb["tb_sr"][:], sb["s_sp"][:],
                                      float(1.0 / math.sqrt(DH)), 0.0,
                                      op0=ALU.mult, op1=ALU.add))
    block.vector(v2)

    def t1(te):
        te.wait_ge(S["tabv"].h, S["tabv"].n)
        S["tmm"].inc(te.matmul(pb[0][:, 0:H], sb["s_ones1f"][:], sb["tb_sr"][:],
                               start=True, stop=True))
    block.tensor(t1)

    def a2(s):
        s.wait_ge(S["tmm"].h, S["tmm"].n)
        S["tabs"].inc(s.copy(sb["tb_sb"][:], pb[0][:, 0:H]))
    block.scalar(a2)

    def v3(v):
        v.wait_ge(S["tabs"].h, S["tabs"].n)
        _vs(ctx, v, v.tensor_tensor(sb["tb_ks"][:], sb["tb_sin"][:], sb["tb_sb"][:], op=ALU.mult))
        S["tabv"].inc(v.tensor_tensor(sb["tb_kb"][:], sb["tb_ks"][:], sb["s_bk"][:],
                                      op=ALU.mult))
    block.vector(v3)

    def gdbg(g):
        g.wait_ge(S["tabv"].h, S["tabv"].n)
        for i, nm in enumerate(["tb_cos", "tb_qb", "tb_ks", "tb_kb", "s_hel", "tb_r", "tb_sin", "tb_s2", "tb_sq2", "tb_sb"]):
            S["cg"].inc(g.dma_start(out=t["dbgt"][:, i * H:(i + 1) * H], in_=sb[nm][:]), 16)
    block.gpsimd(gdbg)


def _ph1_prep(ctx, block):
    t, sb, S, carve = ctx.t, ctx.sb, ctx.S, ctx.carve
    d = float(STEP64)
    d2, d3 = d * d, d * d * d

    IN = [carve("arB", i * 4096, F32, 1024) for i in range(7)]
    TMP = [carve("arB", (7 + i) * 4096, F32, 1024) for i in range(8)]
    OUT = [carve("arC", i * 2048, BF16, 1024) for i in range(12)]

    for w in range(3):
        base_l = S["p1l"].n

        def loads(sy, w=w):
            if w > 0:
                sy.wait_ge(S["p1v"].h, 6 * w)
            for a in range(5):
                S["p1l"].inc(sy.dma_start(out=IN[a], in_=t["halo"][w, a]), 16)
            S["p1l"].inc(sy.dma_start(out=IN[5], in_=t["f0col"][w]), 16)
            S["p1l"].inc(sy.dma_start(out=IN[6], in_=t["betacol"][w]), 16)
        block.sync(loads)

        stores = []

        def vops(v, w=w, base_l=base_l, stores=stores):
            v.wait_ge(S["p1l"].h, base_l + 7 * 16)
            if w > 0:
                v.wait_ge(S["p1s"].h, 17 * 16 * w)
            h0, h1, h2, h3, h4, f0, bet = IN
            Bc, C, Dc, Pp, Q, t1, t2, _x = TMP
            v.tensor_tensor(t1, h2, h0, op=ALU.subtract)
            v.tensor_scalar(Bc, t1, 0.5, 0.0, op0=ALU.mult, op1=ALU.add)
            v.scalar_tensor_tensor(t1, h1, -2.5, h0, op0=ALU.mult, op1=ALU.add)
            v.scalar_tensor_tensor(t1, h3, -0.5, t1, op0=ALU.mult, op1=ALU.add)
            v.scalar_tensor_tensor(C, h2, 2.0, t1, op0=ALU.mult, op1=ALU.add)
            v.scalar_tensor_tensor(t1, h1, -3.0, h0, op0=ALU.mult, op1=ALU.add)
            v.scalar_tensor_tensor(t1, h2, 3.0, t1, op0=ALU.mult, op1=ALU.add)
            v.scalar_tensor_tensor(t1, h3, -1.0, t1, op0=ALU.mult, op1=ALU.add)
            v.tensor_scalar(Dc, t1, -0.5, 0.0, op0=ALU.mult, op1=ALU.add)
            v.scalar_tensor_tensor(t1, h2, -2.5, h1, op0=ALU.mult, op1=ALU.add)
            v.scalar_tensor_tensor(t1, h4, -0.5, t1, op0=ALU.mult, op1=ALU.add)
            v.scalar_tensor_tensor(t1, h3, 2.0, t1, op0=ALU.mult, op1=ALU.add)
            v.tensor_tensor(t1, t1, C, op=ALU.subtract)
            v.scalar_tensor_tensor(Pp, Dc, -3.0, t1, op0=ALU.mult, op1=ALU.add)
            v.scalar_tensor_tensor(t1, h2, -3.0, h1, op0=ALU.mult, op1=ALU.add)
            v.scalar_tensor_tensor(t1, h3, 3.0, t1, op0=ALU.mult, op1=ALU.add)
            v.scalar_tensor_tensor(t1, h4, -1.0, t1, op0=ALU.mult, op1=ALU.add)
            v.tensor_scalar(t1, t1, -0.5, 0.0, op0=ALU.mult, op1=ALU.add)
            v.tensor_tensor(Q, t1, Dc, op=ALU.subtract)

            slot = [0]

            def put(g_ap, dests):
                s_hi = OUT[slot[0] % 12]; slot[0] += 1
                s_lo = OUT[slot[0] % 12]; slot[0] += 1
                v.tensor_copy(s_hi, g_ap)
                last = v.tensor_tensor(s_lo, g_ap, s_hi, op=ALU.subtract)
                for dd, which in dests:
                    stores.append((dd, s_hi if which == "hi" else s_lo))
                return last

            v.tensor_tensor(t2, f0, Dc, op=ALU.mult)
            v.tensor_tensor(t2, t2, C, op=ALU.add)
            v.tensor_tensor(t2, t2, f0, op=ALU.mult)
            v.tensor_tensor(t2, t2, Bc, op=ALU.add)
            v.tensor_tensor(t2, t2, f0, op=ALU.mult)
            v.tensor_tensor(t2, t2, h1, op=ALU.add)
            S["p1v"].inc(put(t2, [(("recsC", 0), "hi"), (("recsC", 1), "hi"),
                                  (("recsC", 2), "lo")]))
            v.tensor_scalar(t1, Dc, 3.0, 0.0, op0=ALU.mult, op1=ALU.add)
            v.tensor_tensor(t2, t1, f0, op=ALU.mult)
            v.scalar_tensor_tensor(t2, C, 2.0, t2, op0=ALU.mult, op1=ALU.add)
            v.tensor_tensor(t2, t2, f0, op=ALU.mult)
            v.tensor_tensor(t2, t2, Bc, op=ALU.add)
            v.tensor_scalar(t2, t2, d, 0.0, op0=ALU.mult, op1=ALU.add)
            S["p1v"].inc(put(t2, [(("recsC", 3), "hi"), (("recsC", 4), "hi"),
                                  (("recsC", 5), "lo")]))
            v.tensor_tensor(t2, t1, f0, op=ALU.mult)
            v.tensor_tensor(t2, t2, C, op=ALU.add)
            v.tensor_scalar(t2, t2, d2, 0.0, op0=ALU.mult, op1=ALU.add)
            S["p1v"].inc(put(t2, [(("recsC", 6), "hi"), (("recsC", 7), "hi"),
                                  (("recsC", 8), "lo")]))
            v.tensor_scalar(t2, Dc, d3, 0.0, op0=ALU.mult, op1=ALU.add)
            S["p1v"].inc(put(t2, [(("recsC", 9), "hi"), (("recsC", 10), "hi"),
                                  (("recsC", 11), "lo")]))
            v.tensor_tensor(t1, Q, bet, op=ALU.mult)
            v.tensor_scalar(t2, Pp, d2, 0.0, op0=ALU.mult, op1=ALU.add)
            v.scalar_tensor_tensor(t2, t1, -d3, t2, op0=ALU.mult, op1=ALU.add)
            S["p1v"].inc(put(t2, [(("recsA", 0), "hi"), (("recsA", 1), "lo")]))
            v.tensor_scalar(t2, Q, d3, 0.0, op0=ALU.mult, op1=ALU.add)
            S["p1v"].inc(put(t2, [(("recsA", 2), "hi"), (("recsA", 3), "hi"),
                                  (("recsA", 4), "lo")]))
        block.vector(vops)

        pv_base = S["p1v"].n - 6

        def gstores(g, w=w, stores=stores, pv_base=pv_base):
            per_group = [3, 3, 3, 3, 2, 3]
            done = 0
            for gi, cnt in enumerate(per_group):
                g.wait_ge(S["p1v"].h, pv_base + gi + 1)
                for _ in range(cnt):
                    (recname, row), src = stores[done]
                    done += 1
                    dst = ctx.t[recname][w][:, row, :].rearrange("(p k) j -> p k j", p=128)
                    S["p1s"].inc(g.dma_start(
                        out=dst, in_=src.rearrange("p (k j) -> p k j", k=2)), 16)
        block.gpsimd(gstores)


def _ph1_main(ctx, block):
    t, sb, S, pb = ctx.t, ctx.sb, ctx.S, ctx.pb
    carve = ctx.carve

    NSL = 4
    crb = [carve("arB", i * 4096, BF16, 2048, "p (r j) -> p r j", r=4)[0:12]
           for i in range(NSL)]
    arb = [carve("arB", 16384 + i * 4096, BF16, 2048, "p (r j) -> p r j", r=4)[0:5]
           for i in range(NSL)]
    nbb = [carve("arB", 32768 + i * 4096, BF16, 2048, "p (r j) -> p r j", r=4)[0:2]
           for i in range(NSL)]
    sR = [carve("arC", 24576 + i * 2048, F32, 512) for i in range(2)]
    sS = [carve("arC", 28672 + i * 2048, F32, 512) for i in range(2)]
    sSA = [carve("arC", 32768 + i * 2048, F32, 512) for i in range(2)]
    stg = [carve("arC", 36864 + i * 1024, BF16, 512) for i in range(3)]

    NB = 64
    RPB = 4
    base = dict(wls=S["wls"].n, wmm=S["wmm"].n, wact=S["wact"].n,
                wve=S["wve"].n, wst=S["wst"].n, p1s=S["p1s"].n, tabs=S["tabs"].n,
                cg=S["cg"].n, p1v=S["p1v"].n)

    def syncs(sy):
        sy.wait_ge(S["p1s"].h, base["p1s"])
        for w in range(3):
            for b in range(NB):
                gb = w * NB + b
                if gb >= NSL:
                    sy.wait_ge(S["wmm"].h, base["wmm"] + (gb - NSL + 1) * RPB)
                sl = gb % NSL
                S["wls"].inc(sy.dma_start(
                    out=crb[sl], in_=t["recsC"][w, b * RPB:(b + 1) * RPB].rearrange("r a j -> a r j")), 16)
                S["wls"].inc(sy.dma_start(
                    out=arb[sl], in_=t["recsA"][w, b * RPB:(b + 1) * RPB].rearrange("r a j -> a r j")), 16)
                S["wls"].inc(sy.dma_start(
                    out=nbb[sl], in_=t["nbrows"][w, b * RPB:(b + 1) * RPB].rearrange("r a j -> a r j")), 16)
    block.sync(syncs)

    def tes(te):
        te.wait_ge(S["tabs"].h, base["tabs"])
        te.wait_ge(S["cg"].h, base["cg"])
        te.wait_ge(S["p1v"].h, base["p1v"])
        for w in range(3):
            for b in range(NB):
                gb = w * NB + b
                sl = gb % NSL
                te.wait_ge(S["wls"].h, base["wls"] + 48 * (gb + 1))
                for i in range(RPB):
                    r = gb * RPB + i
                    if r >= 2:
                        te.wait_ge(S["wve"].h, base["wve"] + 2 * (r - 1))
                        te.wait_ge(S["wact"].h, base["wact"] + 2 * (r - 1))
                    te.matmul(pb[r % 2][:, :], sb["s_cb"][:], crb[sl][:, i, :],
                              start=True, stop=True)
                    te.matmul(pb[2 + r % 2][:, :], sb["s_ab"][:], arb[sl][:, i, :],
                              start=True, stop=True)
                    S["wmm"].inc(te.matmul(pb[4 + r % 2][:, :], sb["s_ones2"][:],
                                           nbb[sl][:, i, :], start=True, stop=True))
    block.tensor(tes)

    def acts(s):
        for r in range(3 * NB * RPB):
            s.wait_ge(S["wmm"].h, base["wmm"] + r + 1)
            if r >= 2:
                s.wait_ge(S["wve"].h, base["wve"] + 2 * (r - 2) + 1)
            s.activation(sR[r % 2], pb[4 + r % 2][:, :], AF.Relu,
                         bias=sb["s_iota"][:, 0:1], scale=1.0)
            S["wact"].inc(s.activation(sS[r % 2], sR[r % 2], AF.Square,
                                       bias=0.0, scale=1.0), 2)
    block.scalar(acts)

    def ves(v):
        for r in range(3 * NB * RPB):
            v.wait_ge(S["wact"].h, base["wact"] + 2 * (r + 1))
            v.wait_ge(S["wmm"].h, base["wmm"] + r + 1)
            if r >= 3:
                v.wait_ge(S["wst"].h, base["wst"] + 16 * (r - 2))
            v.tensor_tensor(sSA[r % 2], sS[r % 2], pb[2 + r % 2][:, :], op=ALU.mult)
            S["wve"].inc(v.tensor_tensor(stg[r % 3], sSA[r % 2], pb[r % 2][:, :],
                                         op=ALU.add), 2)
    block.vector(ves)

    def gps(g):
        for w in range(3):
            nic = [16, 16, 64][w]
            wt = [t["wgg"], t["wgu"], t["wgd"]][w]
            for b in range(NB):
                for i in range(RPB):
                    r = (w * NB + b) * RPB + i
                    rec = b * RPB + i
                    ic, ob = rec % nic, rec // nic
                    g.wait_ge(S["wve"].h, base["wve"] + 2 * (r + 1))
                    dst = wt[4 * ob:4 * ob + 4, ic].rearrange("k p j -> p k j")
                    S["wst"].inc(g.dma_start(
                        out=dst, in_=stg[r % 3].rearrange("p (k j) -> p k j", k=4)), 16)
    block.gpsimd(gps)


def _ph2_qkv(ctx, block):
    t, sb, S, pb = ctx.t, ctx.sb, ctx.S, ctx.pb
    carve = ctx.carve
    s_bv = ctx.s_bv

    vxT = carve("arB", 0, BF16, 16 * 2048, "p (c j) -> p c j", c=16)
    vws = [carve("arC", i * 4096, BF16, 16 * 128, "p (c j) -> p c j", c=16) for i in range(2)]
    vvw = [carve("arC", 8192 + i * 16384, BF16, 16 * 512, "p (c j) -> p c j", c=16) for i in range(2)]
    stg = [carve("arC", 40960 + i * 1024, BF16, 512) for i in range(3)]

    groups = []
    slabs = []
    for th in range(2):
        for os in range(16):
            slabs.append(("kq", "wkT", th, os))
            for qb in range(4):
                groups.append(dict(kind="K", th=th, os=os, qb=qb, slab=len(slabs) - 1))
        for ob in range(4):
            slabs.append(("v", "wvT", th, ob))
            for tt in range(16):
                groups.append(dict(kind="V", th=th, ob=ob, tt=tt, slab=len(slabs) - 1))
        if th == 0:
            for os in range(16):
                slabs.append(("kq", "wqT", th, os))
                for qb in range(4):
                    groups.append(dict(kind="Q", th=th, os=os, qb=qb, slab=len(slabs) - 1))
    slab_last = {}
    for gi, g in enumerate(groups):
        slab_last[g["slab"]] = gi
    kq_ct = v_ct = 0
    slab_slot = {}
    for si, (knd, _, _, _) in enumerate(slabs):
        if knd == "kq":
            slab_slot[si] = kq_ct % 2
            kq_ct += 1
        else:
            slab_slot[si] = v_ct % 2
            v_ct += 1
    prev_same = {}
    last_of = {}
    for si, (knd, _, _, _) in enumerate(slabs):
        key = (knd, slab_slot[si])
        prev_same[si] = last_of.get(key)
        last_of[key] = si
    th0_last = max(gi for gi, g in enumerate(groups) if g["th"] == 0)

    base = dict(q_xl=S["q_xl"].n, q_wl=S["q_wl"].n, q_mm=S["q_mm"].n,
                q_ev=S["q_ev"].n, q_st=S["q_st"].n, wst=S["wst"].n,
                cg=S["cg"].n, wve=S["wve"].n, wact=S["wact"].n, tabv=S["tabv"].n)

    def gps(g):
        g.wait_ge(S["wve"].h, base["wve"])
        g.wait_ge(S["wst"].h, base["wst"])
        S["q_xl"].inc(g.dma_start(
            out=vxT, in_=t["xT"][:, 0:2048].rearrange("(c p) j -> p c j", p=128)), 16)
        for si, (knd, wname, th, idx) in enumerate(slabs):
            if th == 1 and slabs[si - 1][2] == 0:
                g.wait_ge(S["q_mm"].h, base["q_mm"] + th0_last + 1)
                S["q_xl"].inc(g.dma_start(
                    out=vxT, in_=t["xT"][:, 2048:4096].rearrange("(c p) j -> p c j", p=128)), 16)
            ps = prev_same[si]
            if ps is not None:
                g.wait_ge(S["q_mm"].h, base["q_mm"] + slab_last[ps] + 1)
            if knd == "kq":
                dst = vws[slab_slot[si]]
                src = t[wname][:, idx * 128:(idx + 1) * 128].rearrange("(c p) j -> p c j", p=128)
            else:
                dst = vvw[slab_slot[si]]
                src = t[wname][:, idx * 512:(idx + 1) * 512].rearrange("(c p) j -> p c j", p=128)
            S["q_wl"].inc(g.dma_start(out=dst, in_=src), 16)
    block.gpsimd(gps)

    def tes(te):
        te.wait_ge(S["cg"].h, base["cg"])
        te.wait_ge(S["tabv"].h, base["tabv"])
        te.wait_ge(S["wve"].h, base["wve"])
        te.wait_ge(S["wact"].h, base["wact"])
        for gi, gr in enumerate(groups):
            need_xl = 16 * (1 if gr["th"] == 0 else 2)
            te.wait_ge(S["q_xl"].h, base["q_xl"] + need_xl)
            te.wait_ge(S["q_wl"].h, base["q_wl"] + 16 * (gr["slab"] + 1))
            if gi >= 2:
                te.wait_ge(S["q_ev"].h, base["q_ev"] + gi - 1)
            p = pb[gi % 2]
            if gr["kind"] in ("K", "Q"):
                sl = slab_slot[gr["slab"]]
                for dc in range(NDC):
                    mm = te.matmul(p[:, :], vws[sl][:, dc, :],
                                   vxT[:, dc, gr["qb"] * 512:(gr["qb"] + 1) * 512],
                                   start=(dc == 0), stop=(dc == NDC - 1))
                S["q_mm"].inc(mm)
            else:
                sl = slab_slot[gr["slab"]]
                tt = gr["tt"]
                for dc in range(NDC):
                    te.matmul(p[:, :], vxT[:, dc, tt * 128:(tt + 1) * 128],
                              vvw[sl][:, dc, :], start=(dc == 0), stop=False)
                S["q_mm"].inc(te.matmul(
                    p[:, :], sb["s_ones1b"][:],
                    s_bv[0:1, gr["ob"] * 512:(gr["ob"] + 1) * 512],
                    start=False, stop=True))
    block.tensor(tes)

    def acts(s):
        for gi, gr in enumerate(groups):
            s.wait_ge(S["q_mm"].h, base["q_mm"] + gi + 1)
            if gi >= 3:
                s.wait_ge(S["q_st"].h, base["q_st"] + 16 * (gi - 2))
            p = pb[gi % 2]
            dst = stg[gi % 3]
            if gr["kind"] == "K":
                os = gr["os"]
                S["q_ev"].inc(s.activation(dst, p[:, :], AF.Identity,
                                           bias=sb["tb_kb"][:, os:os + 1],
                                           scale=sb["tb_ks"][:, os:os + 1]))
            elif gr["kind"] == "Q":
                os = gr["os"]
                S["q_ev"].inc(s.activation(dst, p[:, :], AF.Identity,
                                           bias=sb["tb_qb"][:, os:os + 1],
                                           scale=sb["tb_cos"][:, os:os + 1]))
            else:
                S["q_ev"].inc(s.copy(dst, p[:, :]))
    block.scalar(acts)

    def syncs(sy):
        for gi, gr in enumerate(groups):
            sy.wait_ge(S["q_ev"].h, base["q_ev"] + gi + 1)
            src = stg[gi % 3]
            if gr["kind"] == "K":
                dst = t["KT"][gr["os"] * 128:(gr["os"] + 1) * 128,
                              gr["th"] * 2048 + gr["qb"] * 512:
                              gr["th"] * 2048 + (gr["qb"] + 1) * 512]
                S["q_st"].inc(sy.dma_start(out=dst, in_=src), 16)
            elif gr["kind"] == "Q":
                dst = t["QT"][gr["os"] * 128:(gr["os"] + 1) * 128,
                              gr["qb"] * 512:(gr["qb"] + 1) * 512]
                S["q_st"].inc(sy.dma_start(out=dst, in_=src), 16)
            else:
                tc = gr["th"] * 16 + gr["tt"]
                dst = t["Vt"][gr["ob"] * 4:(gr["ob"] + 1) * 4, tc].rearrange("h p j -> p h j")
                S["q_st"].inc(sy.dma_start(out=dst, in_=src.rearrange("p (h j) -> p h j", h=4)), 16)
    block.sync(syncs)


def _ph3_attn(ctx, block):
    t, sb, S, pb = ctx.t, ctx.sb, ctx.S, ctx.pb
    psS1 = ctx.psS1
    carve = ctx.carve

    vK = [carve("arB", i * 8192, BF16, 4096) for i in range(2)]
    vV = [carve("arB", 16384 + i * 8192, BF16, 32 * 128, "p (c j) -> p c j", c=32) for i in range(2)]
    vQ = [carve("arB", 32768 + i * 1024, BF16, 512) for i in range(2)]
    vprob = [carve("arB", 34816 + i * 1024, BF16, 512) for i in range(3)]
    dacc = [carve("arB", 38912 + i * 2048, F32, 512) for i in range(2)]
    vDrb = carve("arB", 43008, F32, 512)
    r_dr = ctx.sb["rowsB"][0:1, 0:512]
    normT = carve("arA", 0, BF16, 16 * 2048, "p (h j) -> p h j", h=16)

    NC = 32
    base = dict(a_kv=S["a_kv"].n, a_q=S["a_q"].n, a_s=S["a_s"].n, a_x=S["a_x"].n,
                a_v=S["a_v"].n, a_d=S["a_d"].n, a_f=S["a_f"].n,
                q_mm=S["q_mm"].n, q_st=S["q_st"].n, q_ev=S["q_ev"].n)

    def syncs(sy):
        sy.wait_ge(S["q_st"].h, base["q_st"])
        for h in range(H):
            if h >= 2:
                sy.wait_ge(S["a_v"].h, base["a_v"] + 128 * (h - 1))
            S["a_kv"].inc(sy.dma_start(out=vK[h % 2], in_=t["KT"][h * 128:(h + 1) * 128, :]), 16)
            S["a_kv"].inc(sy.dma_start(out=vV[h % 2], in_=t["Vt"][h].rearrange("c p j -> p c j")), 16)
            for qb in range(4):
                gq = h * 4 + qb
                if gq >= 2:
                    sy.wait_ge(S["a_s"].h, base["a_s"] + 32 * (gq - 1))
                S["a_q"].inc(sy.dma_start(
                    out=vQ[gq % 2],
                    in_=t["QT"][h * 128:(h + 1) * 128, qb * 512:(qb + 1) * 512]), 16)
    block.sync(syncs)

    def tes(te):
        te.wait_ge(S["q_mm"].h, base["q_mm"])
        te.wait_ge(S["q_ev"].h, base["q_ev"])
        for h in range(H):
            te.wait_ge(S["a_kv"].h, base["a_kv"] + 32 * (h + 1))
            for qb in range(4):
                gq = h * 4 + qb
                te.wait_ge(S["a_q"].h, base["a_q"] + 16 * (gq + 1))
                for c in range(NC):
                    wv = 33 * gq + c - 1
                    if wv >= 1:
                        te.wait_ge(S["a_x"].h, base["a_x"] + wv)
                    S["a_s"].inc(te.matmul(pb[c % 2][:, :],
                                           vK[h % 2][:, c * 128:(c + 1) * 128],
                                           vQ[gq % 2], start=True, stop=True))
                    cc = c - 1
                    if cc >= 0:
                        te.wait_ge(S["a_x"].h, base["a_x"] + 33 * gq + cc + 1)
                        if cc == 0 and gq >= 1:
                            te.wait_ge(S["a_d"].h, base["a_d"] + 34 * gq)
                        S["a_v"].inc(te.matmul(pb[2][:, :], vV[h % 2][:, cc, :],
                                               vprob[cc % 3], start=(cc == 0), stop=False))
                cc = NC - 1
                te.wait_ge(S["a_x"].h, base["a_x"] + 33 * gq + cc + 1)
                S["a_v"].inc(te.matmul(pb[2][:, :], vV[h % 2][:, cc, :],
                                       vprob[cc % 3], start=False, stop=True))
                te.wait_ge(S["a_d"].h, base["a_d"] + 34 * gq + 32)
                S["a_f"].inc(te.matmul(psS1[:, :], sb["s_onescol"][:],
                                       dacc[(NC - 1) % 2], start=True, stop=True))
                te.wait_ge(S["a_d"].h, base["a_d"] + 34 * gq + 33)
                S["a_f"].inc(te.matmul(pb[4][:, :], sb["s_ones1f"][:], r_dr,
                                       start=True, stop=True))
    block.tensor(tes)

    def acts(s):
        for h in range(H):
            for qb in range(4):
                gq = h * 4 + qb
                for c in range(NC):
                    s.wait_ge(S["a_s"].h, base["a_s"] + 32 * gq + c + 1)
                    wv = 32 * gq + c - 2
                    if wv >= 1:
                        s.wait_ge(S["a_v"].h, base["a_v"] + wv)
                    wd = 34 * gq + c - 2
                    if wd >= 1:
                        s.wait_ge(S["a_d"].h, base["a_d"] + wd)
                    S["a_x"].inc(s.activation(vprob[c % 3], pb[c % 2][:, :], AF.Exp,
                                              bias=0.0, scale=1.0))
                s.wait_ge(S["a_f"].h, base["a_f"] + 2 * gq + 2)
                S["a_x"].inc(s.copy(vDrb, pb[4][:, :]))
    block.scalar(acts)

    def ves(v):
        for h in range(H):
            for qb in range(4):
                gq = h * 4 + qb
                for c in range(NC):
                    v.wait_ge(S["a_x"].h, base["a_x"] + 33 * gq + c + 1)
                    if c == 0:
                        S["a_d"].inc(v.tensor_copy(dacc[0], vprob[0]))
                    else:
                        S["a_d"].inc(v.tensor_tensor(dacc[c % 2], dacc[(c - 1) % 2],
                                                     vprob[c % 3], op=ALU.add))
                v.wait_ge(S["a_f"].h, base["a_f"] + 2 * gq + 1)
                S["a_d"].inc(v.reciprocal(r_dr, psS1[0:1, :]))
                v.wait_ge(S["a_x"].h, base["a_x"] + 33 * (gq + 1))
                v.wait_ge(S["a_v"].h, base["a_v"] + 32 * (gq + 1))
                S["a_d"].inc(v.tensor_tensor(normT[:, h, qb * 512:(qb + 1) * 512],
                                             pb[2][:, :], vDrb, op=ALU.mult))
    block.vector(ves)


def _ph4_wo_ln1(ctx, block):
    t, sb, S, pb = ctx.t, ctx.sb, ctx.S, ctx.pb
    psS1, psS2 = ctx.psS1, ctx.psS2
    carve = ctx.carve

    normT = carve("arA", 0, BF16, 16 * 2048, "p (h j) -> p h j", h=16)
    h1T = carve("arA", 0, BF16, 16 * 2048, "p (h j) -> p h j", h=16)
    vWo = carve("arB", 0, BF16, 16 * 2048, "p (c j) -> p c j", c=16)
    vz = [carve("arC", i * 2048, F32, 512) for i in range(2)]
    vxf = [carve("arC", 4096 + i * 2048, F32, 512) for i in range(2)]
    vz1 = [carve("arC", 8192 + i * 2048, F32, 512) for i in range(2)]
    vsq = [carve("arC", 12288 + i * 2048, F32, 512) for i in range(2)]
    vzl = [carve("arC", 16384 + i * 2048, F32, 512) for i in range(2)]
    vh1 = [carve("arC", 20480 + i * 2048, F32, 512) for i in range(2)]
    vRR = carve("arC", 24576, F32, 512)
    vMM = carve("arC", 26624, F32, 512)
    r_sz = sb["rowsA"][0:1, :]
    r_sq = sb["rowsA"][32:33, :]
    r_r = sb["rowsB"][0:1, :]
    r_tmp = sb["rowsB"][32:33, :]

    base = dict(o_wl=S["o_wl"].n, o_mm=S["o_mm"].n, o_ev=S["o_ev"].n, o_xl=S["o_xl"].n,
                o_z=S["o_z"].n, o_sq=S["o_sq"].n, o_st=S["o_st"].n, o_sc=S["o_sc"].n,
                o_s2=S["o_s2"].n, a_d=S["a_d"].n, a_x=S["a_x"].n, a_v=S["a_v"].n,
                l_zl=S["l_zl"].n, l_ap=S["l_ap"].n, l_hb=S["l_hb"].n,
                g_bc=S["g_bc"].n, g_cp=S["g_cp"].n)

    def gps(g):
        g.wait_ge(S["a_d"].h, base["a_d"])
        S["o_wl"].inc(g.dma_start(
            out=vWo, in_=t["woT"].ap().rearrange("(c p) o -> p c o", p=128)), 16)
        for qb in range(4):
            for ot in range(16):
                gi = qb * 16 + ot
                g.wait_ge(S["o_z"].h, base["o_z"] + gi + 1)
                S["o_st"].inc(g.dma_start(
                    out=t["z1"][ot * 128:(ot + 1) * 128, qb * 512:(qb + 1) * 512],
                    in_=vz1[ot % 2]), 16)
        for qb in range(4):
            for dc in range(16):
                ai = qb * 16 + dc
                g.wait_ge(S["l_ap"].h, base["l_ap"] + 2 * ai + 1)
                S["o_st"].inc(g.dma_start(
                    out=t["h1f"][dc * 128:(dc + 1) * 128, qb * 512:(qb + 1) * 512],
                    in_=vh1[dc % 2]), 16)
    block.gpsimd(gps)

    def syncs(sy):
        for qb in range(4):
            for ot in range(16):
                gi = qb * 16 + ot
                if gi >= 2:
                    sy.wait_ge(S["o_z"].h, base["o_z"] + gi - 1)
                S["o_xl"].inc(sy.dma_start(
                    out=vxf[ot % 2],
                    in_=t["xT"][ot * 128:(ot + 1) * 128, qb * 512:(qb + 1) * 512]), 16)
        for qb in range(4):
            for dc in range(16):
                ai = qb * 16 + dc
                if ai == 0:
                    sy.wait_ge(S["o_st"].h, base["o_st"] + 16 * 64)
                if ai >= 2:
                    sy.wait_ge(S["l_ap"].h, base["l_ap"] + 2 * (ai - 1))
                S["l_zl"].inc(sy.dma_start(
                    out=vzl[dc % 2],
                    in_=t["z1"][dc * 128:(dc + 1) * 128, qb * 512:(qb + 1) * 512]), 16)
    block.sync(syncs)

    def tes(te):
        te.wait_ge(S["o_wl"].h, base["o_wl"] + 16)
        te.wait_ge(S["a_d"].h, base["a_d"])
        te.wait_ge(S["a_x"].h, base["a_x"])
        for qb in range(4):
            for ot in range(16):
                gi = qb * 16 + ot
                if gi >= 2:
                    te.wait_ge(S["o_ev"].h, base["o_ev"] + gi - 1)
                p = pb[ot % 2]
                for dc in range(NDC):
                    mm = te.matmul(p[:, :], vWo[:, dc, ot * 128:(ot + 1) * 128],
                                   normT[:, dc, qb * 512:(qb + 1) * 512],
                                   start=(dc == 0), stop=(dc == NDC - 1))
                S["o_mm"].inc(mm)
                po = ot - 1
                if po >= 0:
                    gp = qb * 16 + po
                    te.wait_ge(S["o_z"].h, base["o_z"] + gp + 1)
                    te.wait_ge(S["o_sq"].h, base["o_sq"] + gp + 1)
                    if po == 0:
                        te.wait_ge(S["o_sc"].h, base["o_sc"] + 2 * qb)
                    te.matmul(psS1[:, :], sb["s_onescol"][:], vz1[po % 2],
                              start=(po == 0), stop=False)
                    S["o_s2"].inc(te.matmul(psS2[:, :], sb["s_onescol"][:], vsq[po % 2],
                                            start=(po == 0), stop=False))
            po = 15
            gp = qb * 16 + po
            te.wait_ge(S["o_z"].h, base["o_z"] + gp + 1)
            te.wait_ge(S["o_sq"].h, base["o_sq"] + gp + 1)
            te.matmul(psS1[:, :], sb["s_onescol"][:], vz1[po % 2], start=False, stop=True)
            S["o_s2"].inc(te.matmul(psS2[:, :], sb["s_onescol"][:], vsq[po % 2],
                                    start=False, stop=True))
    block.tensor(tes)

    def acts(s):
        for qb in range(4):
            for ot in range(16):
                gi = qb * 16 + ot
                s.wait_ge(S["o_mm"].h, base["o_mm"] + gi + 1)
                if gi >= 2:
                    s.wait_ge(S["o_z"].h, base["o_z"] + gi - 1)
                S["o_ev"].inc(s.activation(vz[ot % 2], pb[ot % 2][:, :], AF.Identity,
                                           bias=sb["s_bo"][:, ot:ot + 1], scale=1.0))
                s.wait_ge(S["o_z"].h, base["o_z"] + gi + 1)
                if gi >= 2:
                    s.wait_ge(S["o_s2"].h, base["o_s2"] + gi - 1)
                S["o_sq"].inc(s.activation(vsq[ot % 2], vz1[ot % 2], AF.Square,
                                           bias=0.0, scale=1.0))
            s.wait_ge(S["o_s2"].h, base["o_s2"] + 16 * (qb + 1))
            s.copy(r_sz[0:1, qb * 512:(qb + 1) * 512], psS1[0:1, :])
            S["o_sc"].inc(s.copy(r_sq[0:1, qb * 512:(qb + 1) * 512], psS2[0:1, :]), 2)
    block.scalar(acts)

    def ves(v):
        for qb in range(4):
            for ot in range(16):
                gi = qb * 16 + ot
                v.wait_ge(S["o_ev"].h, base["o_ev"] + gi + 1)
                v.wait_ge(S["o_xl"].h, base["o_xl"] + 16 * (gi + 1))
                if gi >= 2:
                    v.wait_ge(S["o_sq"].h, base["o_sq"] + gi - 1)
                    v.wait_ge(S["o_st"].h, base["o_st"] + 16 * (gi - 1))
                    v.wait_ge(S["o_s2"].h, base["o_s2"] + gi - 1)
                S["o_z"].inc(v.tensor_tensor(vz1[ot % 2], vz[ot % 2], vxf[ot % 2],
                                             op=ALU.add))
        v.wait_ge(S["o_sc"].h, base["o_sc"] + 8)
        _vs(ctx, v, v.tensor_scalar(r_sz, r_sz, float(1.0 / D), 0.0, op0=ALU.mult, op1=ALU.add))
        _vs(ctx, v, v.tensor_tensor(r_tmp, r_sz, r_sz, op=ALU.mult))
        S["o_z"].inc(v.scalar_tensor_tensor(r_sq, r_sq, float(1.0 / D), r_tmp,
                                            op0=ALU.mult, op1=ALU.subtract))
    block.vector(ves)

    def acts2(s):
        s.wait_ge(S["o_z"].h, S["o_z"].n)
        S["o_sc"].inc(s.activation(r_tmp, r_sq, AF.Sqrt, bias=sb["s_eps"][0:1, 0:1], scale=1.0))
    block.scalar(acts2)

    def ves2(v):
        v.wait_ge(S["o_sc"].h, S["o_sc"].n)
        _vs(ctx, v, v.reciprocal(r_r, r_tmp))
        S["l_hb"].inc(v.tensor_tensor(r_sz, r_sz, r_r, op=ALU.mult))
    block.vector(ves2)

    lb = dict(l_hb=S["l_hb"].n, l_zl=base["l_zl"], l_ap=base["l_ap"],
              g_bc=base["g_bc"], g_cp=base["g_cp"], o_st=base["o_st"])

    def tes2(te):
        te.wait_ge(S["l_hb"].h, lb["l_hb"])
        for qb in range(4):
            if qb >= 1:
                te.wait_ge(S["l_ap"].h, lb["l_ap"] + 32 * qb)
            te.matmul(pb[4][:, :], sb["s_ones1f"][:],
                      r_r[0:1, qb * 512:(qb + 1) * 512], start=True, stop=True)
            S["g_bc"].inc(te.matmul(pb[5][:, :], sb["s_ones1f"][:],
                                    r_sz[0:1, qb * 512:(qb + 1) * 512],
                                    start=True, stop=True))
    block.tensor(tes2)

    def acts3(s):
        for qb in range(4):
            s.wait_ge(S["g_bc"].h, lb["g_bc"] + qb + 1)
            if qb >= 1:
                s.wait_ge(S["l_ap"].h, lb["l_ap"] + 32 * qb)
            s.copy(vRR, pb[4][:, :])
            S["g_cp"].inc(s.copy(vMM, pb[5][:, :]), 2)
    block.scalar(acts3)

    def ves3(v):
        for qb in range(4):
            v.wait_ge(S["g_cp"].h, lb["g_cp"] + 2 * (qb + 1))
            for dc in range(16):
                ai = qb * 16 + dc
                v.wait_ge(S["l_zl"].h, lb["l_zl"] + 16 * (ai + 1))
                if ai >= 2:
                    v.wait_ge(S["o_st"].h, lb["o_st"] + 16 * 64 + 16 * (ai - 1))
                v.tensor_tensor(vh1[dc % 2], vzl[dc % 2], vRR, op=ALU.mult)
                S["l_ap"].inc(v.tensor_tensor(vh1[dc % 2], vh1[dc % 2], vMM,
                                              op=ALU.subtract))
                S["l_ap"].inc(v.tensor_copy(h1T[:, dc, qb * 512:(qb + 1) * 512],
                                            vh1[dc % 2]))
    block.vector(ves3)


def _ph5_ffn(ctx, block):
    t, sb, S, pb = ctx.t, ctx.sb, ctx.S, ctx.pb
    psS1, psS2 = ctx.psS1, ctx.psS2
    carve = ctx.carve

    h1T = carve("arA", 0, BF16, 16 * 2048, "p (h j) -> p h j", h=16)
    interT = carve("arB", 0, BF16, 64 * 512, "p (c j) -> p c j", c=64)
    gsl = [carve("arC", i * 4096, BF16, 16 * 128, "p (c j) -> p c j", c=16) for i in range(2)]
    usl = [carve("arC", 8192 + i * 4096, BF16, 16 * 128, "p (c j) -> p c j", c=16) for i in range(2)]
    dsl = [carve("arC", 16384 + i * 4096, BF16, 16 * 128, "p (c j) -> p c j", c=16) for i in range(2)]
    vsil = [carve("arC", 24576 + i * 1024, BF16, 512) for i in range(2)]
    vz = [carve("arC", 26624 + i * 2048, F32, 512) for i in range(2)]
    vz2 = [carve("arC", 30720 + i * 2048, F32, 512) for i in range(2)]
    vsq = [carve("arC", 34816 + i * 2048, F32, 512) for i in range(2)]
    vhl = [carve("arC", 38912 + i * 2048, F32, 512) for i in range(2)]
    r_sz = sb["rowsA"][0:1, :]
    r_sq = sb["rowsA"][32:33, :]
    r_r = sb["rowsB"][0:1, :]
    r_tmp = sb["rowsB"][32:33, :]

    base = dict(f_wl=S["f_wl"].n, f_mm=S["f_mm"].n, f_sil=S["f_sil"].n,
                f_mu=S["f_mu"].n,
                f_dwl=S["f_dwl"].n, f_dmm=S["f_dmm"].n, f_ev=S["f_ev"].n,
                f_hl=S["f_hl"].n, f_z2=S["f_z2"].n, f_sq=S["f_sq"].n,
                f_st=S["f_st"].n, f_sc=S["f_sc"].n, f_s2=S["f_s2"].n,
                l_ap=S["l_ap"].n, wst=S["wst"].n, o_st=S["o_st"].n,
                o_z=S["o_z"].n, o_s2=S["o_s2"].n, o_sc=S["o_sc"].n,
                g_bc=S["g_bc"].n, g_cp=S["g_cp"].n, l_hb=S["l_hb"].n)

    def syncs(sy):
        sy.wait_ge(S["wst"].h, base["wst"])
        sy.wait_ge(S["o_z"].h, base["o_z"])
        sy.wait_ge(S["o_s2"].h, base["o_s2"])
        sy.wait_ge(S["o_st"].h, base["o_st"])
        for tb in range(4):
            for it in range(64):
                si = tb * 64 + it
                if si >= 2:
                    sy.wait_ge(S["f_mm"].h, base["f_mm"] + 2 * (si - 1))
                S["f_wl"].inc(sy.dma_start(
                    out=gsl[it % 2], in_=t["wgg"][it].rearrange("c p j -> p c j")), 16)
                S["f_wl"].inc(sy.dma_start(
                    out=usl[it % 2], in_=t["wgu"][it].rearrange("c p j -> p c j")), 16)
            for ot in (range(16) if _PH5_DOWN else []):
                hi = tb * 16 + ot
                for qq in range(4):
                    dq = ot * 4 + qq
                    di = tb * 64 + dq
                    if di == 0:
                        sy.wait_ge(S["l_ap"].h, base["l_ap"])
                        sy.wait_ge(S["g_cp"].h, base["g_cp"])
                    if di >= 2:
                        sy.wait_ge(S["f_dmm"].h, base["f_dmm"] + di - 1)
                    S["f_dwl"].inc(sy.dma_start(
                        out=dsl[dq % 2],
                        in_=t["wgd"][ot, qq * 16:(qq + 1) * 16].rearrange("c p j -> p c j")), 16)
                if hi >= 2:
                    sy.wait_ge(S["f_z2"].h, base["f_z2"] + hi - 1)
                S["f_hl"].inc(sy.dma_start(
                    out=vhl[ot % 2],
                    in_=t["h1f"][ot * 128:(ot + 1) * 128, tb * 512:(tb + 1) * 512]), 16)
    block.sync(syncs)

    def tes(te):
        te.wait_ge(S["l_ap"].h, base["l_ap"])
        te.wait_ge(S["g_cp"].h, base["g_cp"])
        for tb in range(4):
            for it in range(64):
                si = tb * 64 + it
                te.wait_ge(S["f_wl"].h, base["f_wl"] + 32 * (si + 1))
                if _PH5_DOWN and tb >= 1 and it == 0:
                    te.wait_ge(S["f_dmm"].h, base["f_dmm"] + 64 * tb)
                if si >= 2:
                    te.wait_ge(S["f_sil"].h, base["f_sil"] + si - 1)
                    te.wait_ge(S["f_mu"].h, base["f_mu"] + si - 1)
                for dc in range(NDC):
                    mm = te.matmul(pb[2 * (si % 2)][:, :], gsl[it % 2][:, dc, :],
                                   h1T[:, dc, tb * 512:(tb + 1) * 512],
                                   start=(dc == 0), stop=(dc == NDC - 1))
                S["f_mm"].inc(mm)
                for dc in range(NDC):
                    mm = te.matmul(pb[1 + 2 * (si % 2)][:, :], usl[it % 2][:, dc, :],
                                   h1T[:, dc, tb * 512:(tb + 1) * 512],
                                   start=(dc == 0), stop=(dc == NDC - 1))
                S["f_mm"].inc(mm)
            for ot in (range(16) if _PH5_DOWN else []):
                hi = tb * 16 + ot
                te.wait_ge(S["f_mu"].h, base["f_mu"] + 64 * (tb + 1))
                if hi >= 2:
                    te.wait_ge(S["f_ev"].h, base["f_ev"] + hi - 1)
                p = pb[4 + ot % 2]
                for qq in range(4):
                    di = tb * 64 + ot * 4 + qq
                    te.wait_ge(S["f_dwl"].h, base["f_dwl"] + 16 * (di + 1))
                    for c in range(16):
                        ic = qq * 16 + c
                        mm = te.matmul(p[:, :], dsl[(ot * 4 + qq) % 2][:, c, :],
                                       interT[:, ic, :],
                                       start=(ic == 0), stop=(ic == 63))
                    S["f_dmm"].inc(mm)
                po = ot - 1
                if po >= 0:
                    hp = tb * 16 + po
                    te.wait_ge(S["f_z2"].h, base["f_z2"] + hp + 1)
                    te.wait_ge(S["f_sq"].h, base["f_sq"] + hp + 1)
                    if po == 0:
                        te.wait_ge(S["f_sc"].h, base["f_sc"] + 2 * tb)
                    te.matmul(psS1[:, :], sb["s_onescol"][:], vz2[po % 2],
                              start=(po == 0), stop=False)
                    S["f_s2"].inc(te.matmul(psS2[:, :], sb["s_onescol"][:], vsq[po % 2],
                                            start=(po == 0), stop=False))
            if _PH5_DOWN:
                po = 15
                hp = tb * 16 + po
                te.wait_ge(S["f_z2"].h, base["f_z2"] + hp + 1)
                te.wait_ge(S["f_sq"].h, base["f_sq"] + hp + 1)
                te.matmul(psS1[:, :], sb["s_onescol"][:], vz2[po % 2], start=False, stop=True)
                S["f_s2"].inc(te.matmul(psS2[:, :], sb["s_onescol"][:], vsq[po % 2],
                                        start=False, stop=True))
    block.tensor(tes)

    def acts(s):
        for tb in range(4):
            for it in range(64):
                si = tb * 64 + it
                s.wait_ge(S["f_mm"].h, base["f_mm"] + 2 * si + 1)
                if si >= 2:
                    s.wait_ge(S["f_mu"].h, base["f_mu"] + si - 1)
                S["f_sil"].inc(s.activation(vsil[si % 2], pb[2 * (si % 2)][:, :],
                                            AF.Silu, bias=0.0, scale=1.0))
            for ot in (range(16) if _PH5_DOWN else []):
                hi = tb * 16 + ot
                s.wait_ge(S["f_dmm"].h, base["f_dmm"] + 4 * (hi + 1))
                if hi >= 2:
                    s.wait_ge(S["f_z2"].h, base["f_z2"] + hi - 1)
                S["f_ev"].inc(s.activation(vz[ot % 2], pb[4 + ot % 2][:, :], AF.Copy,
                                           bias=0.0, scale=1.0))
                s.wait_ge(S["f_z2"].h, base["f_z2"] + hi + 1)
                if hi >= 2:
                    s.wait_ge(S["f_s2"].h, base["f_s2"] + hi - 1)
                S["f_sq"].inc(s.activation(vsq[ot % 2], vz2[ot % 2], AF.Square,
                                           bias=0.0, scale=1.0))
            if _PH5_DOWN:
                s.wait_ge(S["f_s2"].h, base["f_s2"] + 16 * (tb + 1))
                if tb == 0:
                    s.wait_ge(S["g_bc"].h, base["g_bc"])
                s.copy(r_sz[0:1, tb * 512:(tb + 1) * 512], psS1[0:1, :])
                S["f_sc"].inc(s.copy(r_sq[0:1, tb * 512:(tb + 1) * 512], psS2[0:1, :]), 2)
    block.scalar(acts)

    def ves(v):
        for tb in range(4):
            for it in range(64):
                si = tb * 64 + it
                v.wait_ge(S["f_sil"].h, base["f_sil"] + si + 1)
                v.wait_ge(S["f_mm"].h, base["f_mm"] + 2 * si + 2)
                if _PH5_DOWN and tb >= 1 and it == 0:
                    v.wait_ge(S["f_dmm"].h, base["f_dmm"] + 64 * tb)
                S["f_mu"].inc(v.tensor_tensor(interT[:, it, :], vsil[si % 2],
                                               pb[1 + 2 * (si % 2)][:, :], op=ALU.mult))
            for ot in (range(16) if _PH5_DOWN else []):
                hi = tb * 16 + ot
                v.wait_ge(S["f_ev"].h, base["f_ev"] + hi + 1)
                v.wait_ge(S["f_hl"].h, base["f_hl"] + 16 * (hi + 1))
                if hi >= 2:
                    v.wait_ge(S["f_sq"].h, base["f_sq"] + hi - 1)
                    v.wait_ge(S["f_st"].h, base["f_st"] + 16 * (hi - 1))
                    v.wait_ge(S["f_s2"].h, base["f_s2"] + hi - 1)
                S["f_z2"].inc(v.tensor_tensor(vz2[ot % 2], vz[ot % 2], vhl[ot % 2],
                                              op=ALU.add))
        if not _PH5_DOWN:
            return
        v.wait_ge(S["f_sc"].h, base["f_sc"] + 8)
        _vs(ctx, v, v.tensor_scalar(r_sz, r_sz, float(1.0 / D), 0.0, op0=ALU.mult, op1=ALU.add))
        _vs(ctx, v, v.tensor_tensor(r_tmp, r_sz, r_sz, op=ALU.mult))
        S["f_z2"].inc(v.scalar_tensor_tensor(r_sq, r_sq, float(1.0 / D), r_tmp,
                                             op0=ALU.mult, op1=ALU.subtract))
    block.vector(ves)

    def gps(g):
        for tb in (range(4) if _PH5_DOWN else []):
            for ot in range(16):
                hi = tb * 16 + ot
                g.wait_ge(S["f_z2"].h, base["f_z2"] + hi + 1)
                S["f_st"].inc(g.dma_start(
                    out=t["z2"][ot * 128:(ot + 1) * 128, tb * 512:(tb + 1) * 512],
                    in_=vz2[ot % 2]), 16)
    block.gpsimd(gps)

    def acts2(s):
        if not _PH5_DOWN:
            return
        s.wait_ge(S["f_z2"].h, S["f_z2"].n)
        S["f_sc"].inc(s.activation(r_tmp, r_sq, AF.Sqrt, bias=sb["s_eps"][0:1, 0:1], scale=1.0))
    block.scalar(acts2)

    def ves2(v):
        if not _PH5_DOWN:
            return
        v.wait_ge(S["f_sc"].h, S["f_sc"].n)
        _vs(ctx, v, v.reciprocal(r_r, r_tmp))
        S["l_hb"].inc(v.tensor_tensor(r_sz, r_sz, r_r, op=ALU.mult))
    block.vector(ves2)


def _ph6_out(ctx, block):
    t, sb, S, pb = ctx.t, ctx.sb, ctx.S, ctx.pb
    carve = ctx.carve

    vzl = [carve("arC", i * 2048, F32, 512) for i in range(2)]
    vout = [carve("arC", 4096 + i * 2048, F32, 512) for i in range(2)]
    vRR = carve("arC", 8192, F32, 512)
    vMM = carve("arC", 10240, F32, 512)
    r_sz = sb["rowsA"][0:1, :]
    r_r = sb["rowsB"][0:1, :]

    base = dict(l_hb=S["l_hb"].n, g_zl=S["g_zl"].n, g_ap=S["g_ap"].n,
                g_st=S["g_st"].n, g_bc=S["g_bc"].n, g_cp=S["g_cp"].n,
                f_st=S["f_st"].n, f_mm=S["f_mm"].n, f_ev=S["f_ev"].n)

    def tes(te):
        te.wait_ge(S["l_hb"].h, base["l_hb"])
        te.wait_ge(S["f_ev"].h, base["f_ev"])
        for qb in range(4):
            if qb >= 1:
                te.wait_ge(S["g_ap"].h, base["g_ap"] + 32 * qb)
            te.matmul(pb[4][:, :], sb["s_ones1f"][:],
                      r_r[0:1, qb * 512:(qb + 1) * 512], start=True, stop=True)
            S["g_bc"].inc(te.matmul(pb[5][:, :], sb["s_ones1f"][:],
                                    r_sz[0:1, qb * 512:(qb + 1) * 512],
                                    start=True, stop=True))
    block.tensor(tes)

    def acts(s):
        for qb in range(4):
            s.wait_ge(S["g_bc"].h, base["g_bc"] + qb + 1)
            if qb >= 1:
                s.wait_ge(S["g_ap"].h, base["g_ap"] + 32 * qb)
            s.copy(vRR, pb[4][:, :])
            S["g_cp"].inc(s.copy(vMM, pb[5][:, :]), 2)
    block.scalar(acts)

    def syncs(sy):
        sy.wait_ge(S["f_st"].h, base["f_st"])
        sy.wait_ge(S["f_mm"].h, base["f_mm"])
        for qb in range(4):
            for dc in range(16):
                ai = qb * 16 + dc
                if ai >= 2:
                    sy.wait_ge(S["g_ap"].h, base["g_ap"] + 2 * (ai - 1))
                S["g_zl"].inc(sy.dma_start(
                    out=vzl[dc % 2],
                    in_=t["z2"][dc * 128:(dc + 1) * 128, qb * 512:(qb + 1) * 512]), 16)
    block.sync(syncs)

    def ves(v):
        for qb in range(4):
            v.wait_ge(S["g_cp"].h, base["g_cp"] + 2 * (qb + 1))
            for dc in range(16):
                ai = qb * 16 + dc
                v.wait_ge(S["g_zl"].h, base["g_zl"] + 16 * (ai + 1))
                if ai >= 2:
                    v.wait_ge(S["g_st"].h, base["g_st"] + 16 * (ai - 1))
                v.tensor_tensor(vout[dc % 2], vzl[dc % 2], vRR, op=ALU.mult)
                S["g_ap"].inc(v.tensor_tensor(vout[dc % 2], vout[dc % 2], vMM,
                                              op=ALU.subtract), 2)
    block.vector(ves)

    def gps(g):
        for qb in range(4):
            for dc in range(16):
                ai = qb * 16 + dc
                g.wait_ge(S["g_ap"].h, base["g_ap"] + 2 * (ai + 1))
                S["g_st"].inc(g.dma_start(
                    out=t["outT"][dc * 128:(dc + 1) * 128, qb * 512:(qb + 1) * 512],
                    in_=vout[dc % 2]), 16)
        g.wait_ge(S["g_st"].h, base["g_st"] + 16 * 64)
    block.gpsimd(gps)


def kernel(**inputs):
    hc = _host_consts()
    in_maps = [_core_in_map(c, inputs, hc) for c in range(8)]
    nc, _ = _build()
    res = run_bass_kernel_spmd(nc, in_maps, core_ids=list(range(8)))
    out = np.zeros((B, S, D), np.float32)
    for c in range(8):
        b, half = c // 2, c % 2
        out[b, half * TOK:(half + 1) * TOK, :] = np.asarray(res.results[c]["outT"]).T
    return out


# version 06
# speedup vs baseline: 1.1655x; 1.0009x over previous
import math
import numpy as np
from contextlib import ExitStack

import concourse.bass as bass
import concourse.mybir as mybir
from concourse.bass_utils import run_bass_kernel_spmd

F32 = mybir.dt.float32
BF16 = mybir.dt.bfloat16
U8 = mybir.dt.uint8
AF = mybir.ActivationFunctionType
ALU = mybir.AluOpType

B, S, D, H, DH, FF = 4, 4096, 2048, 16, 128, 8192
_PH5_DOWN = True
TOK = 2048
TFULL = 4096
NDC = D // 128
NCTRL = 130156
STEP64 = (NCTRL - 1) / (D * FF - 1)
NREC = 256
MAGIC = 12582912.0
TWO_PI = float(2.0 * np.pi)
LN_EPS = 1e-5

_bfd = None
def _bf16(x):
    global _bfd
    if _bfd is None:
        import ml_dtypes
        _bfd = ml_dtypes.bfloat16
    return np.asarray(x, np.float64).astype(_bfd).astype(np.float64)


_HOSTC = None

def _host_consts():
    global _HOSTC
    if _HOSTC is not None:
        return _HOSTC
    d = {}
    halos = np.zeros((3, 5, 128, 1024), np.int64)
    f0col = np.zeros((3, 128, 1024), np.float32)
    betacol = np.zeros((3, 128, 1024), np.float32)
    nbrows = np.zeros((3, NREC, 2, 512), np.float32)
    for w, (O, I) in enumerate([(FF, D), (FF, D), (D, FF)]):
        nic = I // 128
        rec = np.arange(NREC)
        ic = rec % nic
        ob = rec // nic
        j = np.arange(512)
        n0 = (ob[:, None] * 512 + j[None, :]).astype(np.int64) * I + (ic[:, None] * 128)
        t0 = n0.astype(np.float64) * STEP64
        sA = np.minimum(np.floor(t0).astype(np.int64), NCTRL - 2)
        f0 = t0 - sA
        beta = (1.0 - f0) / STEP64
        pp = (rec // 2)[:, None].repeat(512, 1)
        xx = (rec % 2)[:, None] * 512 + j[None, :]
        for a in range(5):
            hid = np.clip(sA - 1 + a, 0, NCTRL - 1)
            halos[w, a, pp, xx] = hid
        f0col[w, pp, xx] = f0.astype(np.float32)
        betacol[w, pp, xx] = beta.astype(np.float32)
        nbh = _bf16(-beta)
        nbl = _bf16(-beta - nbh)
        nbrows[w, :, 0, :] = nbh.astype(np.float32)
        nbrows[w, :, 1, :] = nbl.astype(np.float32)
    d["halo_idx"] = halos
    d["f0col"] = f0col
    d["betacol"] = betacol
    d["nbrows"] = nbrows
    pv = np.arange(128, dtype=np.float64)
    basis = np.stack([pv**m for m in range(4)], axis=0)
    bh = _bf16(basis)
    bl = _bf16(basis - bh)
    cb = np.zeros((12, 128), np.float32)
    for m in range(4):
        cb[3 * m + 0] = bh[m]
        cb[3 * m + 1] = bl[m]
        cb[3 * m + 2] = bh[m]
    d["cbasis"] = cb
    ab = np.zeros((5, 128), np.float32)
    ab[0] = 1.0
    ab[1] = 1.0
    ab[2] = bh[1]
    ab[3] = bl[1]
    ab[4] = bh[1]
    d["abasis"] = ab
    d["ones2"] = np.ones((2, 128), np.float32)
    d["iotacol"] = np.arange(128, dtype=np.float32)[:, None].copy()
    d["onescol"] = np.ones((128, 1), np.float32)
    d["ones1f"] = np.ones((1, 128), np.float32)
    d["ones128"] = np.ones((128, 128), np.float32)
    d["epscol"] = np.full((128, 1), 1e-5, np.float32)
    _HOSTC = d
    return d


def _core_in_map(c, inp, hc):
    b, half = c // 2, c % 2
    x = np.asarray(inp["L0_attn_x"])
    xb = x[b]
    own = xb[half * TOK:(half + 1) * TOK]
    oth = xb[(1 - half) * TOK:(2 - half) * TOK]
    xT = np.ascontiguousarray(np.concatenate([own, oth], axis=0).T)
    m = {"xT": xT}
    for nm, key in [("wqT", "L0_attn_Wq"), ("wkT", "L0_attn_Wk"),
                    ("wvT", "L0_attn_Wv"), ("woT", "L0_attn_Wo")]:
        m[nm] = np.ascontiguousarray(np.asarray(inp[key]).T)
    for nm, key in [("bq2d", "L0_attn_bq"), ("bk2d", "L0_attn_bk"),
                    ("bo2d", "L0_attn_bo"), ("hel2d", "L0_attn_helix_rotation")]:
        m[nm] = np.ascontiguousarray(np.asarray(inp[key]).reshape(H, DH).T)
    m["bv_row"] = np.ascontiguousarray(np.asarray(inp["L0_attn_bv"]).reshape(1, D))
    m["spiral_row"] = np.ascontiguousarray(np.asarray(inp["L0_attn_spiral_weight"]).reshape(1, H))
    cps = [np.asarray(inp["L0_ffn_gate_cp"]), np.asarray(inp["L0_ffn_up_cp"]),
           np.asarray(inp["L0_ffn_down_cp"])]
    halo = np.zeros((3, 5, 128, 1024), np.float32)
    for w in range(3):
        halo[w] = np.take(cps[w], hc["halo_idx"][w])
    m["halo"] = halo
    import ml_dtypes
    for nm in ["f0col", "betacol", "cbasis", "abasis", "ones2",
               "iotacol", "onescol", "ones1f", "ones128", "epscol"]:
        m[nm] = hc[nm]
    m["nbrows"] = hc["nbrows"].astype(ml_dtypes.bfloat16)
    return m


class _Ctx:
    pass


class _S:
    def __init__(self, h):
        self.h = h
        self.n = 0

    def inc(self, instr, k=1):
        instr.then_inc(self.h, k)
        self.n += k
        return self.n


def _build(dump=(), upto=9):
    nc = bass.Bass("TRN2", target_bir_lowering=False, debug=False)
    t = {}

    def dram(name, shape, dtype, kind="Internal"):
        if name in dump:
            kind = "ExternalOutput"
        t[name] = nc.dram_tensor(name, list(shape), dtype, kind=kind)

    def din(name, shape, dtype=F32):
        t[name] = nc.dram_tensor(name, list(shape), dtype, kind="ExternalInput")

    din("xT", [D, TFULL])
    for nm in ["wqT", "wkT", "wvT", "woT"]:
        din(nm, [D, D])
    for nm in ["bq2d", "bk2d", "bo2d", "hel2d"]:
        din(nm, [128, H])
    din("bv_row", [1, D])
    din("spiral_row", [1, H])
    din("halo", [3, 5, 128, 1024])
    din("f0col", [3, 128, 1024])
    din("betacol", [3, 128, 1024])
    din("nbrows", [3, NREC, 2, 512], BF16)
    din("cbasis", [12, 128])
    din("abasis", [5, 128])
    din("ones2", [2, 128])
    din("iotacol", [128, 1])
    din("onescol", [128, 1])
    din("ones1f", [1, 128])
    din("ones128", [128, 128])
    din("epscol", [128, 1])

    dram("recsC", [3, NREC, 12, 512], BF16)
    dram("recsA", [3, NREC, 5, 512], BF16)
    dram("wgg", [64, 16, 128, 128], BF16)
    dram("wgu", [64, 16, 128, 128], BF16)
    dram("wgd", [16, 64, 128, 128], BF16)
    dram("KT", [D, TFULL], BF16)
    dram("Vt", [H, TFULL // 128, 128, 128], BF16)
    dram("QT", [D, TOK], BF16)
    dram("z1", [D, TOK], F32)
    dram("h1f", [D, TOK], F32)
    dram("z2", [D, TOK], F32)
    dram("dbgt", [128, 160], F32)
    dram("outT", [D, TOK], F32, kind="ExternalOutput")

    ctx = _Ctx()
    ctx.nc = nc
    ctx.t = t

    with ExitStack() as es:
        E = es.enter_context
        sbd = {}

        def sbuf(name, shape, dtype):
            sbd[name] = E(nc.sbuf_tensor(name, list(shape), dtype))

        sbuf("s_cb", [12, 128], BF16)
        sbuf("s_ab", [5, 128], BF16)
        sbuf("s_ones2", [2, 128], BF16)
        sbuf("s_iota", [128, 1], F32)
        sbuf("s_onescol", [128, 1], F32)
        sbuf("s_ones1f", [1, 128], F32)
        sbuf("s_onesall", [128, 128], F32)
        sbuf("s_eps", [128, 1], F32)
        sbuf("s_ones1b", [1, 128], BF16)
        sbuf("s_hel", [128, H], F32)
        sbuf("s_bq", [128, H], F32)
        sbuf("s_bk", [128, H], F32)
        sbuf("s_bo", [128, H], F32)
        sbuf("s_sp", [1, H], F32)
        for nm in ["tb_r", "tb_sin", "tb_s2", "tb_sq2", "tb_cos", "tb_qb",
                   "tb_ks", "tb_kb", "tb_t1", "tb_sb"]:
            sbuf(nm, [128, H], F32)
        sbuf("tb_sr", [1, H], F32)
        sbuf("rowsA", [128, TOK], F32)
        sbuf("rowsB", [128, TOK], F32)

        sbuf("arA", [128, 65536], U8)
        sbuf("arB", [128, 65536], U8)
        sbuf("arC", [128, 49152], U8)

        def carve(ar, off, dtype, n, rearr=None, **kw):
            esz = 2 if dtype == BF16 else 4
            ap = sbd[ar][:, off:off + n * esz].bitcast(dtype)
            if rearr:
                ap = ap.rearrange(rearr, **kw)
            return ap

        ctx.carve = carve
        ctx.sb = sbd

        pb = [E(nc.psum_tensor(f"pb{i}", [128, 512], F32)) for i in range(6)]
        psS1 = E(nc.psum_tensor("psS1", [1, 512], F32))
        psS2 = E(nc.psum_tensor("psS2", [1, 512], F32))
        ctx.pb, ctx.psS1, ctx.psS2 = pb, psS1, psS2

        sems = {}
        for nm in ["cl", "cg", "tabv", "tabs", "tmm",
                   "p1l", "p1v", "p1s",
                   "wls", "wln", "wmm", "wact", "wve", "wst",
                   "q_xl", "q_wl", "q_mm", "q_ev", "q_st",
                   "a_kv", "a_q", "a_s", "a_x", "a_v", "a_d", "a_f",
                   "o_wl", "o_mm", "o_ev", "o_xl", "o_z", "o_sq", "o_st",
                   "o_sc", "o_s2",
                   "l_zl", "l_ap", "l_hb",
                   "f_wl", "f_mm", "f_sil", "f_dwl", "f_dmm", "f_ev",
                   "f_hl", "f_z2", "f_sq", "f_st", "f_sc", "f_s2", "f_mu",
                   "g_zl", "g_ap", "g_st", "g_bc", "g_cp", "vsf", "asf"]:
            sems[nm] = _S(E(nc.semaphore(nm)))
        ctx.S = sems

        with nc.Block() as block:
            phases = [_ph0, _ph1_prep, _ph1_main, _ph2_qkv, _ph3_attn,
                      _ph4_wo_ln1, _ph5_ffn, _ph6_out]
            for pi, ph in enumerate(phases):
                if pi <= upto:
                    ph(ctx, block)
            if upto < len(phases) - 1:
                def _tail(g):
                    for nm in ["p1s", "wst", "q_st", "o_st", "f_st", "g_st"]:
                        if sems[nm].n:
                            g.wait_ge(sems[nm].h, sems[nm].n)
                block.gpsimd(_tail)

    return nc, ctx


def _vs(ctx, v, instr):
    n = ctx.S["vsf"].inc(instr)
    v.wait_ge(ctx.S["vsf"].h, n)


def _as(ctx, s, instr):
    n = ctx.S["asf"].inc(instr)
    s.wait_ge(ctx.S["asf"].h, n)


def _ph0(ctx, block):
    t, sb, S, pb = ctx.t, ctx.sb, ctx.S, ctx.pb
    s_bv = ctx.carve("arA", 49152, BF16, D)[0:1, :]
    ctx.s_bv = s_bv

    def g_loads(g):
        for nm, dst in [("cbasis", "s_cb"), ("abasis", "s_ab"), ("ones2", "s_ones2")]:
            S["cg"].inc(g.dma_start(out=sb[dst][:], in_=t[nm][:, :]), 16)
        S["cg"].inc(g.dma_start(out=sb["s_ones1b"][:], in_=t["ones1f"][:, :]), 16)
        S["cg"].inc(g.dma_start(out=s_bv, in_=t["bv_row"][:, :]), 16)
    block.gpsimd(g_loads)

    def s_loads(sy):
        for nm, dst in [("iotacol", "s_iota"), ("onescol", "s_onescol"),
                        ("ones1f", "s_ones1f"), ("ones128", "s_onesall"), ("epscol", "s_eps"), ("hel2d", "s_hel"), ("bq2d", "s_bq"),
                        ("bk2d", "s_bk"), ("bo2d", "s_bo"), ("spiral_row", "s_sp")]:
            S["cl"].inc(sy.dma_start(out=sb[dst][:], in_=t[nm][:, :]), 16)
    block.sync(s_loads)

    def v1(v):
        v.wait_ge(S["cl"].h, S["cl"].n)
        BIG = 1e30
        _vs(ctx, v, v.tensor_scalar(sb["tb_t1"][:], sb["s_hel"][:], float(np.pi), BIG,
                                    op0=ALU.subtract, op1=ALU.mult))
        _vs(ctx, v, v.tensor_scalar(sb["tb_t1"][:], sb["tb_t1"][:], 0.0, 1.0,
                                    op0=ALU.max, op1=ALU.min))
        _vs(ctx, v, v.tensor_scalar(sb["tb_s2"][:], sb["s_hel"][:], float(-np.pi), -BIG,
                                    op0=ALU.subtract, op1=ALU.mult))
        _vs(ctx, v, v.tensor_scalar(sb["tb_s2"][:], sb["tb_s2"][:], 0.0, 1.0,
                                    op0=ALU.max, op1=ALU.min))
        _vs(ctx, v, v.tensor_tensor(sb["tb_t1"][:], sb["tb_t1"][:], sb["tb_s2"][:], op=ALU.subtract))
        _vs(ctx, v, v.scalar_tensor_tensor(sb["tb_r"][:], sb["tb_t1"][:], -TWO_PI, sb["s_hel"][:],
                                           op0=ALU.mult, op1=ALU.add))
        S["tabv"].inc(v.tensor_scalar(sb["tb_r"][:], sb["tb_r"][:], float(np.pi),
                                      float(-np.pi), op0=ALU.min, op1=ALU.max))
    block.vector(v1)

    def a1(s):
        s.wait_ge(S["tabv"].h, S["tabv"].n)
        _as(ctx, s, s.activation(sb["tb_sin"][:], sb["tb_r"][:], AF.Sin, bias=0.0, scale=1.0))
        _as(ctx, s, s.activation(sb["tb_s2"][:], sb["tb_r"][:], AF.Sin, bias=0.0, scale=0.5))
        S["tabs"].inc(s.activation(sb["tb_sq2"][:], sb["tb_s2"][:], AF.Square,
                                   bias=0.0, scale=1.0))
    block.scalar(a1)

    def v2(v):
        v.wait_ge(S["tabs"].h, S["tabs"].n)
        _vs(ctx, v, v.tensor_scalar(sb["tb_cos"][:], sb["tb_sq2"][:], -2.0, 1.0, op0=ALU.mult, op1=ALU.add))
        _vs(ctx, v, v.tensor_tensor(sb["tb_qb"][:], sb["tb_cos"][:], sb["s_bq"][:], op=ALU.mult))
        S["tabv"].inc(v.tensor_scalar(sb["tb_sr"][:], sb["s_sp"][:],
                                      float(1.0 / math.sqrt(DH)), 0.0,
                                      op0=ALU.mult, op1=ALU.add))
    block.vector(v2)

    def t1(te):
        te.wait_ge(S["tabv"].h, S["tabv"].n)
        S["tmm"].inc(te.matmul(pb[0][:, 0:H], sb["s_ones1f"][:], sb["tb_sr"][:],
                               start=True, stop=True))
    block.tensor(t1)

    def a2(s):
        s.wait_ge(S["tmm"].h, S["tmm"].n)
        S["tabs"].inc(s.copy(sb["tb_sb"][:], pb[0][:, 0:H]))
    block.scalar(a2)

    def v3(v):
        v.wait_ge(S["tabs"].h, S["tabs"].n)
        _vs(ctx, v, v.tensor_tensor(sb["tb_ks"][:], sb["tb_sin"][:], sb["tb_sb"][:], op=ALU.mult))
        S["tabv"].inc(v.tensor_tensor(sb["tb_kb"][:], sb["tb_ks"][:], sb["s_bk"][:],
                                      op=ALU.mult))
    block.vector(v3)

    def gdbg(g):
        g.wait_ge(S["tabv"].h, S["tabv"].n)
        for i, nm in enumerate(["tb_cos", "tb_qb", "tb_ks", "tb_kb", "s_hel", "tb_r", "tb_sin", "tb_s2", "tb_sq2", "tb_sb"]):
            S["cg"].inc(g.dma_start(out=t["dbgt"][:, i * H:(i + 1) * H], in_=sb[nm][:]), 16)
    block.gpsimd(gdbg)


def _ph1_prep(ctx, block):
    t, sb, S, carve = ctx.t, ctx.sb, ctx.S, ctx.carve
    d = float(STEP64)
    d2, d3 = d * d, d * d * d

    IN = [carve("arB", i * 4096, F32, 1024) for i in range(7)]
    TMP = [carve("arB", (7 + i) * 4096, F32, 1024) for i in range(8)]
    OUT = [carve("arC", i * 2048, BF16, 1024) for i in range(12)]

    for w in range(3):
        base_l = S["p1l"].n

        def loads(sy, w=w):
            if w > 0:
                sy.wait_ge(S["p1v"].h, 6 * w)
            for a in range(5):
                S["p1l"].inc(sy.dma_start(out=IN[a], in_=t["halo"][w, a]), 16)
            S["p1l"].inc(sy.dma_start(out=IN[5], in_=t["f0col"][w]), 16)
            S["p1l"].inc(sy.dma_start(out=IN[6], in_=t["betacol"][w]), 16)
        block.sync(loads)

        stores = []

        def vops(v, w=w, base_l=base_l, stores=stores):
            v.wait_ge(S["p1l"].h, base_l + 7 * 16)
            if w > 0:
                v.wait_ge(S["p1s"].h, 17 * 16 * w)
            h0, h1, h2, h3, h4, f0, bet = IN
            Bc, C, Dc, Pp, Q, t1, t2, _x = TMP
            v.tensor_tensor(t1, h2, h0, op=ALU.subtract)
            v.tensor_scalar(Bc, t1, 0.5, 0.0, op0=ALU.mult, op1=ALU.add)
            v.scalar_tensor_tensor(t1, h1, -2.5, h0, op0=ALU.mult, op1=ALU.add)
            v.scalar_tensor_tensor(t1, h3, -0.5, t1, op0=ALU.mult, op1=ALU.add)
            v.scalar_tensor_tensor(C, h2, 2.0, t1, op0=ALU.mult, op1=ALU.add)
            v.scalar_tensor_tensor(t1, h1, -3.0, h0, op0=ALU.mult, op1=ALU.add)
            v.scalar_tensor_tensor(t1, h2, 3.0, t1, op0=ALU.mult, op1=ALU.add)
            v.scalar_tensor_tensor(t1, h3, -1.0, t1, op0=ALU.mult, op1=ALU.add)
            v.tensor_scalar(Dc, t1, -0.5, 0.0, op0=ALU.mult, op1=ALU.add)
            v.scalar_tensor_tensor(t1, h2, -2.5, h1, op0=ALU.mult, op1=ALU.add)
            v.scalar_tensor_tensor(t1, h4, -0.5, t1, op0=ALU.mult, op1=ALU.add)
            v.scalar_tensor_tensor(t1, h3, 2.0, t1, op0=ALU.mult, op1=ALU.add)
            v.tensor_tensor(t1, t1, C, op=ALU.subtract)
            v.scalar_tensor_tensor(Pp, Dc, -3.0, t1, op0=ALU.mult, op1=ALU.add)
            v.scalar_tensor_tensor(t1, h2, -3.0, h1, op0=ALU.mult, op1=ALU.add)
            v.scalar_tensor_tensor(t1, h3, 3.0, t1, op0=ALU.mult, op1=ALU.add)
            v.scalar_tensor_tensor(t1, h4, -1.0, t1, op0=ALU.mult, op1=ALU.add)
            v.tensor_scalar(t1, t1, -0.5, 0.0, op0=ALU.mult, op1=ALU.add)
            v.tensor_tensor(Q, t1, Dc, op=ALU.subtract)

            slot = [0]

            def put(g_ap, dests):
                s_hi = OUT[slot[0] % 12]; slot[0] += 1
                s_lo = OUT[slot[0] % 12]; slot[0] += 1
                v.tensor_copy(s_hi, g_ap)
                last = v.tensor_tensor(s_lo, g_ap, s_hi, op=ALU.subtract)
                for dd, which in dests:
                    stores.append((dd, s_hi if which == "hi" else s_lo))
                return last

            v.tensor_tensor(t2, f0, Dc, op=ALU.mult)
            v.tensor_tensor(t2, t2, C, op=ALU.add)
            v.tensor_tensor(t2, t2, f0, op=ALU.mult)
            v.tensor_tensor(t2, t2, Bc, op=ALU.add)
            v.tensor_tensor(t2, t2, f0, op=ALU.mult)
            v.tensor_tensor(t2, t2, h1, op=ALU.add)
            S["p1v"].inc(put(t2, [(("recsC", 0), "hi"), (("recsC", 1), "hi"),
                                  (("recsC", 2), "lo")]))
            v.tensor_scalar(t1, Dc, 3.0, 0.0, op0=ALU.mult, op1=ALU.add)
            v.tensor_tensor(t2, t1, f0, op=ALU.mult)
            v.scalar_tensor_tensor(t2, C, 2.0, t2, op0=ALU.mult, op1=ALU.add)
            v.tensor_tensor(t2, t2, f0, op=ALU.mult)
            v.tensor_tensor(t2, t2, Bc, op=ALU.add)
            v.tensor_scalar(t2, t2, d, 0.0, op0=ALU.mult, op1=ALU.add)
            S["p1v"].inc(put(t2, [(("recsC", 3), "hi"), (("recsC", 4), "hi"),
                                  (("recsC", 5), "lo")]))
            v.tensor_tensor(t2, t1, f0, op=ALU.mult)
            v.tensor_tensor(t2, t2, C, op=ALU.add)
            v.tensor_scalar(t2, t2, d2, 0.0, op0=ALU.mult, op1=ALU.add)
            S["p1v"].inc(put(t2, [(("recsC", 6), "hi"), (("recsC", 7), "hi"),
                                  (("recsC", 8), "lo")]))
            v.tensor_scalar(t2, Dc, d3, 0.0, op0=ALU.mult, op1=ALU.add)
            S["p1v"].inc(put(t2, [(("recsC", 9), "hi"), (("recsC", 10), "hi"),
                                  (("recsC", 11), "lo")]))
            v.tensor_tensor(t1, Q, bet, op=ALU.mult)
            v.tensor_scalar(t2, Pp, d2, 0.0, op0=ALU.mult, op1=ALU.add)
            v.scalar_tensor_tensor(t2, t1, -d3, t2, op0=ALU.mult, op1=ALU.add)
            S["p1v"].inc(put(t2, [(("recsA", 0), "hi"), (("recsA", 1), "lo")]))
            v.tensor_scalar(t2, Q, d3, 0.0, op0=ALU.mult, op1=ALU.add)
            S["p1v"].inc(put(t2, [(("recsA", 2), "hi"), (("recsA", 3), "hi"),
                                  (("recsA", 4), "lo")]))
        block.vector(vops)

        pv_base = S["p1v"].n - 6

        def gstores(g, w=w, stores=stores, pv_base=pv_base):
            per_group = [3, 3, 3, 3, 2, 3]
            done = 0
            for gi, cnt in enumerate(per_group):
                g.wait_ge(S["p1v"].h, pv_base + gi + 1)
                for _ in range(cnt):
                    (recname, row), src = stores[done]
                    done += 1
                    dst = ctx.t[recname][w][:, row, :].rearrange("(p k) j -> p k j", p=128)
                    S["p1s"].inc(g.dma_start(
                        out=dst, in_=src.rearrange("p (k j) -> p k j", k=2)), 16)
        block.gpsimd(gstores)


def _ph1_main(ctx, block):
    t, sb, S, pb = ctx.t, ctx.sb, ctx.S, ctx.pb
    carve = ctx.carve

    NSL = 4
    crb = [carve("arB", i * 4096, BF16, 2048, "p (r j) -> p r j", r=4)[0:12]
           for i in range(NSL)]
    arb = [carve("arB", 16384 + i * 4096, BF16, 2048, "p (r j) -> p r j", r=4)[0:5]
           for i in range(NSL)]
    nbb = [carve("arB", 32768 + i * 4096, BF16, 2048, "p (r j) -> p r j", r=4)[0:2]
           for i in range(NSL)]
    sR = [carve("arC", 24576 + i * 2048, F32, 512) for i in range(2)]
    sS = [carve("arC", 28672 + i * 2048, F32, 512) for i in range(2)]
    sSA = [carve("arC", 32768 + i * 2048, F32, 512) for i in range(2)]
    stg = [carve("arC", 36864 + i * 1024, BF16, 512) for i in range(3)]

    NB = 64
    RPB = 4
    base = dict(wls=S["wls"].n, wmm=S["wmm"].n, wact=S["wact"].n,
                wve=S["wve"].n, wst=S["wst"].n, p1s=S["p1s"].n, tabs=S["tabs"].n,
                cg=S["cg"].n, p1v=S["p1v"].n)

    def syncs(sy):
        sy.wait_ge(S["p1s"].h, base["p1s"])
        for w in range(3):
            for b in range(NB):
                gb = w * NB + b
                if gb >= NSL:
                    sy.wait_ge(S["wmm"].h, base["wmm"] + (gb - NSL + 1) * RPB)
                sl = gb % NSL
                S["wls"].inc(sy.dma_start(
                    out=crb[sl], in_=t["recsC"][w, b * RPB:(b + 1) * RPB].rearrange("r a j -> a r j")), 16)
                S["wls"].inc(sy.dma_start(
                    out=arb[sl], in_=t["recsA"][w, b * RPB:(b + 1) * RPB].rearrange("r a j -> a r j")), 16)
                S["wls"].inc(sy.dma_start(
                    out=nbb[sl], in_=t["nbrows"][w, b * RPB:(b + 1) * RPB].rearrange("r a j -> a r j")), 16)
    block.sync(syncs)

    def tes(te):
        te.wait_ge(S["tabs"].h, base["tabs"])
        te.wait_ge(S["cg"].h, base["cg"])
        te.wait_ge(S["p1v"].h, base["p1v"])
        for w in range(3):
            for b in range(NB):
                gb = w * NB + b
                sl = gb % NSL
                te.wait_ge(S["wls"].h, base["wls"] + 48 * (gb + 1))
                for i in range(RPB):
                    r = gb * RPB + i
                    if r >= 2:
                        te.wait_ge(S["wve"].h, base["wve"] + 2 * (r - 1))
                        te.wait_ge(S["wact"].h, base["wact"] + 2 * (r - 1))
                    te.matmul(pb[r % 2][:, :], sb["s_cb"][:], crb[sl][:, i, :],
                              start=True, stop=True)
                    te.matmul(pb[2 + r % 2][:, :], sb["s_ab"][:], arb[sl][:, i, :],
                              start=True, stop=True)
                    S["wmm"].inc(te.matmul(pb[4 + r % 2][:, :], sb["s_ones2"][:],
                                           nbb[sl][:, i, :], start=True, stop=True))
    block.tensor(tes)

    def acts(s):
        for r in range(3 * NB * RPB):
            s.wait_ge(S["wmm"].h, base["wmm"] + r + 1)
            if r >= 2:
                s.wait_ge(S["wve"].h, base["wve"] + 2 * (r - 2) + 1)
            s.activation(sR[r % 2], pb[4 + r % 2][:, :], AF.Relu,
                         bias=sb["s_iota"][:, 0:1], scale=1.0)
            S["wact"].inc(s.activation(sS[r % 2], sR[r % 2], AF.Square,
                                       bias=0.0, scale=1.0), 2)
    block.scalar(acts)

    def ves(v):
        for r in range(3 * NB * RPB):
            v.wait_ge(S["wact"].h, base["wact"] + 2 * (r + 1))
            v.wait_ge(S["wmm"].h, base["wmm"] + r + 1)
            if r >= 3:
                v.wait_ge(S["wst"].h, base["wst"] + 16 * (r - 2))
            v.tensor_tensor(sSA[r % 2], sS[r % 2], pb[2 + r % 2][:, :], op=ALU.mult)
            S["wve"].inc(v.tensor_tensor(stg[r % 3], sSA[r % 2], pb[r % 2][:, :],
                                         op=ALU.add), 2)
    block.vector(ves)

    def gps(g):
        for w in range(3):
            nic = [16, 16, 64][w]
            wt = [t["wgg"], t["wgu"], t["wgd"]][w]
            for b in range(NB):
                for i in range(RPB):
                    r = (w * NB + b) * RPB + i
                    rec = b * RPB + i
                    ic, ob = rec % nic, rec // nic
                    g.wait_ge(S["wve"].h, base["wve"] + 2 * (r + 1))
                    dst = wt[4 * ob:4 * ob + 4, ic].rearrange("k p j -> p k j")
                    S["wst"].inc(g.dma_start(
                        out=dst, in_=stg[r % 3].rearrange("p (k j) -> p k j", k=4)), 16)
    block.gpsimd(gps)


def _ph2_qkv(ctx, block):
    t, sb, S, pb = ctx.t, ctx.sb, ctx.S, ctx.pb
    carve = ctx.carve
    s_bv = ctx.s_bv

    vxT = carve("arB", 0, BF16, 16 * 2048, "p (c j) -> p c j", c=16)
    vws = [carve("arC", i * 4096, BF16, 16 * 128, "p (c j) -> p c j", c=16) for i in range(2)]
    vvw = [carve("arC", 8192 + i * 16384, BF16, 16 * 512, "p (c j) -> p c j", c=16) for i in range(2)]
    stg = [carve("arC", 40960 + i * 1024, BF16, 512) for i in range(3)]

    groups = []
    slabs = []
    for th in range(2):
        for os in range(16):
            slabs.append(("kq", "wkT", th, os))
            for qb in range(4):
                groups.append(dict(kind="K", th=th, os=os, qb=qb, slab=len(slabs) - 1))
        for ob in range(4):
            slabs.append(("v", "wvT", th, ob))
            for tt in range(16):
                groups.append(dict(kind="V", th=th, ob=ob, tt=tt, slab=len(slabs) - 1))
        if th == 0:
            for os in range(16):
                slabs.append(("kq", "wqT", th, os))
                for qb in range(4):
                    groups.append(dict(kind="Q", th=th, os=os, qb=qb, slab=len(slabs) - 1))
    slab_last = {}
    for gi, g in enumerate(groups):
        slab_last[g["slab"]] = gi
    kq_ct = v_ct = 0
    slab_slot = {}
    for si, (knd, _, _, _) in enumerate(slabs):
        if knd == "kq":
            slab_slot[si] = kq_ct % 2
            kq_ct += 1
        else:
            slab_slot[si] = v_ct % 2
            v_ct += 1
    prev_same = {}
    last_of = {}
    for si, (knd, _, _, _) in enumerate(slabs):
        key = (knd, slab_slot[si])
        prev_same[si] = last_of.get(key)
        last_of[key] = si
    th0_last = max(gi for gi, g in enumerate(groups) if g["th"] == 0)

    base = dict(q_xl=S["q_xl"].n, q_wl=S["q_wl"].n, q_mm=S["q_mm"].n,
                q_ev=S["q_ev"].n, q_st=S["q_st"].n, wst=S["wst"].n,
                cg=S["cg"].n, wve=S["wve"].n, wact=S["wact"].n, tabv=S["tabv"].n)

    def gps(g):
        g.wait_ge(S["wve"].h, base["wve"])
        g.wait_ge(S["wst"].h, base["wst"])
        S["q_xl"].inc(g.dma_start(
            out=vxT, in_=t["xT"][:, 0:2048].rearrange("(c p) j -> p c j", p=128)), 16)
        for si, (knd, wname, th, idx) in enumerate(slabs):
            if th == 1 and slabs[si - 1][2] == 0:
                g.wait_ge(S["q_mm"].h, base["q_mm"] + th0_last + 1)
                S["q_xl"].inc(g.dma_start(
                    out=vxT, in_=t["xT"][:, 2048:4096].rearrange("(c p) j -> p c j", p=128)), 16)
            ps = prev_same[si]
            if ps is not None:
                g.wait_ge(S["q_mm"].h, base["q_mm"] + slab_last[ps] + 1)
            if knd == "kq":
                dst = vws[slab_slot[si]]
                src = t[wname][:, idx * 128:(idx + 1) * 128].rearrange("(c p) j -> p c j", p=128)
            else:
                dst = vvw[slab_slot[si]]
                src = t[wname][:, idx * 512:(idx + 1) * 512].rearrange("(c p) j -> p c j", p=128)
            S["q_wl"].inc(g.dma_start(out=dst, in_=src), 16)
    block.gpsimd(gps)

    def tes(te):
        te.wait_ge(S["cg"].h, base["cg"])
        te.wait_ge(S["tabv"].h, base["tabv"])
        te.wait_ge(S["wve"].h, base["wve"])
        te.wait_ge(S["wact"].h, base["wact"])
        for gi, gr in enumerate(groups):
            need_xl = 16 * (1 if gr["th"] == 0 else 2)
            te.wait_ge(S["q_xl"].h, base["q_xl"] + need_xl)
            te.wait_ge(S["q_wl"].h, base["q_wl"] + 16 * (gr["slab"] + 1))
            if gi >= 2:
                te.wait_ge(S["q_ev"].h, base["q_ev"] + gi - 1)
            p = pb[gi % 2]
            if gr["kind"] in ("K", "Q"):
                sl = slab_slot[gr["slab"]]
                for dc in range(NDC):
                    mm = te.matmul(p[:, :], vws[sl][:, dc, :],
                                   vxT[:, dc, gr["qb"] * 512:(gr["qb"] + 1) * 512],
                                   start=(dc == 0), stop=(dc == NDC - 1))
                S["q_mm"].inc(mm)
            else:
                sl = slab_slot[gr["slab"]]
                tt = gr["tt"]
                for dc in range(NDC):
                    te.matmul(p[:, :], vxT[:, dc, tt * 128:(tt + 1) * 128],
                              vvw[sl][:, dc, :], start=(dc == 0), stop=False)
                S["q_mm"].inc(te.matmul(
                    p[:, :], sb["s_ones1b"][:],
                    s_bv[0:1, gr["ob"] * 512:(gr["ob"] + 1) * 512],
                    start=False, stop=True))
    block.tensor(tes)

    def acts(s):
        for gi, gr in enumerate(groups):
            s.wait_ge(S["q_mm"].h, base["q_mm"] + gi + 1)
            if gi >= 3:
                s.wait_ge(S["q_st"].h, base["q_st"] + 16 * (gi - 2))
            p = pb[gi % 2]
            dst = stg[gi % 3]
            if gr["kind"] == "K":
                os = gr["os"]
                S["q_ev"].inc(s.activation(dst, p[:, :], AF.Identity,
                                           bias=sb["tb_kb"][:, os:os + 1],
                                           scale=sb["tb_ks"][:, os:os + 1]))
            elif gr["kind"] == "Q":
                os = gr["os"]
                S["q_ev"].inc(s.activation(dst, p[:, :], AF.Identity,
                                           bias=sb["tb_qb"][:, os:os + 1],
                                           scale=sb["tb_cos"][:, os:os + 1]))
            else:
                S["q_ev"].inc(s.copy(dst, p[:, :]))
    block.scalar(acts)

    def syncs(sy):
        for gi, gr in enumerate(groups):
            sy.wait_ge(S["q_ev"].h, base["q_ev"] + gi + 1)
            src = stg[gi % 3]
            if gr["kind"] == "K":
                dst = t["KT"][gr["os"] * 128:(gr["os"] + 1) * 128,
                              gr["th"] * 2048 + gr["qb"] * 512:
                              gr["th"] * 2048 + (gr["qb"] + 1) * 512]
                S["q_st"].inc(sy.dma_start(out=dst, in_=src), 16)
            elif gr["kind"] == "Q":
                dst = t["QT"][gr["os"] * 128:(gr["os"] + 1) * 128,
                              gr["qb"] * 512:(gr["qb"] + 1) * 512]
                S["q_st"].inc(sy.dma_start(out=dst, in_=src), 16)
            else:
                tc = gr["th"] * 16 + gr["tt"]
                dst = t["Vt"][gr["ob"] * 4:(gr["ob"] + 1) * 4, tc].rearrange("h p j -> p h j")
                S["q_st"].inc(sy.dma_start(out=dst, in_=src.rearrange("p (h j) -> p h j", h=4)), 16)
    block.sync(syncs)


def _ph3_attn(ctx, block):
    t, sb, S, pb = ctx.t, ctx.sb, ctx.S, ctx.pb
    psS1 = ctx.psS1
    carve = ctx.carve

    vK = [carve("arB", i * 8192, BF16, 4096) for i in range(2)]
    vV = [carve("arB", 16384 + i * 8192, BF16, 32 * 128, "p (c j) -> p c j", c=32) for i in range(2)]
    vQ = [carve("arB", 32768 + i * 1024, BF16, 512) for i in range(2)]
    vprob = [carve("arB", 34816 + i * 1024, BF16, 512) for i in range(3)]
    dacc = [carve("arB", 38912 + i * 2048, F32, 512) for i in range(2)]
    vDrb = carve("arB", 43008, F32, 512)
    r_dr = ctx.sb["rowsB"][0:1, 0:512]
    normT = carve("arA", 0, BF16, 16 * 2048, "p (h j) -> p h j", h=16)

    NC = 32
    base = dict(a_kv=S["a_kv"].n, a_q=S["a_q"].n, a_s=S["a_s"].n, a_x=S["a_x"].n,
                a_v=S["a_v"].n, a_d=S["a_d"].n, a_f=S["a_f"].n,
                q_mm=S["q_mm"].n, q_st=S["q_st"].n, q_ev=S["q_ev"].n)

    def syncs(sy):
        sy.wait_ge(S["q_st"].h, base["q_st"])
        for h in range(H):
            if h >= 2:
                sy.wait_ge(S["a_v"].h, base["a_v"] + 128 * (h - 1))
            S["a_kv"].inc(sy.dma_start(out=vK[h % 2], in_=t["KT"][h * 128:(h + 1) * 128, :]), 16)
            S["a_kv"].inc(sy.dma_start(out=vV[h % 2], in_=t["Vt"][h].rearrange("c p j -> p c j")), 16)
            for qb in range(4):
                gq = h * 4 + qb
                if gq >= 2:
                    sy.wait_ge(S["a_s"].h, base["a_s"] + 32 * (gq - 1))
                S["a_q"].inc(sy.dma_start(
                    out=vQ[gq % 2],
                    in_=t["QT"][h * 128:(h + 1) * 128, qb * 512:(qb + 1) * 512]), 16)
    block.sync(syncs)

    def tes(te):
        te.wait_ge(S["q_mm"].h, base["q_mm"])
        te.wait_ge(S["q_ev"].h, base["q_ev"])
        for h in range(H):
            te.wait_ge(S["a_kv"].h, base["a_kv"] + 32 * (h + 1))
            for qb in range(4):
                gq = h * 4 + qb
                te.wait_ge(S["a_q"].h, base["a_q"] + 16 * (gq + 1))
                for c in range(NC):
                    wv = 33 * gq + c - 1
                    if wv >= 1:
                        te.wait_ge(S["a_x"].h, base["a_x"] + wv)
                    S["a_s"].inc(te.matmul(pb[c % 2][:, :],
                                           vK[h % 2][:, c * 128:(c + 1) * 128],
                                           vQ[gq % 2], start=True, stop=True))
                    cc = c - 1
                    if cc >= 0:
                        te.wait_ge(S["a_x"].h, base["a_x"] + 33 * gq + cc + 1)
                        if cc == 0 and gq >= 1:
                            te.wait_ge(S["a_d"].h, base["a_d"] + 34 * gq)
                        S["a_v"].inc(te.matmul(pb[2][:, :], vV[h % 2][:, cc, :],
                                               vprob[cc % 3], start=(cc == 0), stop=False))
                cc = NC - 1
                te.wait_ge(S["a_x"].h, base["a_x"] + 33 * gq + cc + 1)
                S["a_v"].inc(te.matmul(pb[2][:, :], vV[h % 2][:, cc, :],
                                       vprob[cc % 3], start=False, stop=True))
                te.wait_ge(S["a_d"].h, base["a_d"] + 34 * gq + 32)
                S["a_f"].inc(te.matmul(psS1[:, :], sb["s_onescol"][:],
                                       dacc[(NC - 1) % 2], start=True, stop=True))
                te.wait_ge(S["a_d"].h, base["a_d"] + 34 * gq + 33)
                S["a_f"].inc(te.matmul(pb[4][:, :], sb["s_ones1f"][:], r_dr,
                                       start=True, stop=True))
    block.tensor(tes)

    def acts(s):
        for h in range(H):
            for qb in range(4):
                gq = h * 4 + qb
                for c in range(NC):
                    s.wait_ge(S["a_s"].h, base["a_s"] + 32 * gq + c + 1)
                    wv = 32 * gq + c - 2
                    if wv >= 1:
                        s.wait_ge(S["a_v"].h, base["a_v"] + wv)
                    wd = 34 * gq + c - 2
                    if wd >= 1:
                        s.wait_ge(S["a_d"].h, base["a_d"] + wd)
                    S["a_x"].inc(s.activation(vprob[c % 3], pb[c % 2][:, :], AF.Exp,
                                              bias=0.0, scale=1.0))
                s.wait_ge(S["a_f"].h, base["a_f"] + 2 * gq + 2)
                S["a_x"].inc(s.copy(vDrb, pb[4][:, :]))
    block.scalar(acts)

    def ves(v):
        for h in range(H):
            for qb in range(4):
                gq = h * 4 + qb
                for c in range(NC):
                    v.wait_ge(S["a_x"].h, base["a_x"] + 33 * gq + c + 1)
                    if c == 0:
                        S["a_d"].inc(v.tensor_copy(dacc[0], vprob[0]))
                    else:
                        S["a_d"].inc(v.tensor_tensor(dacc[c % 2], dacc[(c - 1) % 2],
                                                     vprob[c % 3], op=ALU.add))
                v.wait_ge(S["a_f"].h, base["a_f"] + 2 * gq + 1)
                S["a_d"].inc(v.reciprocal(r_dr, psS1[0:1, :]))
                v.wait_ge(S["a_x"].h, base["a_x"] + 33 * (gq + 1))
                v.wait_ge(S["a_v"].h, base["a_v"] + 32 * (gq + 1))
                S["a_d"].inc(v.tensor_tensor(normT[:, h, qb * 512:(qb + 1) * 512],
                                             pb[2][:, :], vDrb, op=ALU.mult))
    block.vector(ves)


def _ph4_wo_ln1(ctx, block):
    t, sb, S, pb = ctx.t, ctx.sb, ctx.S, ctx.pb
    psS1, psS2 = ctx.psS1, ctx.psS2
    carve = ctx.carve

    normT = carve("arA", 0, BF16, 16 * 2048, "p (h j) -> p h j", h=16)
    h1T = carve("arA", 0, BF16, 16 * 2048, "p (h j) -> p h j", h=16)
    vWo = carve("arB", 0, BF16, 16 * 2048, "p (c j) -> p c j", c=16)
    vz = [carve("arC", i * 2048, F32, 512) for i in range(2)]
    vxf = [carve("arC", 4096 + i * 2048, F32, 512) for i in range(2)]
    vz1 = [carve("arC", 8192 + i * 2048, F32, 512) for i in range(2)]
    vsq = [carve("arC", 12288 + i * 2048, F32, 512) for i in range(2)]
    vzl = [carve("arC", 16384 + i * 2048, F32, 512) for i in range(2)]
    vh1 = [carve("arC", 20480 + i * 2048, F32, 512) for i in range(2)]
    vRR = carve("arC", 24576, F32, 512)
    vMM = carve("arC", 26624, F32, 512)
    vtp = [carve("arC", 28672 + i * 2048, F32, 512) for i in range(2)]
    r_sz = sb["rowsA"][0:1, :]
    r_sq = sb["rowsA"][32:33, :]
    r_r = sb["rowsB"][0:1, :]
    r_tmp = sb["rowsB"][32:33, :]

    base = dict(o_wl=S["o_wl"].n, o_mm=S["o_mm"].n, o_ev=S["o_ev"].n, o_xl=S["o_xl"].n,
                o_z=S["o_z"].n, o_sq=S["o_sq"].n, o_st=S["o_st"].n, o_sc=S["o_sc"].n,
                o_s2=S["o_s2"].n, a_d=S["a_d"].n, a_x=S["a_x"].n, a_v=S["a_v"].n,
                l_zl=S["l_zl"].n, l_ap=S["l_ap"].n, l_hb=S["l_hb"].n,
                g_bc=S["g_bc"].n, g_cp=S["g_cp"].n)

    def gps(g):
        g.wait_ge(S["a_d"].h, base["a_d"])
        S["o_wl"].inc(g.dma_start(
            out=vWo, in_=t["woT"].ap().rearrange("(c p) o -> p c o", p=128)), 16)
        for qb in range(4):
            for ot in range(16):
                gi = qb * 16 + ot
                g.wait_ge(S["o_z"].h, base["o_z"] + gi + 1)
                S["o_st"].inc(g.dma_start(
                    out=t["z1"][ot * 128:(ot + 1) * 128, qb * 512:(qb + 1) * 512],
                    in_=vz1[ot % 2]), 16)
        for qb in range(4):
            for dc in range(16):
                ai = qb * 16 + dc
                g.wait_ge(S["l_ap"].h, base["l_ap"] + 2 * ai + 1)
                S["o_st"].inc(g.dma_start(
                    out=t["h1f"][dc * 128:(dc + 1) * 128, qb * 512:(qb + 1) * 512],
                    in_=vh1[dc % 2]), 16)
    block.gpsimd(gps)

    def syncs(sy):
        for qb in range(4):
            for ot in range(16):
                gi = qb * 16 + ot
                if gi >= 2:
                    sy.wait_ge(S["o_z"].h, base["o_z"] + gi - 1)
                S["o_xl"].inc(sy.dma_start(
                    out=vxf[ot % 2],
                    in_=t["xT"][ot * 128:(ot + 1) * 128, qb * 512:(qb + 1) * 512]), 16)
        for qb in range(4):
            for dc in range(16):
                ai = qb * 16 + dc
                if ai == 0:
                    sy.wait_ge(S["o_st"].h, base["o_st"] + 16 * 64)
                if ai >= 2:
                    sy.wait_ge(S["l_ap"].h, base["l_ap"] + 2 * (ai - 1))
                S["l_zl"].inc(sy.dma_start(
                    out=vzl[dc % 2],
                    in_=t["z1"][dc * 128:(dc + 1) * 128, qb * 512:(qb + 1) * 512]), 16)
    block.sync(syncs)

    def tes(te):
        te.wait_ge(S["o_wl"].h, base["o_wl"] + 16)
        te.wait_ge(S["a_d"].h, base["a_d"])
        te.wait_ge(S["a_x"].h, base["a_x"])
        for qb in range(4):
            for ot in range(16):
                gi = qb * 16 + ot
                if gi >= 2:
                    te.wait_ge(S["o_ev"].h, base["o_ev"] + gi - 1)
                p = pb[ot % 2]
                for dc in range(NDC):
                    mm = te.matmul(p[:, :], vWo[:, dc, ot * 128:(ot + 1) * 128],
                                   normT[:, dc, qb * 512:(qb + 1) * 512],
                                   start=(dc == 0), stop=(dc == NDC - 1))
                S["o_mm"].inc(mm)
                po = ot - 1
                if po >= 0:
                    gp = qb * 16 + po
                    te.wait_ge(S["o_z"].h, base["o_z"] + gp + 1)
                    te.wait_ge(S["o_sq"].h, base["o_sq"] + gp + 1)
                    if po == 0:
                        te.wait_ge(S["o_sc"].h, base["o_sc"] + 2 * qb)
                    te.matmul(psS1[:, :], sb["s_onescol"][:], vz1[po % 2],
                              start=(po == 0), stop=False)
                    S["o_s2"].inc(te.matmul(psS2[:, :], sb["s_onescol"][:], vsq[po % 2],
                                            start=(po == 0), stop=False))
            po = 15
            gp = qb * 16 + po
            te.wait_ge(S["o_z"].h, base["o_z"] + gp + 1)
            te.wait_ge(S["o_sq"].h, base["o_sq"] + gp + 1)
            te.matmul(psS1[:, :], sb["s_onescol"][:], vz1[po % 2], start=False, stop=True)
            S["o_s2"].inc(te.matmul(psS2[:, :], sb["s_onescol"][:], vsq[po % 2],
                                    start=False, stop=True))
    block.tensor(tes)

    def acts(s):
        for qb in range(4):
            for ot in range(16):
                gi = qb * 16 + ot
                s.wait_ge(S["o_mm"].h, base["o_mm"] + gi + 1)
                if gi >= 2:
                    s.wait_ge(S["o_z"].h, base["o_z"] + gi - 1)
                S["o_ev"].inc(s.activation(vz[ot % 2], pb[ot % 2][:, :], AF.Identity,
                                           bias=sb["s_bo"][:, ot:ot + 1], scale=1.0))
                s.wait_ge(S["o_z"].h, base["o_z"] + gi + 1)
                if gi >= 2:
                    s.wait_ge(S["o_s2"].h, base["o_s2"] + gi - 1)
                S["o_sq"].inc(s.activation(vsq[ot % 2], vz1[ot % 2], AF.Square,
                                           bias=0.0, scale=1.0))
            s.wait_ge(S["o_s2"].h, base["o_s2"] + 16 * (qb + 1))
            s.copy(r_sz[0:1, qb * 512:(qb + 1) * 512], psS1[0:1, :])
            S["o_sc"].inc(s.copy(r_sq[0:1, qb * 512:(qb + 1) * 512], psS2[0:1, :]), 2)
    block.scalar(acts)

    def ves(v):
        for qb in range(4):
            for ot in range(16):
                gi = qb * 16 + ot
                v.wait_ge(S["o_ev"].h, base["o_ev"] + gi + 1)
                v.wait_ge(S["o_xl"].h, base["o_xl"] + 16 * (gi + 1))
                if gi >= 2:
                    v.wait_ge(S["o_sq"].h, base["o_sq"] + gi - 1)
                    v.wait_ge(S["o_st"].h, base["o_st"] + 16 * (gi - 1))
                    v.wait_ge(S["o_s2"].h, base["o_s2"] + gi - 1)
                S["o_z"].inc(v.tensor_tensor(vz1[ot % 2], vz[ot % 2], vxf[ot % 2],
                                             op=ALU.add))
        v.wait_ge(S["o_sc"].h, base["o_sc"] + 8)
        _vs(ctx, v, v.tensor_scalar(r_sz, r_sz, float(1.0 / D), 0.0, op0=ALU.mult, op1=ALU.add))
        _vs(ctx, v, v.tensor_tensor(r_tmp, r_sz, r_sz, op=ALU.mult))
        S["o_z"].inc(v.scalar_tensor_tensor(r_sq, r_sq, float(1.0 / D), r_tmp,
                                            op0=ALU.mult, op1=ALU.subtract))
    block.vector(ves)

    def acts2(s):
        s.wait_ge(S["o_z"].h, S["o_z"].n)
        S["o_sc"].inc(s.activation(r_tmp, r_sq, AF.Sqrt, bias=sb["s_eps"][0:1, 0:1], scale=1.0))
    block.scalar(acts2)

    def ves2(v):
        v.wait_ge(S["o_sc"].h, S["o_sc"].n)
        _vs(ctx, v, v.reciprocal(r_r, r_tmp))
        S["l_hb"].inc(v.tensor_tensor(r_sz, r_sz, r_r, op=ALU.mult))
    block.vector(ves2)

    lb = dict(l_hb=S["l_hb"].n, l_zl=base["l_zl"], l_ap=base["l_ap"],
              g_bc=base["g_bc"], g_cp=base["g_cp"], o_st=base["o_st"])

    def tes2(te):
        te.wait_ge(S["l_hb"].h, lb["l_hb"])
        for qb in range(4):
            if qb >= 1:
                te.wait_ge(S["l_ap"].h, lb["l_ap"] + 32 * qb)
            te.matmul(pb[4][:, :], sb["s_ones1f"][:],
                      r_r[0:1, qb * 512:(qb + 1) * 512], start=True, stop=True)
            S["g_bc"].inc(te.matmul(pb[5][:, :], sb["s_ones1f"][:],
                                    r_sz[0:1, qb * 512:(qb + 1) * 512],
                                    start=True, stop=True))
    block.tensor(tes2)

    def acts3(s):
        for qb in range(4):
            s.wait_ge(S["g_bc"].h, lb["g_bc"] + qb + 1)
            if qb >= 1:
                s.wait_ge(S["l_ap"].h, lb["l_ap"] + 32 * qb)
            s.copy(vRR, pb[4][:, :])
            S["g_cp"].inc(s.copy(vMM, pb[5][:, :]), 2)
    block.scalar(acts3)

    def ves3(v):
        for qb in range(4):
            v.wait_ge(S["g_cp"].h, lb["g_cp"] + 2 * (qb + 1))
            for dc in range(16):
                ai = qb * 16 + dc
                v.wait_ge(S["l_zl"].h, lb["l_zl"] + 16 * (ai + 1))
                if ai >= 2:
                    v.wait_ge(S["o_st"].h, lb["o_st"] + 16 * 64 + 16 * (ai - 1))
                v.tensor_tensor(vtp[dc % 2], vzl[dc % 2], vRR, op=ALU.mult)
                S["l_ap"].inc(v.tensor_tensor(vh1[dc % 2], vtp[dc % 2], vMM,
                                              op=ALU.subtract))
                S["l_ap"].inc(v.tensor_copy(h1T[:, dc, qb * 512:(qb + 1) * 512],
                                            vh1[dc % 2]))
    block.vector(ves3)


def _ph5_ffn(ctx, block):
    t, sb, S, pb = ctx.t, ctx.sb, ctx.S, ctx.pb
    psS1, psS2 = ctx.psS1, ctx.psS2
    carve = ctx.carve

    h1T = carve("arA", 0, BF16, 16 * 2048, "p (h j) -> p h j", h=16)
    interT = carve("arB", 0, BF16, 64 * 512, "p (c j) -> p c j", c=64)
    gsl = [carve("arC", i * 4096, BF16, 16 * 128, "p (c j) -> p c j", c=16) for i in range(2)]
    usl = [carve("arC", 8192 + i * 4096, BF16, 16 * 128, "p (c j) -> p c j", c=16) for i in range(2)]
    dsl = [carve("arC", 16384 + i * 4096, BF16, 16 * 128, "p (c j) -> p c j", c=16) for i in range(2)]
    vsil = [carve("arC", 24576 + i * 1024, BF16, 512) for i in range(2)]
    vz = [carve("arC", 26624 + i * 2048, F32, 512) for i in range(2)]
    vz2 = [carve("arC", 30720 + i * 2048, F32, 512) for i in range(2)]
    vsq = [carve("arC", 34816 + i * 2048, F32, 512) for i in range(2)]
    vhl = [carve("arC", 38912 + i * 2048, F32, 512) for i in range(2)]
    r_sz = sb["rowsA"][0:1, :]
    r_sq = sb["rowsA"][32:33, :]
    r_r = sb["rowsB"][0:1, :]
    r_tmp = sb["rowsB"][32:33, :]

    base = dict(f_wl=S["f_wl"].n, f_mm=S["f_mm"].n, f_sil=S["f_sil"].n,
                f_mu=S["f_mu"].n,
                f_dwl=S["f_dwl"].n, f_dmm=S["f_dmm"].n, f_ev=S["f_ev"].n,
                f_hl=S["f_hl"].n, f_z2=S["f_z2"].n, f_sq=S["f_sq"].n,
                f_st=S["f_st"].n, f_sc=S["f_sc"].n, f_s2=S["f_s2"].n,
                l_ap=S["l_ap"].n, wst=S["wst"].n, o_st=S["o_st"].n,
                o_z=S["o_z"].n, o_s2=S["o_s2"].n, o_sc=S["o_sc"].n,
                g_bc=S["g_bc"].n, g_cp=S["g_cp"].n, l_hb=S["l_hb"].n)

    def syncs(sy):
        sy.wait_ge(S["wst"].h, base["wst"])
        sy.wait_ge(S["o_z"].h, base["o_z"])
        sy.wait_ge(S["o_s2"].h, base["o_s2"])
        sy.wait_ge(S["o_st"].h, base["o_st"])
        for tb in range(4):
            for it in range(64):
                si = tb * 64 + it
                if si >= 2:
                    sy.wait_ge(S["f_mm"].h, base["f_mm"] + 2 * (si - 1))
                S["f_wl"].inc(sy.dma_start(
                    out=gsl[it % 2], in_=t["wgg"][it].rearrange("c p j -> p c j")), 16)
                S["f_wl"].inc(sy.dma_start(
                    out=usl[it % 2], in_=t["wgu"][it].rearrange("c p j -> p c j")), 16)
            for ot in (range(16) if _PH5_DOWN else []):
                hi = tb * 16 + ot
                for qq in range(4):
                    dq = ot * 4 + qq
                    di = tb * 64 + dq
                    if di == 0:
                        sy.wait_ge(S["l_ap"].h, base["l_ap"])
                        sy.wait_ge(S["g_cp"].h, base["g_cp"])
                    if di >= 2:
                        sy.wait_ge(S["f_dmm"].h, base["f_dmm"] + di - 1)
                    S["f_dwl"].inc(sy.dma_start(
                        out=dsl[dq % 2],
                        in_=t["wgd"][ot, qq * 16:(qq + 1) * 16].rearrange("c p j -> p c j")), 16)
                if hi >= 2:
                    sy.wait_ge(S["f_z2"].h, base["f_z2"] + hi - 1)
                S["f_hl"].inc(sy.dma_start(
                    out=vhl[ot % 2],
                    in_=t["h1f"][ot * 128:(ot + 1) * 128, tb * 512:(tb + 1) * 512]), 16)
    block.sync(syncs)

    def tes(te):
        te.wait_ge(S["l_ap"].h, base["l_ap"])
        te.wait_ge(S["g_cp"].h, base["g_cp"])
        for tb in range(4):
            for it in range(64):
                si = tb * 64 + it
                te.wait_ge(S["f_wl"].h, base["f_wl"] + 32 * (si + 1))
                if _PH5_DOWN and tb >= 1 and it == 0:
                    te.wait_ge(S["f_dmm"].h, base["f_dmm"] + 64 * tb)
                if si >= 2:
                    te.wait_ge(S["f_sil"].h, base["f_sil"] + si - 1)
                    te.wait_ge(S["f_mu"].h, base["f_mu"] + si - 1)
                for dc in range(NDC):
                    mm = te.matmul(pb[2 * (si % 2)][:, :], gsl[it % 2][:, dc, :],
                                   h1T[:, dc, tb * 512:(tb + 1) * 512],
                                   start=(dc == 0), stop=(dc == NDC - 1))
                S["f_mm"].inc(mm)
                for dc in range(NDC):
                    mm = te.matmul(pb[1 + 2 * (si % 2)][:, :], usl[it % 2][:, dc, :],
                                   h1T[:, dc, tb * 512:(tb + 1) * 512],
                                   start=(dc == 0), stop=(dc == NDC - 1))
                S["f_mm"].inc(mm)
            for ot in (range(16) if _PH5_DOWN else []):
                hi = tb * 16 + ot
                te.wait_ge(S["f_mu"].h, base["f_mu"] + 64 * (tb + 1))
                if hi >= 2:
                    te.wait_ge(S["f_ev"].h, base["f_ev"] + hi - 1)
                p = pb[4 + ot % 2]
                for qq in range(4):
                    di = tb * 64 + ot * 4 + qq
                    te.wait_ge(S["f_dwl"].h, base["f_dwl"] + 16 * (di + 1))
                    for c in range(16):
                        ic = qq * 16 + c
                        mm = te.matmul(p[:, :], dsl[(ot * 4 + qq) % 2][:, c, :],
                                       interT[:, ic, :],
                                       start=(ic == 0), stop=(ic == 63))
                    S["f_dmm"].inc(mm)
                po = ot - 1
                if po >= 0:
                    hp = tb * 16 + po
                    te.wait_ge(S["f_z2"].h, base["f_z2"] + hp + 1)
                    te.wait_ge(S["f_sq"].h, base["f_sq"] + hp + 1)
                    if po == 0:
                        te.wait_ge(S["f_sc"].h, base["f_sc"] + 2 * tb)
                    te.matmul(psS1[:, :], sb["s_onescol"][:], vz2[po % 2],
                              start=(po == 0), stop=False)
                    S["f_s2"].inc(te.matmul(psS2[:, :], sb["s_onescol"][:], vsq[po % 2],
                                            start=(po == 0), stop=False))
            if _PH5_DOWN:
                po = 15
                hp = tb * 16 + po
                te.wait_ge(S["f_z2"].h, base["f_z2"] + hp + 1)
                te.wait_ge(S["f_sq"].h, base["f_sq"] + hp + 1)
                te.matmul(psS1[:, :], sb["s_onescol"][:], vz2[po % 2], start=False, stop=True)
                S["f_s2"].inc(te.matmul(psS2[:, :], sb["s_onescol"][:], vsq[po % 2],
                                        start=False, stop=True))
    block.tensor(tes)

    def acts(s):
        for tb in range(4):
            for it in range(64):
                si = tb * 64 + it
                s.wait_ge(S["f_mm"].h, base["f_mm"] + 2 * si + 1)
                if si >= 2:
                    s.wait_ge(S["f_mu"].h, base["f_mu"] + si - 1)
                S["f_sil"].inc(s.activation(vsil[si % 2], pb[2 * (si % 2)][:, :],
                                            AF.Silu, bias=0.0, scale=1.0))
            for ot in (range(16) if _PH5_DOWN else []):
                hi = tb * 16 + ot
                s.wait_ge(S["f_dmm"].h, base["f_dmm"] + 4 * (hi + 1))
                if hi >= 2:
                    s.wait_ge(S["f_z2"].h, base["f_z2"] + hi - 1)
                S["f_ev"].inc(s.activation(vz[ot % 2], pb[4 + ot % 2][:, :], AF.Copy,
                                           bias=0.0, scale=1.0))
                s.wait_ge(S["f_z2"].h, base["f_z2"] + hi + 1)
                if hi >= 2:
                    s.wait_ge(S["f_s2"].h, base["f_s2"] + hi - 1)
                S["f_sq"].inc(s.activation(vsq[ot % 2], vz2[ot % 2], AF.Square,
                                           bias=0.0, scale=1.0))
            if _PH5_DOWN:
                s.wait_ge(S["f_s2"].h, base["f_s2"] + 16 * (tb + 1))
                if tb == 0:
                    s.wait_ge(S["g_bc"].h, base["g_bc"])
                s.copy(r_sz[0:1, tb * 512:(tb + 1) * 512], psS1[0:1, :])
                S["f_sc"].inc(s.copy(r_sq[0:1, tb * 512:(tb + 1) * 512], psS2[0:1, :]), 2)
    block.scalar(acts)

    def ves(v):
        for tb in range(4):
            for it in range(64):
                si = tb * 64 + it
                v.wait_ge(S["f_sil"].h, base["f_sil"] + si + 1)
                v.wait_ge(S["f_mm"].h, base["f_mm"] + 2 * si + 2)
                if _PH5_DOWN and tb >= 1 and it == 0:
                    v.wait_ge(S["f_dmm"].h, base["f_dmm"] + 64 * tb)
                S["f_mu"].inc(v.tensor_tensor(interT[:, it, :], vsil[si % 2],
                                               pb[1 + 2 * (si % 2)][:, :], op=ALU.mult))
            for ot in (range(16) if _PH5_DOWN else []):
                hi = tb * 16 + ot
                v.wait_ge(S["f_ev"].h, base["f_ev"] + hi + 1)
                v.wait_ge(S["f_hl"].h, base["f_hl"] + 16 * (hi + 1))
                if hi >= 2:
                    v.wait_ge(S["f_sq"].h, base["f_sq"] + hi - 1)
                    v.wait_ge(S["f_st"].h, base["f_st"] + 16 * (hi - 1))
                    v.wait_ge(S["f_s2"].h, base["f_s2"] + hi - 1)
                S["f_z2"].inc(v.tensor_tensor(vz2[ot % 2], vz[ot % 2], vhl[ot % 2],
                                              op=ALU.add))
        if not _PH5_DOWN:
            return
        v.wait_ge(S["f_sc"].h, base["f_sc"] + 8)
        _vs(ctx, v, v.tensor_scalar(r_sz, r_sz, float(1.0 / D), 0.0, op0=ALU.mult, op1=ALU.add))
        _vs(ctx, v, v.tensor_tensor(r_tmp, r_sz, r_sz, op=ALU.mult))
        S["f_z2"].inc(v.scalar_tensor_tensor(r_sq, r_sq, float(1.0 / D), r_tmp,
                                             op0=ALU.mult, op1=ALU.subtract))
    block.vector(ves)

    def gps(g):
        for tb in (range(4) if _PH5_DOWN else []):
            for ot in range(16):
                hi = tb * 16 + ot
                g.wait_ge(S["f_z2"].h, base["f_z2"] + hi + 1)
                S["f_st"].inc(g.dma_start(
                    out=t["z2"][ot * 128:(ot + 1) * 128, tb * 512:(tb + 1) * 512],
                    in_=vz2[ot % 2]), 16)
    block.gpsimd(gps)

    def acts2(s):
        if not _PH5_DOWN:
            return
        s.wait_ge(S["f_z2"].h, S["f_z2"].n)
        S["f_sc"].inc(s.activation(r_tmp, r_sq, AF.Sqrt, bias=sb["s_eps"][0:1, 0:1], scale=1.0))
    block.scalar(acts2)

    def ves2(v):
        if not _PH5_DOWN:
            return
        v.wait_ge(S["f_sc"].h, S["f_sc"].n)
        _vs(ctx, v, v.reciprocal(r_r, r_tmp))
        S["l_hb"].inc(v.tensor_tensor(r_sz, r_sz, r_r, op=ALU.mult))
    block.vector(ves2)


def _ph6_out(ctx, block):
    t, sb, S, pb = ctx.t, ctx.sb, ctx.S, ctx.pb
    carve = ctx.carve

    vzl = [carve("arC", i * 2048, F32, 512) for i in range(2)]
    vout = [carve("arC", 4096 + i * 2048, F32, 512) for i in range(2)]
    vRR = carve("arC", 8192, F32, 512)
    vMM = carve("arC", 10240, F32, 512)
    vtp = [carve("arC", 12288 + i * 2048, F32, 512) for i in range(2)]
    r_sz = sb["rowsA"][0:1, :]
    r_r = sb["rowsB"][0:1, :]

    base = dict(l_hb=S["l_hb"].n, g_zl=S["g_zl"].n, g_ap=S["g_ap"].n,
                g_st=S["g_st"].n, g_bc=S["g_bc"].n, g_cp=S["g_cp"].n,
                f_st=S["f_st"].n, f_mm=S["f_mm"].n, f_ev=S["f_ev"].n)

    def tes(te):
        te.wait_ge(S["l_hb"].h, base["l_hb"])
        te.wait_ge(S["f_ev"].h, base["f_ev"])
        for qb in range(4):
            if qb >= 1:
                te.wait_ge(S["g_ap"].h, base["g_ap"] + 32 * qb)
            te.matmul(pb[4][:, :], sb["s_ones1f"][:],
                      r_r[0:1, qb * 512:(qb + 1) * 512], start=True, stop=True)
            S["g_bc"].inc(te.matmul(pb[5][:, :], sb["s_ones1f"][:],
                                    r_sz[0:1, qb * 512:(qb + 1) * 512],
                                    start=True, stop=True))
    block.tensor(tes)

    def acts(s):
        for qb in range(4):
            s.wait_ge(S["g_bc"].h, base["g_bc"] + qb + 1)
            if qb >= 1:
                s.wait_ge(S["g_ap"].h, base["g_ap"] + 32 * qb)
            s.copy(vRR, pb[4][:, :])
            S["g_cp"].inc(s.copy(vMM, pb[5][:, :]), 2)
    block.scalar(acts)

    def syncs(sy):
        sy.wait_ge(S["f_st"].h, base["f_st"])
        sy.wait_ge(S["f_mm"].h, base["f_mm"])
        for qb in range(4):
            for dc in range(16):
                ai = qb * 16 + dc
                if ai >= 2:
                    sy.wait_ge(S["g_ap"].h, base["g_ap"] + 2 * (ai - 1))
                S["g_zl"].inc(sy.dma_start(
                    out=vzl[dc % 2],
                    in_=t["z2"][dc * 128:(dc + 1) * 128, qb * 512:(qb + 1) * 512]), 16)
    block.sync(syncs)

    def ves(v):
        for qb in range(4):
            v.wait_ge(S["g_cp"].h, base["g_cp"] + 2 * (qb + 1))
            for dc in range(16):
                ai = qb * 16 + dc
                v.wait_ge(S["g_zl"].h, base["g_zl"] + 16 * (ai + 1))
                if ai >= 2:
                    v.wait_ge(S["g_st"].h, base["g_st"] + 16 * (ai - 1))
                v.tensor_tensor(vtp[dc % 2], vzl[dc % 2], vRR, op=ALU.mult)
                S["g_ap"].inc(v.tensor_tensor(vout[dc % 2], vtp[dc % 2], vMM,
                                              op=ALU.subtract), 2)
    block.vector(ves)

    def gps(g):
        for qb in range(4):
            for dc in range(16):
                ai = qb * 16 + dc
                g.wait_ge(S["g_ap"].h, base["g_ap"] + 2 * (ai + 1))
                S["g_st"].inc(g.dma_start(
                    out=t["outT"][dc * 128:(dc + 1) * 128, qb * 512:(qb + 1) * 512],
                    in_=vout[dc % 2]), 16)
        g.wait_ge(S["g_st"].h, base["g_st"] + 16 * 64)
    block.gpsimd(gps)


def kernel(**inputs):
    hc = _host_consts()
    in_maps = [_core_in_map(c, inputs, hc) for c in range(8)]
    nc, _ = _build()
    res = run_bass_kernel_spmd(nc, in_maps, core_ids=list(range(8)))
    out = np.zeros((B, S, D), np.float32)
    for c in range(8):
        b, half = c // 2, c % 2
        out[b, half * TOK:(half + 1) * TOK, :] = np.asarray(res.results[c]["outT"]).T
    return out


# version 11
# speedup vs baseline: 1.2225x; 1.0489x over previous
import math
import numpy as np
from contextlib import ExitStack

import concourse.bass as bass
import concourse.mybir as mybir
from concourse.bass_utils import run_bass_kernel_spmd

F32 = mybir.dt.float32
BF16 = mybir.dt.bfloat16
U8 = mybir.dt.uint8
AF = mybir.ActivationFunctionType
ALU = mybir.AluOpType

B, S, D, H, DH, FF = 4, 4096, 2048, 16, 128, 8192
_PH5_DOWN = True
TOK = 2048
TFULL = 4096
NDC = D // 128
NCTRL = 130156
STEP64 = (NCTRL - 1) / (D * FF - 1)
NREC = 256
MAGIC = 12582912.0
TWO_PI = float(2.0 * np.pi)
LN_EPS = 1e-5

_bfd = None
def _bf16(x):
    global _bfd
    if _bfd is None:
        import ml_dtypes
        _bfd = ml_dtypes.bfloat16
    return np.asarray(x, np.float64).astype(_bfd).astype(np.float64)


_HOSTC = None

def _host_consts():
    global _HOSTC
    if _HOSTC is not None:
        return _HOSTC
    d = {}
    halos = np.zeros((3, 5, 128, 1024), np.int64)
    f0col = np.zeros((3, 128, 1024), np.float32)
    betacol = np.zeros((3, 128, 1024), np.float32)
    nbrows = np.zeros((3, NREC, 2, 512), np.float32)
    for w, (O, I) in enumerate([(FF, D), (FF, D), (D, FF)]):
        nic = I // 128
        rec = np.arange(NREC)
        ic = rec % nic
        ob = rec // nic
        j = np.arange(512)
        n0 = (ob[:, None] * 512 + j[None, :]).astype(np.int64) * I + (ic[:, None] * 128)
        t0 = n0.astype(np.float64) * STEP64
        sA = np.minimum(np.floor(t0).astype(np.int64), NCTRL - 2)
        f0 = t0 - sA
        beta = (1.0 - f0) / STEP64
        pp = (rec // 2)[:, None].repeat(512, 1)
        xx = (rec % 2)[:, None] * 512 + j[None, :]
        for a in range(5):
            hid = np.clip(sA - 1 + a, 0, NCTRL - 1)
            halos[w, a, pp, xx] = hid
        f0col[w, pp, xx] = f0.astype(np.float32)
        betacol[w, pp, xx] = beta.astype(np.float32)
        nbh = _bf16(-beta)
        nbl = _bf16(-beta - nbh)
        nbrows[w, :, 0, :] = nbh.astype(np.float32)
        nbrows[w, :, 1, :] = nbl.astype(np.float32)
    d["halo_idx"] = halos
    d["f0col"] = f0col
    d["betacol"] = betacol
    d["nbrows"] = nbrows
    pv = np.arange(128, dtype=np.float64)
    basis = np.stack([pv**m for m in range(4)], axis=0)
    bh = _bf16(basis)
    bl = _bf16(basis - bh)
    cb = np.zeros((12, 128), np.float32)
    for m in range(4):
        cb[3 * m + 0] = bh[m]
        cb[3 * m + 1] = bl[m]
        cb[3 * m + 2] = bh[m]
    d["cbasis"] = cb
    ab = np.zeros((5, 128), np.float32)
    ab[0] = 1.0
    ab[1] = 1.0
    ab[2] = bh[1]
    ab[3] = bl[1]
    ab[4] = bh[1]
    d["abasis"] = ab
    d["ones2"] = np.ones((2, 128), np.float32)
    d["iotacol"] = np.arange(128, dtype=np.float32)[:, None].copy()
    d["onescol"] = np.ones((128, 1), np.float32)
    d["ones1f"] = np.ones((1, 128), np.float32)
    d["ones128"] = np.ones((128, 128), np.float32)
    d["epscol"] = np.full((128, 1), 1e-5, np.float32)
    _HOSTC = d
    return d


def _core_in_map(c, inp, hc):
    b, half = c // 2, c % 2
    x = np.asarray(inp["L0_attn_x"])
    xb = x[b]
    own = xb[half * TOK:(half + 1) * TOK]
    oth = xb[(1 - half) * TOK:(2 - half) * TOK]
    xT = np.ascontiguousarray(np.concatenate([own, oth], axis=0).T)
    m = {"xT": xT}
    for nm, key in [("wqT", "L0_attn_Wq"), ("wkT", "L0_attn_Wk"),
                    ("wvT", "L0_attn_Wv"), ("woT", "L0_attn_Wo")]:
        m[nm] = np.ascontiguousarray(np.asarray(inp[key]).T)
    for nm, key in [("bq2d", "L0_attn_bq"), ("bk2d", "L0_attn_bk"),
                    ("bo2d", "L0_attn_bo"), ("hel2d", "L0_attn_helix_rotation")]:
        m[nm] = np.ascontiguousarray(np.asarray(inp[key]).reshape(H, DH).T)
    m["bv_row"] = np.ascontiguousarray(np.asarray(inp["L0_attn_bv"]).reshape(1, D))
    m["spiral_row"] = np.ascontiguousarray(np.asarray(inp["L0_attn_spiral_weight"]).reshape(1, H))
    cps = [np.asarray(inp["L0_ffn_gate_cp"]), np.asarray(inp["L0_ffn_up_cp"]),
           np.asarray(inp["L0_ffn_down_cp"])]
    halo = np.zeros((3, 5, 128, 1024), np.float32)
    for w in range(3):
        halo[w] = np.take(cps[w], hc["halo_idx"][w])
    m["halo"] = halo
    import ml_dtypes
    for nm in ["f0col", "betacol", "cbasis", "abasis", "ones2",
               "iotacol", "onescol", "ones1f", "ones128", "epscol"]:
        m[nm] = hc[nm]
    m["nbrows"] = hc["nbrows"].astype(ml_dtypes.bfloat16)
    return m


class _Ctx:
    pass


class _S:
    def __init__(self, h):
        self.h = h
        self.n = 0

    def inc(self, instr, k=1):
        instr.then_inc(self.h, k)
        self.n += k
        return self.n


def _build(dump=(), upto=9):
    nc = bass.Bass("TRN2", target_bir_lowering=False, debug=False)
    t = {}

    def dram(name, shape, dtype, kind="Internal"):
        if name in dump:
            kind = "ExternalOutput"
        t[name] = nc.dram_tensor(name, list(shape), dtype, kind=kind)

    def din(name, shape, dtype=F32):
        t[name] = nc.dram_tensor(name, list(shape), dtype, kind="ExternalInput")

    din("xT", [D, TFULL])
    for nm in ["wqT", "wkT", "wvT", "woT"]:
        din(nm, [D, D])
    for nm in ["bq2d", "bk2d", "bo2d", "hel2d"]:
        din(nm, [128, H])
    din("bv_row", [1, D])
    din("spiral_row", [1, H])
    din("halo", [3, 5, 128, 1024])
    din("f0col", [3, 128, 1024])
    din("betacol", [3, 128, 1024])
    din("nbrows", [3, NREC, 2, 512], BF16)
    din("cbasis", [12, 128])
    din("abasis", [5, 128])
    din("ones2", [2, 128])
    din("iotacol", [128, 1])
    din("onescol", [128, 1])
    din("ones1f", [1, 128])
    din("ones128", [128, 128])
    din("epscol", [128, 1])

    dram("recsC", [3, NREC, 12, 512], BF16)
    dram("recsA", [3, NREC, 5, 512], BF16)
    dram("wgg", [64, 16, 128, 128], BF16)
    dram("wgu", [64, 16, 128, 128], BF16)
    dram("wgd", [16, 64, 128, 128], BF16)
    dram("KT", [D, TFULL], BF16)
    dram("Vt", [H, TFULL // 128, 128, 128], BF16)
    dram("QT", [D, TOK], BF16)
    dram("z1", [D, TOK], F32)
    dram("h1f", [D, TOK], F32)
    dram("z2", [D, TOK], F32)
    dram("dbgt", [128, 160], F32)
    dram("outT", [D, TOK], F32, kind="ExternalOutput")

    ctx = _Ctx()
    ctx.nc = nc
    ctx.t = t

    with ExitStack() as es:
        E = es.enter_context
        sbd = {}

        def sbuf(name, shape, dtype):
            sbd[name] = E(nc.sbuf_tensor(name, list(shape), dtype))

        sbuf("s_cb", [12, 128], BF16)
        sbuf("s_ab", [5, 128], BF16)
        sbuf("s_ones2", [2, 128], BF16)
        sbuf("s_iota", [128, 1], F32)
        sbuf("s_onescol", [128, 1], F32)
        sbuf("s_ones1f", [1, 128], F32)
        sbuf("s_onesall", [128, 128], F32)
        sbuf("s_eps", [128, 1], F32)
        sbuf("s_ones1b", [1, 128], BF16)
        sbuf("s_hel", [128, H], F32)
        sbuf("s_bq", [128, H], F32)
        sbuf("s_bk", [128, H], F32)
        sbuf("s_bo", [128, H], F32)
        sbuf("s_sp", [1, H], F32)
        for nm in ["tb_r", "tb_sin", "tb_s2", "tb_sq2", "tb_cos", "tb_qb",
                   "tb_ks", "tb_kb", "tb_t1", "tb_sb"]:
            sbuf(nm, [128, H], F32)
        sbuf("tb_sr", [1, H], F32)
        sbuf("rowsA", [128, TOK], F32)
        sbuf("rowsB", [128, TOK], F32)

        sbuf("arA", [128, 65536], U8)
        sbuf("arB", [128, 65536], U8)
        sbuf("arC", [128, 49152], U8)
        sbuf("arD", [128, 12288], U8)

        def carve(ar, off, dtype, n, rearr=None, **kw):
            esz = 2 if dtype == BF16 else 4
            ap = sbd[ar][:, off:off + n * esz].bitcast(dtype)
            if rearr:
                ap = ap.rearrange(rearr, **kw)
            return ap

        ctx.carve = carve
        ctx.sb = sbd

        pb = [E(nc.psum_tensor(f"pb{i}", [128, 512], F32)) for i in range(6)]
        psS1 = E(nc.psum_tensor("psS1", [1, 512], F32))
        psS2 = E(nc.psum_tensor("psS2", [1, 512], F32))
        ctx.pb, ctx.psS1, ctx.psS2 = pb, psS1, psS2

        sems = {}
        for nm in ["cl", "cg", "tabv", "tabs", "tmm",
                   "p1l", "p1v", "p1s",
                   "wls", "wln", "wmm", "wact", "wve", "wst",
                   "q_xl", "q_wl", "q_mm", "q_ev", "q_st",
                   "a_kv", "a_q", "a_s", "a_x", "a_v", "a_d", "a_f",
                   "o_wl", "o_mm", "o_ev", "o_xl", "o_z", "o_sq", "o_st",
                   "o_sc", "o_s2",
                   "l_zl", "l_ap", "l_hb",
                   "f_wl", "f_mm", "f_sil", "f_dwl", "f_dmm", "f_ev",
                   "f_hl", "f_z2", "f_sq", "f_st", "f_sc", "f_s2", "f_mu",
                   "g_zl", "g_ap", "g_st", "g_bc", "g_cp", "vsf", "asf",
                   "lh1", "lsq", "nh1", "nsq", "n_hb", "o_hf"]:
            sems[nm] = _S(E(nc.semaphore(nm)))
        ctx.S = sems

        with nc.Block() as block:
            phases = [_ph0, _ph1_prep, _ph1_main, _ph2_qkv, _ph3_attn,
                      _ph4_wo_ln1, _ph5_ffn, _ph6_out]
            for pi, ph in enumerate(phases):
                if pi <= upto:
                    ph(ctx, block)
            if upto < len(phases) - 1:
                def _tail(g):
                    for nm in ["p1s", "wst", "q_st", "o_st", "f_st", "g_st"]:
                        if sems[nm].n:
                            g.wait_ge(sems[nm].h, sems[nm].n)
                block.gpsimd(_tail)

    return nc, ctx


def _vs(ctx, v, instr):
    n = ctx.S["vsf"].inc(instr)
    v.wait_ge(ctx.S["vsf"].h, n)


def _as(ctx, s, instr):
    n = ctx.S["asf"].inc(instr)
    s.wait_ge(ctx.S["asf"].h, n)


def _ph0(ctx, block):
    t, sb, S, pb = ctx.t, ctx.sb, ctx.S, ctx.pb
    s_bv = ctx.carve("arA", 49152, BF16, D)[0:1, :]
    ctx.s_bv = s_bv

    def g_loads(g):
        for nm, dst in [("cbasis", "s_cb"), ("abasis", "s_ab"), ("ones2", "s_ones2")]:
            S["cg"].inc(g.dma_start(out=sb[dst][:], in_=t[nm][:, :]), 16)
        S["cg"].inc(g.dma_start(out=sb["s_ones1b"][:], in_=t["ones1f"][:, :]), 16)
        S["cg"].inc(g.dma_start(out=s_bv, in_=t["bv_row"][:, :]), 16)
    block.gpsimd(g_loads)

    def s_loads(sy):
        for nm, dst in [("iotacol", "s_iota"), ("onescol", "s_onescol"),
                        ("ones1f", "s_ones1f"), ("ones128", "s_onesall"), ("epscol", "s_eps"), ("hel2d", "s_hel"), ("bq2d", "s_bq"),
                        ("bk2d", "s_bk"), ("bo2d", "s_bo"), ("spiral_row", "s_sp")]:
            S["cl"].inc(sy.dma_start(out=sb[dst][:], in_=t[nm][:, :]), 16)
    block.sync(s_loads)

    def v1(v):
        v.wait_ge(S["cl"].h, S["cl"].n)
        BIG = 1e30
        _vs(ctx, v, v.tensor_scalar(sb["tb_t1"][:], sb["s_hel"][:], float(np.pi), BIG,
                                    op0=ALU.subtract, op1=ALU.mult))
        _vs(ctx, v, v.tensor_scalar(sb["tb_t1"][:], sb["tb_t1"][:], 0.0, 1.0,
                                    op0=ALU.max, op1=ALU.min))
        _vs(ctx, v, v.tensor_scalar(sb["tb_s2"][:], sb["s_hel"][:], float(-np.pi), -BIG,
                                    op0=ALU.subtract, op1=ALU.mult))
        _vs(ctx, v, v.tensor_scalar(sb["tb_s2"][:], sb["tb_s2"][:], 0.0, 1.0,
                                    op0=ALU.max, op1=ALU.min))
        _vs(ctx, v, v.tensor_tensor(sb["tb_t1"][:], sb["tb_t1"][:], sb["tb_s2"][:], op=ALU.subtract))
        _vs(ctx, v, v.scalar_tensor_tensor(sb["tb_r"][:], sb["tb_t1"][:], -TWO_PI, sb["s_hel"][:],
                                           op0=ALU.mult, op1=ALU.add))
        S["tabv"].inc(v.tensor_scalar(sb["tb_r"][:], sb["tb_r"][:], float(np.pi),
                                      float(-np.pi), op0=ALU.min, op1=ALU.max))
    block.vector(v1)

    def a1(s):
        s.wait_ge(S["tabv"].h, S["tabv"].n)
        _as(ctx, s, s.activation(sb["tb_sin"][:], sb["tb_r"][:], AF.Sin, bias=0.0, scale=1.0))
        _as(ctx, s, s.activation(sb["tb_s2"][:], sb["tb_r"][:], AF.Sin, bias=0.0, scale=0.5))
        S["tabs"].inc(s.activation(sb["tb_sq2"][:], sb["tb_s2"][:], AF.Square,
                                   bias=0.0, scale=1.0))
    block.scalar(a1)

    def v2(v):
        v.wait_ge(S["tabs"].h, S["tabs"].n)
        _vs(ctx, v, v.tensor_scalar(sb["tb_cos"][:], sb["tb_sq2"][:], -2.0, 1.0, op0=ALU.mult, op1=ALU.add))
        _vs(ctx, v, v.tensor_tensor(sb["tb_qb"][:], sb["tb_cos"][:], sb["s_bq"][:], op=ALU.mult))
        S["tabv"].inc(v.tensor_scalar(sb["tb_sr"][:], sb["s_sp"][:],
                                      float(1.0 / math.sqrt(DH)), 0.0,
                                      op0=ALU.mult, op1=ALU.add))
    block.vector(v2)

    def t1(te):
        te.wait_ge(S["tabv"].h, S["tabv"].n)
        S["tmm"].inc(te.matmul(pb[0][:, 0:H], sb["s_ones1f"][:], sb["tb_sr"][:],
                               start=True, stop=True))
    block.tensor(t1)

    def a2(s):
        s.wait_ge(S["tmm"].h, S["tmm"].n)
        S["tabs"].inc(s.copy(sb["tb_sb"][:], pb[0][:, 0:H]))
    block.scalar(a2)

    def v3(v):
        v.wait_ge(S["tabs"].h, S["tabs"].n)
        _vs(ctx, v, v.tensor_tensor(sb["tb_ks"][:], sb["tb_sin"][:], sb["tb_sb"][:], op=ALU.mult))
        S["tabv"].inc(v.tensor_tensor(sb["tb_kb"][:], sb["tb_ks"][:], sb["s_bk"][:],
                                      op=ALU.mult))
    block.vector(v3)

    def gdbg(g):
        g.wait_ge(S["tabv"].h, S["tabv"].n)
        for i, nm in enumerate(["tb_cos", "tb_qb", "tb_ks", "tb_kb", "s_hel", "tb_r", "tb_sin", "tb_s2", "tb_sq2", "tb_sb"]):
            S["cg"].inc(g.dma_start(out=t["dbgt"][:, i * H:(i + 1) * H], in_=sb[nm][:]), 16)
    block.gpsimd(gdbg)


def _ph1_prep(ctx, block):
    t, sb, S, carve = ctx.t, ctx.sb, ctx.S, ctx.carve
    d = float(STEP64)
    d2, d3 = d * d, d * d * d

    IN = [carve("arB", i * 4096, F32, 1024) for i in range(7)]
    TMP = [carve("arB", (7 + i) * 4096, F32, 1024) for i in range(8)]
    OUT = [carve("arC", i * 2048, BF16, 1024) for i in range(12)]

    for w in range(3):
        base_l = S["p1l"].n

        def loads(sy, w=w):
            if w > 0:
                sy.wait_ge(S["p1v"].h, 6 * w)
            for a in range(5):
                S["p1l"].inc(sy.dma_start(out=IN[a], in_=t["halo"][w, a]), 16)
            S["p1l"].inc(sy.dma_start(out=IN[5], in_=t["f0col"][w]), 16)
            S["p1l"].inc(sy.dma_start(out=IN[6], in_=t["betacol"][w]), 16)
        block.sync(loads)

        stores = []

        def vops(v, w=w, base_l=base_l, stores=stores):
            v.wait_ge(S["p1l"].h, base_l + 7 * 16)
            if w > 0:
                v.wait_ge(S["p1s"].h, 17 * 16 * w)
            h0, h1, h2, h3, h4, f0, bet = IN
            Bc, C, Dc, Pp, Q, t1, t2, _x = TMP
            v.tensor_tensor(t1, h2, h0, op=ALU.subtract)
            v.tensor_scalar(Bc, t1, 0.5, 0.0, op0=ALU.mult, op1=ALU.add)
            v.scalar_tensor_tensor(t1, h1, -2.5, h0, op0=ALU.mult, op1=ALU.add)
            v.scalar_tensor_tensor(t1, h3, -0.5, t1, op0=ALU.mult, op1=ALU.add)
            v.scalar_tensor_tensor(C, h2, 2.0, t1, op0=ALU.mult, op1=ALU.add)
            v.scalar_tensor_tensor(t1, h1, -3.0, h0, op0=ALU.mult, op1=ALU.add)
            v.scalar_tensor_tensor(t1, h2, 3.0, t1, op0=ALU.mult, op1=ALU.add)
            v.scalar_tensor_tensor(t1, h3, -1.0, t1, op0=ALU.mult, op1=ALU.add)
            v.tensor_scalar(Dc, t1, -0.5, 0.0, op0=ALU.mult, op1=ALU.add)
            v.scalar_tensor_tensor(t1, h2, -2.5, h1, op0=ALU.mult, op1=ALU.add)
            v.scalar_tensor_tensor(t1, h4, -0.5, t1, op0=ALU.mult, op1=ALU.add)
            v.scalar_tensor_tensor(t1, h3, 2.0, t1, op0=ALU.mult, op1=ALU.add)
            v.tensor_tensor(t1, t1, C, op=ALU.subtract)
            v.scalar_tensor_tensor(Pp, Dc, -3.0, t1, op0=ALU.mult, op1=ALU.add)
            v.scalar_tensor_tensor(t1, h2, -3.0, h1, op0=ALU.mult, op1=ALU.add)
            v.scalar_tensor_tensor(t1, h3, 3.0, t1, op0=ALU.mult, op1=ALU.add)
            v.scalar_tensor_tensor(t1, h4, -1.0, t1, op0=ALU.mult, op1=ALU.add)
            v.tensor_scalar(t1, t1, -0.5, 0.0, op0=ALU.mult, op1=ALU.add)
            v.tensor_tensor(Q, t1, Dc, op=ALU.subtract)

            slot = [0]

            def put(g_ap, dests):
                s_hi = OUT[slot[0] % 12]; slot[0] += 1
                s_lo = OUT[slot[0] % 12]; slot[0] += 1
                v.tensor_copy(s_hi, g_ap)
                last = v.tensor_tensor(s_lo, g_ap, s_hi, op=ALU.subtract)
                for dd, which in dests:
                    stores.append((dd, s_hi if which == "hi" else s_lo))
                return last

            v.tensor_tensor(t2, f0, Dc, op=ALU.mult)
            v.tensor_tensor(t2, t2, C, op=ALU.add)
            v.tensor_tensor(t2, t2, f0, op=ALU.mult)
            v.tensor_tensor(t2, t2, Bc, op=ALU.add)
            v.tensor_tensor(t2, t2, f0, op=ALU.mult)
            v.tensor_tensor(t2, t2, h1, op=ALU.add)
            S["p1v"].inc(put(t2, [(("recsC", 0), "hi"), (("recsC", 1), "hi"),
                                  (("recsC", 2), "lo")]))
            v.tensor_scalar(t1, Dc, 3.0, 0.0, op0=ALU.mult, op1=ALU.add)
            v.tensor_tensor(t2, t1, f0, op=ALU.mult)
            v.scalar_tensor_tensor(t2, C, 2.0, t2, op0=ALU.mult, op1=ALU.add)
            v.tensor_tensor(t2, t2, f0, op=ALU.mult)
            v.tensor_tensor(t2, t2, Bc, op=ALU.add)
            v.tensor_scalar(t2, t2, d, 0.0, op0=ALU.mult, op1=ALU.add)
            S["p1v"].inc(put(t2, [(("recsC", 3), "hi"), (("recsC", 4), "hi"),
                                  (("recsC", 5), "lo")]))
            v.tensor_tensor(t2, t1, f0, op=ALU.mult)
            v.tensor_tensor(t2, t2, C, op=ALU.add)
            v.tensor_scalar(t2, t2, d2, 0.0, op0=ALU.mult, op1=ALU.add)
            S["p1v"].inc(put(t2, [(("recsC", 6), "hi"), (("recsC", 7), "hi"),
                                  (("recsC", 8), "lo")]))
            v.tensor_scalar(t2, Dc, d3, 0.0, op0=ALU.mult, op1=ALU.add)
            S["p1v"].inc(put(t2, [(("recsC", 9), "hi"), (("recsC", 10), "hi"),
                                  (("recsC", 11), "lo")]))
            v.tensor_tensor(t1, Q, bet, op=ALU.mult)
            v.tensor_scalar(t2, Pp, d2, 0.0, op0=ALU.mult, op1=ALU.add)
            v.scalar_tensor_tensor(t2, t1, -d3, t2, op0=ALU.mult, op1=ALU.add)
            S["p1v"].inc(put(t2, [(("recsA", 0), "hi"), (("recsA", 1), "lo")]))
            v.tensor_scalar(t2, Q, d3, 0.0, op0=ALU.mult, op1=ALU.add)
            S["p1v"].inc(put(t2, [(("recsA", 2), "hi"), (("recsA", 3), "hi"),
                                  (("recsA", 4), "lo")]))
        block.vector(vops)

        pv_base = S["p1v"].n - 6

        def gstores(g, w=w, stores=stores, pv_base=pv_base):
            per_group = [3, 3, 3, 3, 2, 3]
            done = 0
            for gi, cnt in enumerate(per_group):
                g.wait_ge(S["p1v"].h, pv_base + gi + 1)
                for _ in range(cnt):
                    (recname, row), src = stores[done]
                    done += 1
                    dst = ctx.t[recname][w][:, row, :].rearrange("(p k) j -> p k j", p=128)
                    S["p1s"].inc(g.dma_start(
                        out=dst, in_=src.rearrange("p (k j) -> p k j", k=2)), 16)
        block.gpsimd(gstores)


def _ph1_main(ctx, block):
    t, sb, S, pb = ctx.t, ctx.sb, ctx.S, ctx.pb
    carve = ctx.carve

    NSL = 4
    crb = [carve("arB", i * 4096, BF16, 2048, "p (r j) -> p r j", r=4)[0:12]
           for i in range(NSL)]
    arb = [carve("arB", 16384 + i * 4096, BF16, 2048, "p (r j) -> p r j", r=4)[0:5]
           for i in range(NSL)]
    nbb = [carve("arB", 32768 + i * 4096, BF16, 2048, "p (r j) -> p r j", r=4)[0:2]
           for i in range(NSL)]
    sR = [carve("arC", 24576 + i * 2048, F32, 512) for i in range(2)]
    sS = [carve("arC", 28672 + i * 2048, F32, 512) for i in range(2)]
    sSA = [carve("arC", 32768 + i * 2048, F32, 512) for i in range(2)]
    stg = [carve("arC", 36864 + i * 1024, BF16, 512) for i in range(3)]

    NB = 64
    RPB = 4
    NR = 3 * NB * RPB
    base = dict(wls=S["wls"].n, wmm=S["wmm"].n, wln=S["wln"].n, wact=S["wact"].n,
                wve=S["wve"].n, wst=S["wst"].n, p1s=S["p1s"].n, tabs=S["tabs"].n,
                cg=S["cg"].n, p1v=S["p1v"].n)
    X = [pb[0], pb[1], pb[2]]
    Y = [pb[3], pb[4], pb[5]]

    def rec_of(r):
        w, rr = divmod(r, NB * RPB)
        return w, rr

    def syncs(sy):
        sy.wait_ge(S["p1s"].h, base["p1s"])
        for w in range(3):
            for b in range(NB):
                gb = w * NB + b
                if gb >= NSL:
                    sy.wait_ge(S["wln"].h, base["wln"] + (gb - NSL + 1) * RPB)
                sl = gb % NSL
                S["wls"].inc(sy.dma_start(
                    out=crb[sl], in_=t["recsC"][w, b * RPB:(b + 1) * RPB].rearrange("r a j -> a r j")), 16)
                S["wls"].inc(sy.dma_start(
                    out=arb[sl], in_=t["recsA"][w, b * RPB:(b + 1) * RPB].rearrange("r a j -> a r j")), 16)
                S["wls"].inc(sy.dma_start(
                    out=nbb[sl], in_=t["nbrows"][w, b * RPB:(b + 1) * RPB].rearrange("r a j -> a r j")), 16)
    block.sync(syncs)

    def tes(te):
        te.wait_ge(S["tabs"].h, base["tabs"])
        te.wait_ge(S["cg"].h, base["cg"])
        te.wait_ge(S["p1v"].h, base["p1v"])
        for r in range(NR + 1):
            if r < NR:
                gb = r // RPB
                i = r % RPB
                sl = gb % NSL
                te.wait_ge(S["wls"].h, base["wls"] + 48 * (gb + 1))
                if r >= 3:
                    te.wait_ge(S["wve"].h, base["wve"] + 2 * (r - 2))
                S["wmm"].inc(te.matmul(X[r % 3][:, :], sb["s_ones2"][:],
                                       nbb[sl][:, i, :], start=True, stop=True))
            p = r - 1
            if p >= 0:
                gb = p // RPB
                i = p % RPB
                sl = gb % NSL
                te.wait_ge(S["wact"].h, base["wact"] + 2 * p + 1)
                te.matmul(X[p % 3][:, :], sb["s_cb"][:], crb[sl][:, i, :],
                          start=True, stop=True)
                S["wln"].inc(te.matmul(Y[p % 3][:, :], sb["s_ab"][:], arb[sl][:, i, :],
                                       start=True, stop=True))
    block.tensor(tes)

    def acts(s):
        for r in range(NR):
            s.wait_ge(S["wmm"].h, base["wmm"] + r + 1)
            if r >= 2:
                s.wait_ge(S["wve"].h, base["wve"] + 2 * (r - 2) + 1)
            s.activation(sR[r % 2], X[r % 3][:, :], AF.Relu,
                         bias=sb["s_iota"][:, 0:1], scale=1.0).then_inc(S["wact"].h, 1)
            S["wact"].n += 1
            s.activation(sS[r % 2], sR[r % 2], AF.Square,
                         bias=0.0, scale=1.0).then_inc(S["wact"].h, 1)
            S["wact"].n += 1
    block.scalar(acts)

    def ves(v):
        for r in range(NR):
            v.wait_ge(S["wact"].h, base["wact"] + 2 * (r + 1))
            v.wait_ge(S["wln"].h, base["wln"] + r + 1)
            if r >= 3:
                v.wait_ge(S["wst"].h, base["wst"] + 16 * (r - 2))
            v.tensor_tensor(sSA[r % 2], sS[r % 2], Y[r % 3][:, :], op=ALU.mult).then_inc(S["wve"].h, 1)
            S["wve"].n += 1
            v.tensor_tensor(stg[r % 3], sSA[r % 2], X[r % 3][:, :], op=ALU.add).then_inc(S["wve"].h, 1)
            S["wve"].n += 1
    block.vector(ves)

    def gps(g):
        for r in range(NR):
            w, rec = rec_of(r)
            nic = [16, 16, 64][w]
            wt = [t["wgg"], t["wgu"], t["wgd"]][w]
            ic, ob = rec % nic, rec // nic
            g.wait_ge(S["wve"].h, base["wve"] + 2 * (r + 1))
            dst = wt[4 * ob:4 * ob + 4, ic].rearrange("k p j -> p k j")
            S["wst"].inc(g.dma_start(
                out=dst, in_=stg[r % 3].rearrange("p (k j) -> p k j", k=4)), 16)
    block.gpsimd(gps)


def _ph2_qkv(ctx, block):
    t, sb, S, pb = ctx.t, ctx.sb, ctx.S, ctx.pb
    carve = ctx.carve
    s_bv = ctx.s_bv

    vxT = carve("arB", 0, BF16, 16 * 2048, "p (c j) -> p c j", c=16)
    vws = [carve("arC", i * 4096, BF16, 16 * 128, "p (c j) -> p c j", c=16) for i in range(2)]
    vvw = [carve("arC", 8192 + i * 16384, BF16, 16 * 512, "p (c j) -> p c j", c=16) for i in range(2)]
    stg = [carve("arC", 40960 + i * 1024, BF16, 512) for i in range(3)]

    groups = []
    slabs = []
    for th in range(2):
        for os in range(16):
            slabs.append(("kq", "wkT", th, os))
            for qb in range(4):
                groups.append(dict(kind="K", th=th, os=os, qb=qb, slab=len(slabs) - 1))
        for ob in range(4):
            slabs.append(("v", "wvT", th, ob))
            for tt in range(16):
                groups.append(dict(kind="V", th=th, ob=ob, tt=tt, slab=len(slabs) - 1))
        if th == 0:
            for os in range(16):
                slabs.append(("kq", "wqT", th, os))
                for qb in range(4):
                    groups.append(dict(kind="Q", th=th, os=os, qb=qb, slab=len(slabs) - 1))
    slab_last = {}
    for gi, g in enumerate(groups):
        slab_last[g["slab"]] = gi
    kq_ct = v_ct = 0
    slab_slot = {}
    for si, (knd, _, _, _) in enumerate(slabs):
        if knd == "kq":
            slab_slot[si] = kq_ct % 2
            kq_ct += 1
        else:
            slab_slot[si] = v_ct % 2
            v_ct += 1
    prev_same = {}
    last_of = {}
    for si, (knd, _, _, _) in enumerate(slabs):
        key = (knd, slab_slot[si])
        prev_same[si] = last_of.get(key)
        last_of[key] = si
    th0_last = max(gi for gi, g in enumerate(groups) if g["th"] == 0)

    base = dict(q_xl=S["q_xl"].n, q_wl=S["q_wl"].n, q_mm=S["q_mm"].n,
                q_ev=S["q_ev"].n, q_st=S["q_st"].n, wst=S["wst"].n,
                cg=S["cg"].n, wve=S["wve"].n, wact=S["wact"].n, tabv=S["tabv"].n)

    def gps(g):
        g.wait_ge(S["wve"].h, base["wve"])
        g.wait_ge(S["wst"].h, base["wst"])
        S["q_xl"].inc(g.dma_start(
            out=vxT, in_=t["xT"][:, 0:2048].rearrange("(c p) j -> p c j", p=128)), 16)
        for si, (knd, wname, th, idx) in enumerate(slabs):
            if th == 1 and slabs[si - 1][2] == 0:
                g.wait_ge(S["q_mm"].h, base["q_mm"] + th0_last + 1)
                S["q_xl"].inc(g.dma_start(
                    out=vxT, in_=t["xT"][:, 2048:4096].rearrange("(c p) j -> p c j", p=128)), 16)
            ps = prev_same[si]
            if ps is not None:
                g.wait_ge(S["q_mm"].h, base["q_mm"] + slab_last[ps] + 1)
            if knd == "kq":
                dst = vws[slab_slot[si]]
                src = t[wname][:, idx * 128:(idx + 1) * 128].rearrange("(c p) j -> p c j", p=128)
            else:
                dst = vvw[slab_slot[si]]
                src = t[wname][:, idx * 512:(idx + 1) * 512].rearrange("(c p) j -> p c j", p=128)
            S["q_wl"].inc(g.dma_start(out=dst, in_=src), 16)
    block.gpsimd(gps)

    def tes(te):
        te.wait_ge(S["cg"].h, base["cg"])
        te.wait_ge(S["tabv"].h, base["tabv"])
        te.wait_ge(S["wve"].h, base["wve"])
        te.wait_ge(S["wact"].h, base["wact"])
        for gi, gr in enumerate(groups):
            te.wait_ge(S["q_xl"].h, base["q_xl"] + 16 * (1 if gr["th"] == 0 else 2))
            te.wait_ge(S["q_wl"].h, base["q_wl"] + 16 * (gr["slab"] + 1))
            if gi >= 2:
                te.wait_ge(S["q_ev"].h, base["q_ev"] + gi - 1)
            p = pb[gi % 2]
            if gr["kind"] in ("K", "Q"):
                sl = slab_slot[gr["slab"]]
                for dc in range(NDC):
                    mm = te.matmul(p[:, :], vws[sl][:, dc, :],
                                   vxT[:, dc, gr["qb"] * 512:(gr["qb"] + 1) * 512],
                                   start=(dc == 0), stop=(dc == NDC - 1))
                S["q_mm"].inc(mm)
            else:
                sl = slab_slot[gr["slab"]]
                tt = gr["tt"]
                for dc in range(NDC):
                    te.matmul(p[:, :], vxT[:, dc, tt * 128:(tt + 1) * 128],
                              vvw[sl][:, dc, :], start=(dc == 0), stop=False)
                S["q_mm"].inc(te.matmul(
                    p[:, :], sb["s_ones1b"][:],
                    s_bv[0:1, gr["ob"] * 512:(gr["ob"] + 1) * 512],
                    start=False, stop=True))
    block.tensor(tes)

    def acts(s):
        for gi, gr in enumerate(groups):
            s.wait_ge(S["q_mm"].h, base["q_mm"] + gi + 1)
            if gi >= 3:
                s.wait_ge(S["q_st"].h, base["q_st"] + 16 * (gi - 2))
            p = pb[gi % 2]
            dst = stg[gi % 3]
            if gr["kind"] == "K":
                os = gr["os"]
                S["q_ev"].inc(s.activation(dst, p[:, :], AF.Identity,
                                           bias=sb["tb_kb"][:, os:os + 1],
                                           scale=sb["tb_ks"][:, os:os + 1]))
            elif gr["kind"] == "Q":
                os = gr["os"]
                S["q_ev"].inc(s.activation(dst, p[:, :], AF.Identity,
                                           bias=sb["tb_qb"][:, os:os + 1],
                                           scale=sb["tb_cos"][:, os:os + 1]))
            else:
                S["q_ev"].inc(s.copy(dst, p[:, :]))
    block.scalar(acts)

    def syncs(sy):
        for gi, gr in enumerate(groups):
            sy.wait_ge(S["q_ev"].h, base["q_ev"] + gi + 1)
            src = stg[gi % 3]
            if gr["kind"] == "K":
                dst = t["KT"][gr["os"] * 128:(gr["os"] + 1) * 128,
                              gr["th"] * 2048 + gr["qb"] * 512:
                              gr["th"] * 2048 + (gr["qb"] + 1) * 512]
                S["q_st"].inc(sy.dma_start(out=dst, in_=src), 16)
            elif gr["kind"] == "Q":
                dst = t["QT"][gr["os"] * 128:(gr["os"] + 1) * 128,
                              gr["qb"] * 512:(gr["qb"] + 1) * 512]
                S["q_st"].inc(sy.dma_start(out=dst, in_=src), 16)
            else:
                tc = gr["th"] * 16 + gr["tt"]
                dst = t["Vt"][gr["ob"] * 4:(gr["ob"] + 1) * 4, tc].rearrange("h p j -> p h j")
                S["q_st"].inc(sy.dma_start(out=dst, in_=src.rearrange("p (h j) -> p h j", h=4)), 16)
    block.sync(syncs)


def _ph3_attn(ctx, block):
    t, sb, S, pb = ctx.t, ctx.sb, ctx.S, ctx.pb
    psS1 = ctx.psS1
    carve = ctx.carve

    vK = [carve("arB", i * 8192, BF16, 4096) for i in range(2)]
    vV = [carve("arB", 16384 + i * 8192, BF16, 32 * 128, "p (c j) -> p c j", c=32) for i in range(2)]
    vQ = [carve("arB", 32768 + i * 1024, BF16, 512) for i in range(2)]
    vprob = [carve("arB", 34816 + i * 1024, BF16, 512) for i in range(3)]
    dacc = [carve("arB", 38912 + i * 2048, F32, 512) for i in range(2)]
    vDrb = carve("arB", 43008, F32, 512)
    r_dr = ctx.sb["rowsB"][0:1, 0:512]
    normT = carve("arA", 0, BF16, 16 * 2048, "p (h j) -> p h j", h=16)

    NC = 32
    base = dict(a_kv=S["a_kv"].n, a_q=S["a_q"].n, a_s=S["a_s"].n, a_x=S["a_x"].n,
                a_v=S["a_v"].n, a_d=S["a_d"].n, a_f=S["a_f"].n,
                q_mm=S["q_mm"].n, q_st=S["q_st"].n, q_ev=S["q_ev"].n)

    def syncs(sy):
        sy.wait_ge(S["q_st"].h, base["q_st"])
        for h in range(H):
            if h >= 2:
                sy.wait_ge(S["a_v"].h, base["a_v"] + 128 * (h - 1))
            S["a_kv"].inc(sy.dma_start(out=vK[h % 2], in_=t["KT"][h * 128:(h + 1) * 128, :]), 16)
            S["a_kv"].inc(sy.dma_start(out=vV[h % 2], in_=t["Vt"][h].rearrange("c p j -> p c j")), 16)
            for qb in range(4):
                gq = h * 4 + qb
                if gq >= 2:
                    sy.wait_ge(S["a_s"].h, base["a_s"] + 32 * (gq - 1))
                S["a_q"].inc(sy.dma_start(
                    out=vQ[gq % 2],
                    in_=t["QT"][h * 128:(h + 1) * 128, qb * 512:(qb + 1) * 512]), 16)
    block.sync(syncs)

    def tes(te):
        te.wait_ge(S["q_mm"].h, base["q_mm"])
        te.wait_ge(S["q_ev"].h, base["q_ev"])
        for h in range(H):
            te.wait_ge(S["a_kv"].h, base["a_kv"] + 32 * (h + 1))
            for qb in range(4):
                gq = h * 4 + qb
                te.wait_ge(S["a_q"].h, base["a_q"] + 16 * (gq + 1))
                for c in range(NC):
                    wv = 33 * gq + c - 1
                    if wv >= 1:
                        te.wait_ge(S["a_x"].h, base["a_x"] + wv)
                    S["a_s"].inc(te.matmul(pb[c % 2][:, :],
                                           vK[h % 2][:, c * 128:(c + 1) * 128],
                                           vQ[gq % 2], start=True, stop=True))
                    cc = c - 1
                    if cc >= 0:
                        te.wait_ge(S["a_x"].h, base["a_x"] + 33 * gq + cc + 1)
                        if cc == 0 and gq >= 1:
                            te.wait_ge(S["a_d"].h, base["a_d"] + 34 * gq)
                        S["a_v"].inc(te.matmul(pb[2][:, :], vV[h % 2][:, cc, :],
                                               vprob[cc % 3], start=(cc == 0), stop=False))
                cc = NC - 1
                te.wait_ge(S["a_x"].h, base["a_x"] + 33 * gq + cc + 1)
                S["a_v"].inc(te.matmul(pb[2][:, :], vV[h % 2][:, cc, :],
                                       vprob[cc % 3], start=False, stop=True))
                te.wait_ge(S["a_d"].h, base["a_d"] + 34 * gq + 32)
                S["a_f"].inc(te.matmul(psS1[:, :], sb["s_onescol"][:],
                                       dacc[(NC - 1) % 2], start=True, stop=True))
                te.wait_ge(S["a_d"].h, base["a_d"] + 34 * gq + 33)
                S["a_f"].inc(te.matmul(pb[4][:, :], sb["s_ones1f"][:], r_dr,
                                       start=True, stop=True))
    block.tensor(tes)

    def acts(s):
        for h in range(H):
            for qb in range(4):
                gq = h * 4 + qb
                for c in range(NC):
                    s.wait_ge(S["a_s"].h, base["a_s"] + 32 * gq + c + 1)
                    wv = 32 * gq + c - 2
                    if wv >= 1:
                        s.wait_ge(S["a_v"].h, base["a_v"] + wv)
                    wd = 34 * gq + c - 2
                    if wd >= 1:
                        s.wait_ge(S["a_d"].h, base["a_d"] + wd)
                    S["a_x"].inc(s.activation(vprob[c % 3], pb[c % 2][:, :], AF.Exp,
                                              bias=0.0, scale=1.0))
                s.wait_ge(S["a_f"].h, base["a_f"] + 2 * gq + 2)
                S["a_x"].inc(s.copy(vDrb, pb[4][:, :]))
    block.scalar(acts)

    def ves(v):
        for h in range(H):
            for qb in range(4):
                gq = h * 4 + qb
                for c in range(NC):
                    v.wait_ge(S["a_x"].h, base["a_x"] + 33 * gq + c + 1)
                    if c == 0:
                        S["a_d"].inc(v.tensor_copy(dacc[0], vprob[0]))
                    else:
                        S["a_d"].inc(v.tensor_tensor(dacc[c % 2], dacc[(c - 1) % 2],
                                                     vprob[c % 3], op=ALU.add))
                v.wait_ge(S["a_f"].h, base["a_f"] + 2 * gq + 1)
                S["a_d"].inc(v.reciprocal(r_dr, psS1[0:1, :]))
                v.wait_ge(S["a_x"].h, base["a_x"] + 33 * (gq + 1))
                v.wait_ge(S["a_v"].h, base["a_v"] + 32 * (gq + 1))
                S["a_d"].inc(v.tensor_tensor(normT[:, h, qb * 512:(qb + 1) * 512],
                                             pb[2][:, :], vDrb, op=ALU.mult))
    block.vector(ves)


def _ph4_wo_ln1(ctx, block):
    t, sb, S, pb = ctx.t, ctx.sb, ctx.S, ctx.pb
    psS1, psS2 = ctx.psS1, ctx.psS2
    carve = ctx.carve

    normT = carve("arA", 0, BF16, 16 * 2048, "p (h j) -> p h j", h=16)
    h1T = carve("arA", 0, BF16, 16 * 2048, "p (h j) -> p h j", h=16)
    vWo = carve("arB", 0, BF16, 16 * 2048, "p (c j) -> p c j", c=16)
    vz = [carve("arC", i * 2048, F32, 512) for i in range(2)]
    vxf = [carve("arC", 4096 + i * 2048, F32, 512) for i in range(2)]
    vz1 = [carve("arC", 8192 + i * 2048, F32, 512) for i in range(2)]
    vsq = [carve("arC", 12288 + i * 2048, F32, 512) for i in range(2)]
    vzl = [carve("arC", 32768 + i * 2048, F32, 512) for i in range(2)]
    vh1 = [carve("arC", 36864 + i * 2048, F32, 512) for i in range(2)]
    vtp = [carve("arC", 40960 + i * 2048, F32, 512) for i in range(2)]
    vRR = carve("arC", 45056, F32, 512)
    vMM = carve("arC", 47104, F32, 512)
    r_sz = sb["rowsA"][0:1, :]
    r_sq = sb["rowsA"][32:33, :]
    r_r = sb["rowsB"][0:1, :]
    r_tmp = sb["rowsB"][32:33, :]

    base = dict(o_wl=S["o_wl"].n, o_mm=S["o_mm"].n, o_ev=S["o_ev"].n, o_xl=S["o_xl"].n,
                o_z=S["o_z"].n, o_sq=S["o_sq"].n, o_st=S["o_st"].n, o_sc=S["o_sc"].n,
                o_s2=S["o_s2"].n, a_d=S["a_d"].n, a_x=S["a_x"].n, a_v=S["a_v"].n,
                o_hf=S["o_hf"].n,
                l_zl=S["l_zl"].n, l_ap=S["l_ap"].n, l_hb=S["l_hb"].n,
                g_bc=S["g_bc"].n, g_cp=S["g_cp"].n, lh1=S["lh1"].n, lsq=S["lsq"].n)

    def gidx(qb, ot):
        return qb * 16 + ot

    def gps(g):
        g.wait_ge(S["a_d"].h, base["a_d"])
        S["o_wl"].inc(g.dma_start(
            out=vWo, in_=t["woT"].ap().rearrange("(c p) o -> p c o", p=128)), 16)
        def h1f_stores(qb):
            for dc in range(16):
                ai = qb * 16 + dc
                g.wait_ge(S["l_ap"].h, base["l_ap"] + 2 * ai + 1)
                S["o_hf"].inc(g.dma_start(
                    out=t["h1f"][dc * 128:(dc + 1) * 128, qb * 512:(qb + 1) * 512],
                    in_=vh1[dc % 2]), 16)
        for qb in range(4):
            for ot in range(16):
                gi = gidx(qb, ot)
                g.wait_ge(S["o_z"].h, base["o_z"] + gi + 1)
                S["o_st"].inc(g.dma_start(
                    out=t["z1"][ot * 128:(ot + 1) * 128, qb * 512:(qb + 1) * 512],
                    in_=vz1[ot % 2]), 16)
            if qb >= 1:
                h1f_stores(qb - 1)
        h1f_stores(3)
    block.gpsimd(gps)

    def syncs(sy):
        def z1_reload(qb):
            for dc in range(16):
                ai = qb * 16 + dc
                sy.wait_ge(S["o_st"].h, base["o_st"] + 16 * 16 * (qb + 1))
                if ai >= 2:
                    sy.wait_ge(S["l_ap"].h, base["l_ap"] + 2 * (ai - 1))
                S["l_zl"].inc(sy.dma_start(
                    out=vzl[dc % 2],
                    in_=t["z1"][dc * 128:(dc + 1) * 128, qb * 512:(qb + 1) * 512]), 16)
        for qb in range(4):
            for ot in range(16):
                gi = gidx(qb, ot)
                if gi >= 2:
                    sy.wait_ge(S["o_z"].h, base["o_z"] + gi - 1)
                S["o_xl"].inc(sy.dma_start(
                    out=vxf[ot % 2],
                    in_=t["xT"][ot * 128:(ot + 1) * 128, qb * 512:(qb + 1) * 512]), 16)
            if qb >= 1:
                z1_reload(qb - 1)
        z1_reload(3)
    block.sync(syncs)

    def emit_bcast(te, qb):
        te.wait_ge(S["l_hb"].h, base["l_hb"] + qb + 1)
        if qb >= 1:
            te.wait_ge(S["g_cp"].h, base["g_cp"] + 2 * qb)
        te.matmul(pb[4][:, :], sb["s_ones1f"][:],
                  r_r[0:1, qb * 512:(qb + 1) * 512], start=True, stop=True)
        S["g_bc"].inc(te.matmul(pb[5][:, :], sb["s_ones1f"][:],
                                r_sz[0:1, qb * 512:(qb + 1) * 512],
                                start=True, stop=True))

    def tes(te):
        te.wait_ge(S["o_wl"].h, base["o_wl"] + 16)
        te.wait_ge(S["a_d"].h, base["a_d"])
        te.wait_ge(S["a_x"].h, base["a_x"])
        for qb in range(4):
            for ot in range(16):
                gi = gidx(qb, ot)
                if gi >= 2:
                    te.wait_ge(S["o_ev"].h, base["o_ev"] + gi - 1)
                p = pb[ot % 2]
                for dc in range(NDC):
                    mm = te.matmul(p[:, :], vWo[:, dc, ot * 128:(ot + 1) * 128],
                                   normT[:, dc, qb * 512:(qb + 1) * 512],
                                   start=(dc == 0), stop=(dc == NDC - 1))
                S["o_mm"].inc(mm)
                po = ot - 1
                if po >= 0:
                    gp = gidx(qb, po)
                    te.wait_ge(S["o_z"].h, base["o_z"] + gp + 1)
                    te.wait_ge(S["o_sq"].h, base["o_sq"] + gp + 1)
                    if po == 0:
                        te.wait_ge(S["o_sc"].h, base["o_sc"] + 2 * qb)
                    te.matmul(psS1[:, :], sb["s_onescol"][:], vz1[po % 2],
                              start=(po == 0), stop=False)
                    S["o_s2"].inc(te.matmul(psS2[:, :], sb["s_onescol"][:], vsq[po % 2],
                                            start=(po == 0), stop=False))
            po = 15
            gp = gidx(qb, po)
            te.wait_ge(S["o_z"].h, base["o_z"] + gp + 1)
            te.wait_ge(S["o_sq"].h, base["o_sq"] + gp + 1)
            te.matmul(psS1[:, :], sb["s_onescol"][:], vz1[po % 2], start=False, stop=True)
            S["o_s2"].inc(te.matmul(psS2[:, :], sb["s_onescol"][:], vsq[po % 2],
                                    start=False, stop=True))
            if qb >= 1:
                emit_bcast(te, qb - 1)
        emit_bcast(te, 3)
    block.tensor(tes)

    def acts(s):
        for qb in range(4):
            for ot in range(16):
                gi = gidx(qb, ot)
                s.wait_ge(S["o_mm"].h, base["o_mm"] + gi + 1)
                if gi >= 2:
                    s.wait_ge(S["o_z"].h, base["o_z"] + gi - 1)
                S["o_ev"].inc(s.activation(vz[ot % 2], pb[ot % 2][:, :], AF.Identity,
                                           bias=sb["s_bo"][:, ot:ot + 1], scale=1.0))
                s.wait_ge(S["o_z"].h, base["o_z"] + gi + 1)
                if gi >= 2:
                    s.wait_ge(S["o_s2"].h, base["o_s2"] + gi - 1)
                S["o_sq"].inc(s.activation(vsq[ot % 2], vz1[ot % 2], AF.Square,
                                           bias=0.0, scale=1.0))
            s.wait_ge(S["o_s2"].h, base["o_s2"] + 16 * (qb + 1))
            s.copy(r_sz[0:1, qb * 512:(qb + 1) * 512], psS1[0:1, :])
            S["o_sc"].inc(s.copy(r_sq[0:1, qb * 512:(qb + 1) * 512], psS2[0:1, :]), 2)
            s.wait_ge(S["lh1"].h, base["lh1"] + qb + 1)
            S["lsq"].inc(s.activation(r_tmp[0:1, qb * 512:(qb + 1) * 512],
                                      r_sq[0:1, qb * 512:(qb + 1) * 512], AF.Sqrt,
                                      bias=sb["s_eps"][0:1, 0:1], scale=1.0))
            if qb >= 1:
                s.wait_ge(S["g_bc"].h, base["g_bc"] + qb)
                if qb >= 2:
                    s.wait_ge(S["l_ap"].h, base["l_ap"] + 32 * (qb - 1))
                s.copy(vRR, pb[4][:, :])
                S["g_cp"].inc(s.copy(vMM, pb[5][:, :]), 2)
        s.wait_ge(S["g_bc"].h, base["g_bc"] + 4)
        s.wait_ge(S["l_ap"].h, base["l_ap"] + 32 * 3)
        s.copy(vRR, pb[4][:, :])
        S["g_cp"].inc(s.copy(vMM, pb[5][:, :]), 2)
    block.scalar(acts)

    def emit_apply(v, qb):
        v.wait_ge(S["g_cp"].h, base["g_cp"] + 2 * (qb + 1))
        for dc in range(16):
            ai = qb * 16 + dc
            v.wait_ge(S["l_zl"].h, base["l_zl"] + 16 * (ai + 1))
            if ai >= 2:
                v.wait_ge(S["o_hf"].h, base["o_hf"] + 16 * (ai - 1))
            v.tensor_tensor(vtp[dc % 2], vzl[dc % 2], vRR, op=ALU.mult)
            S["l_ap"].inc(v.tensor_tensor(vh1[dc % 2], vtp[dc % 2], vMM,
                                          op=ALU.subtract))
            S["l_ap"].inc(v.tensor_copy(h1T[:, dc, qb * 512:(qb + 1) * 512],
                                        vh1[dc % 2]))

    def ves(v):
        for qb in range(4):
            for ot in range(16):
                gi = gidx(qb, ot)
                v.wait_ge(S["o_ev"].h, base["o_ev"] + gi + 1)
                v.wait_ge(S["o_xl"].h, base["o_xl"] + 16 * (gi + 1))
                if gi >= 2:
                    v.wait_ge(S["o_sq"].h, base["o_sq"] + gi - 1)
                    v.wait_ge(S["o_st"].h, base["o_st"] + 16 * (gi - 1))
                    v.wait_ge(S["o_s2"].h, base["o_s2"] + gi - 1)
                S["o_z"].inc(v.tensor_tensor(vz1[ot % 2], vz[ot % 2], vxf[ot % 2],
                                             op=ALU.add))
            sl = slice(qb * 512, (qb + 1) * 512)
            v.wait_ge(S["o_sc"].h, base["o_sc"] + 2 * (qb + 1))
            _vs(ctx, v, v.tensor_scalar(r_sz[0:1, sl], r_sz[0:1, sl], float(1.0 / D), 0.0,
                                        op0=ALU.mult, op1=ALU.add))
            _vs(ctx, v, v.tensor_tensor(r_tmp[0:1, sl], r_sz[0:1, sl], r_sz[0:1, sl],
                                        op=ALU.mult))
            S["lh1"].inc(v.scalar_tensor_tensor(r_sq[0:1, sl], r_sq[0:1, sl],
                                                float(1.0 / D), r_tmp[0:1, sl],
                                                op0=ALU.mult, op1=ALU.subtract))
            v.wait_ge(S["lsq"].h, base["lsq"] + qb + 1)
            _vs(ctx, v, v.reciprocal(r_r[0:1, sl], r_tmp[0:1, sl]))
            S["l_hb"].inc(v.tensor_tensor(r_sz[0:1, sl], r_sz[0:1, sl], r_r[0:1, sl],
                                          op=ALU.mult))
            if qb >= 1:
                emit_apply(v, qb - 1)
        emit_apply(v, 3)
    block.vector(ves)


def _ph5_ffn(ctx, block):
    t, sb, S, pb = ctx.t, ctx.sb, ctx.S, ctx.pb
    psS1, psS2 = ctx.psS1, ctx.psS2
    carve = ctx.carve

    h1T = carve("arA", 0, BF16, 16 * 2048, "p (h j) -> p h j", h=16)
    interT = carve("arB", 0, BF16, 64 * 512, "p (c j) -> p c j", c=64)
    gsl = [carve("arC", i * 4096, BF16, 16 * 128, "p (c j) -> p c j", c=16) for i in range(2)]
    usl = [carve("arC", 8192 + i * 4096, BF16, 16 * 128, "p (c j) -> p c j", c=16) for i in range(2)]
    dsl = [carve("arC", 16384 + i * 4096, BF16, 16 * 128, "p (c j) -> p c j", c=16) for i in range(2)]
    vz = [carve("arC", 24576 + i * 2048, F32, 512) for i in range(2)]
    vz2 = [carve("arC", 28672 + i * 2048, F32, 512) for i in range(2)]
    vsq = [carve("arC", 32768 + i * 2048, F32, 512) for i in range(2)]
    vhl = [carve("arC", 36864 + i * 2048, F32, 512) for i in range(2)]
    vsil = [carve("arD", 10240 + i * 1024, BF16, 512) for i in range(2)]
    ozl = [carve("arD", i * 2048, F32, 512) for i in range(2)]
    oot = [carve("arD", 4096 + i * 2048, F32, 512) for i in range(2)]
    otp = [carve("arD", 8192, F32, 512)]
    oRR = carve("arC", 45056, F32, 512)
    oMM = carve("arC", 47104, F32, 512)
    r_sz = sb["rowsA"][0:1, :]
    r_sq = sb["rowsA"][32:33, :]
    r_r = sb["rowsB"][0:1, :]
    r_tmp = sb["rowsB"][32:33, :]

    base = dict(f_wl=S["f_wl"].n, f_mm=S["f_mm"].n, f_sil=S["f_sil"].n,
                f_mu=S["f_mu"].n,
                f_dwl=S["f_dwl"].n, f_dmm=S["f_dmm"].n, f_ev=S["f_ev"].n,
                f_hl=S["f_hl"].n, f_z2=S["f_z2"].n, f_sq=S["f_sq"].n,
                f_st=S["f_st"].n, f_sc=S["f_sc"].n, f_s2=S["f_s2"].n,
                l_ap=S["l_ap"].n, wst=S["wst"].n, o_st=S["o_st"].n, o_hf=S["o_hf"].n,
                o_z=S["o_z"].n, o_s2=S["o_s2"].n, o_sc=S["o_sc"].n,
                g_bc=S["g_bc"].n, g_cp=S["g_cp"].n, l_hb=S["l_hb"].n,
                nh1=S["nh1"].n, nsq=S["nsq"].n, n_hb=S["n_hb"].n,
                g_zl=S["g_zl"].n, g_ap=S["g_ap"].n, g_st=S["g_st"].n)

    def syncs(sy):
        sy.wait_ge(S["wst"].h, base["wst"])
        sy.wait_ge(S["o_z"].h, base["o_z"])
        sy.wait_ge(S["o_s2"].h, base["o_s2"])
        for tb in range(4):
            for it in range(64):
                si = tb * 64 + it
                if si >= 2:
                    sy.wait_ge(S["f_mm"].h, base["f_mm"] + 2 * (si - 1))
                S["f_wl"].inc(sy.dma_start(
                    out=gsl[it % 2], in_=t["wgg"][it].rearrange("c p j -> p c j")), 16)
                S["f_wl"].inc(sy.dma_start(
                    out=usl[it % 2], in_=t["wgu"][it].rearrange("c p j -> p c j")), 16)
            for ot in range(16):
                hi = tb * 16 + ot
                for qq in range(4):
                    dq = ot * 4 + qq
                    di = tb * 64 + dq
                    if di >= 2:
                        sy.wait_ge(S["f_dmm"].h, base["f_dmm"] + di - 1)
                    S["f_dwl"].inc(sy.dma_start(
                        out=dsl[dq % 2],
                        in_=t["wgd"][ot, qq * 16:(qq + 1) * 16].rearrange("c p j -> p c j")), 16)
                if hi == 0:
                    sy.wait_ge(S["o_hf"].h, base["o_hf"])
                    sy.wait_ge(S["l_ap"].h, base["l_ap"])
                if hi >= 2:
                    sy.wait_ge(S["f_z2"].h, base["f_z2"] + hi - 1)
                S["f_hl"].inc(sy.dma_start(
                    out=vhl[ot % 2],
                    in_=t["h1f"][ot * 128:(ot + 1) * 128, tb * 512:(tb + 1) * 512]), 16)
            if tb >= 1:
                _emit_out_sync(ctx, sy, tb - 1, ozl, base)
        _emit_out_sync(ctx, sy, 3, ozl, base)
    block.sync(syncs)

    def tes(te):
        te.wait_ge(S["g_cp"].h, base["g_cp"])
        for tb in range(4):
            for it in range(64):
                si = tb * 64 + it
                te.wait_ge(S["f_wl"].h, base["f_wl"] + 32 * (si + 1))
                if it == 0:
                    te.wait_ge(S["l_ap"].h, base["l_ap"] - 128 + 32 * (tb + 1))
                if tb >= 1 and it == 0:
                    te.wait_ge(S["f_dmm"].h, base["f_dmm"] + 64 * tb)
                if si >= 2:
                    te.wait_ge(S["f_sil"].h, base["f_sil"] + si - 1)
                    te.wait_ge(S["f_mu"].h, base["f_mu"] + si - 1)
                for dc in range(NDC):
                    mm = te.matmul(pb[2 * (si % 2)][:, :], gsl[it % 2][:, dc, :],
                                   h1T[:, dc, tb * 512:(tb + 1) * 512],
                                   start=(dc == 0), stop=(dc == NDC - 1))
                S["f_mm"].inc(mm)
                for dc in range(NDC):
                    mm = te.matmul(pb[1 + 2 * (si % 2)][:, :], usl[it % 2][:, dc, :],
                                   h1T[:, dc, tb * 512:(tb + 1) * 512],
                                   start=(dc == 0), stop=(dc == NDC - 1))
                S["f_mm"].inc(mm)
            if tb >= 1:
                _emit_out_bcast(ctx, te, tb - 1, base)
            for ot in range(16):
                hi = tb * 16 + ot
                te.wait_ge(S["f_mu"].h, base["f_mu"] + 64 * (tb + 1))
                if tb >= 1 and ot <= 1:
                    te.wait_ge(S["g_cp"].h, base["g_cp"] + 2 * tb)
                if hi >= 2:
                    te.wait_ge(S["f_ev"].h, base["f_ev"] + hi - 1)
                p = pb[4 + ot % 2]
                for qq in range(4):
                    di = tb * 64 + ot * 4 + qq
                    te.wait_ge(S["f_dwl"].h, base["f_dwl"] + 16 * (di + 1))
                    for c in range(16):
                        ic = qq * 16 + c
                        mm = te.matmul(p[:, :], dsl[(ot * 4 + qq) % 2][:, c, :],
                                       interT[:, ic, :],
                                       start=(ic == 0), stop=(ic == 63))
                    S["f_dmm"].inc(mm)
                po = ot - 1
                if po >= 0:
                    hp = tb * 16 + po
                    te.wait_ge(S["f_z2"].h, base["f_z2"] + hp + 1)
                    te.wait_ge(S["f_sq"].h, base["f_sq"] + hp + 1)
                    if po == 0:
                        te.wait_ge(S["f_sc"].h, base["f_sc"] + 2 * tb)
                    te.matmul(psS1[:, :], sb["s_onescol"][:], vz2[po % 2],
                              start=(po == 0), stop=False)
                    S["f_s2"].inc(te.matmul(psS2[:, :], sb["s_onescol"][:], vsq[po % 2],
                                            start=(po == 0), stop=False))
            po = 15
            hp = tb * 16 + po
            te.wait_ge(S["f_z2"].h, base["f_z2"] + hp + 1)
            te.wait_ge(S["f_sq"].h, base["f_sq"] + hp + 1)
            te.matmul(psS1[:, :], sb["s_onescol"][:], vz2[po % 2], start=False, stop=True)
            S["f_s2"].inc(te.matmul(psS2[:, :], sb["s_onescol"][:], vsq[po % 2],
                                    start=False, stop=True))
        _emit_out_bcast(ctx, te, 3, base)
    block.tensor(tes)

    def acts(s):
        for tb in range(4):
            for it in range(64):
                si = tb * 64 + it
                s.wait_ge(S["f_mm"].h, base["f_mm"] + 2 * si + 1)
                if si >= 2:
                    s.wait_ge(S["f_mu"].h, base["f_mu"] + si - 1)
                S["f_sil"].inc(s.activation(vsil[si % 2], pb[2 * (si % 2)][:, :],
                                            AF.Silu, bias=0.0, scale=1.0))
            if tb >= 1:
                _emit_out_actcopy(ctx, s, tb - 1, oRR, oMM, base)
            for ot in range(16):
                hi = tb * 16 + ot
                s.wait_ge(S["f_dmm"].h, base["f_dmm"] + 4 * (hi + 1))
                if hi >= 2:
                    s.wait_ge(S["f_z2"].h, base["f_z2"] + hi - 1)
                S["f_ev"].inc(s.activation(vz[ot % 2], pb[4 + ot % 2][:, :], AF.Identity,
                                           bias=0.0, scale=1.0))
                s.wait_ge(S["f_z2"].h, base["f_z2"] + hi + 1)
                if hi >= 2:
                    s.wait_ge(S["f_s2"].h, base["f_s2"] + hi - 1)
                S["f_sq"].inc(s.activation(vsq[ot % 2], vz2[ot % 2], AF.Square,
                                           bias=0.0, scale=1.0))
            s.wait_ge(S["f_s2"].h, base["f_s2"] + 16 * (tb + 1))
            if tb == 0:
                s.wait_ge(S["g_bc"].h, base["g_bc"])
                s.wait_ge(S["l_hb"].h, base["l_hb"])
            s.copy(r_sz[0:1, tb * 512:(tb + 1) * 512], psS1[0:1, :])
            S["f_sc"].inc(s.copy(r_sq[0:1, tb * 512:(tb + 1) * 512], psS2[0:1, :]), 2)
            s.wait_ge(S["nh1"].h, base["nh1"] + tb + 1)
            S["nsq"].inc(s.activation(r_tmp[0:1, tb * 512:(tb + 1) * 512],
                                      r_sq[0:1, tb * 512:(tb + 1) * 512], AF.Sqrt,
                                      bias=sb["s_eps"][0:1, 0:1], scale=1.0))
        _emit_out_actcopy(ctx, s, 3, oRR, oMM, base)
    block.scalar(acts)

    def ves(v):
        for tb in range(4):
            for it in range(64):
                si = tb * 64 + it
                v.wait_ge(S["f_sil"].h, base["f_sil"] + si + 1)
                v.wait_ge(S["f_mm"].h, base["f_mm"] + 2 * si + 2)
                if tb >= 1 and it == 0:
                    v.wait_ge(S["f_dmm"].h, base["f_dmm"] + 64 * tb)
                S["f_mu"].inc(v.tensor_tensor(interT[:, it, :], vsil[si % 2],
                                              pb[1 + 2 * (si % 2)][:, :], op=ALU.mult))
            for ot in range(16):
                hi = tb * 16 + ot
                v.wait_ge(S["f_ev"].h, base["f_ev"] + hi + 1)
                v.wait_ge(S["f_hl"].h, base["f_hl"] + 16 * (hi + 1))
                if hi >= 2:
                    v.wait_ge(S["f_sq"].h, base["f_sq"] + hi - 1)
                    v.wait_ge(S["f_st"].h, base["f_st"] + 16 * (hi - 1))
                    v.wait_ge(S["f_s2"].h, base["f_s2"] + hi - 1)
                S["f_z2"].inc(v.tensor_tensor(vz2[ot % 2], vz[ot % 2], vhl[ot % 2],
                                              op=ALU.add))
            sl = slice(tb * 512, (tb + 1) * 512)
            v.wait_ge(S["f_sc"].h, base["f_sc"] + 2 * (tb + 1))
            _vs(ctx, v, v.tensor_scalar(r_sz[0:1, sl], r_sz[0:1, sl], float(1.0 / D), 0.0,
                                        op0=ALU.mult, op1=ALU.add))
            _vs(ctx, v, v.tensor_tensor(r_tmp[0:1, sl], r_sz[0:1, sl], r_sz[0:1, sl],
                                        op=ALU.mult))
            S["nh1"].inc(v.scalar_tensor_tensor(r_sq[0:1, sl], r_sq[0:1, sl],
                                                float(1.0 / D), r_tmp[0:1, sl],
                                                op0=ALU.mult, op1=ALU.subtract))
            v.wait_ge(S["nsq"].h, base["nsq"] + tb + 1)
            _vs(ctx, v, v.reciprocal(r_r[0:1, sl], r_tmp[0:1, sl]))
            S["n_hb"].inc(v.tensor_tensor(r_sz[0:1, sl], r_sz[0:1, sl], r_r[0:1, sl],
                                          op=ALU.mult))
            if tb >= 1:
                _emit_out_apply(ctx, v, tb - 1, ozl, oot, otp, oRR, oMM, base)
        _emit_out_apply(ctx, v, 3, ozl, oot, otp, oRR, oMM, base)
    block.vector(ves)

    def gps(g):
        for tb in range(4):
            for ot in range(16):
                hi = tb * 16 + ot
                g.wait_ge(S["f_z2"].h, base["f_z2"] + hi + 1)
                S["f_st"].inc(g.dma_start(
                    out=t["z2"][ot * 128:(ot + 1) * 128, tb * 512:(tb + 1) * 512],
                    in_=vz2[ot % 2]), 16)
            if tb >= 1:
                _emit_out_store(ctx, g, tb - 1, oot, base)
        _emit_out_store(ctx, g, 3, oot, base)
        g.wait_ge(S["g_st"].h, base["g_st"] + 16 * 64)
    block.gpsimd(gps)


def _emit_out_sync(ctx, sy, qb, ozl, base):
    S = ctx.S
    t = ctx.t
    for dc in range(16):
        ai = qb * 16 + dc
        sy.wait_ge(S["f_st"].h, base["f_st"] + 16 * 16 * (qb + 1))
        if ai >= 2:
            sy.wait_ge(S["g_ap"].h, base["g_ap"] + 2 * (ai - 1))
        S["g_zl"].inc(sy.dma_start(
            out=ozl[dc % 2],
            in_=t["z2"][dc * 128:(dc + 1) * 128, qb * 512:(qb + 1) * 512]), 16)


def _emit_out_bcast(ctx, te, qb, base):
    S, sb, pb = ctx.S, ctx.sb, ctx.pb
    r_sz = sb["rowsA"][0:1, :]
    r_r = sb["rowsB"][0:1, :]
    te.wait_ge(S["f_ev"].h, base["f_ev"] + 16 * (qb + 1))
    te.wait_ge(S["n_hb"].h, base["n_hb"] + qb + 1)
    if qb >= 1:
        te.wait_ge(S["g_cp"].h, base["g_cp"] + 2 * qb)
    te.matmul(pb[4][:, :], sb["s_ones1f"][:],
              r_r[0:1, qb * 512:(qb + 1) * 512], start=True, stop=True)
    S["g_bc"].inc(te.matmul(pb[5][:, :], sb["s_ones1f"][:],
                            r_sz[0:1, qb * 512:(qb + 1) * 512], start=True, stop=True))


def _emit_out_actcopy(ctx, s, qb, oRR, oMM, base):
    S = ctx.S
    if qb == 0:
        s.wait_ge(S["l_ap"].h, base["l_ap"])
    s.wait_ge(S["g_bc"].h, base["g_bc"] + qb + 1)
    if qb >= 1:
        s.wait_ge(S["g_ap"].h, base["g_ap"] + 32 * qb)
    s.copy(oRR, ctx.pb[4][:, :])
    S["g_cp"].inc(s.copy(oMM, ctx.pb[5][:, :]), 2)


def _emit_out_apply(ctx, v, qb, ozl, oot, otp, oRR, oMM, base):
    S = ctx.S
    v.wait_ge(S["g_cp"].h, base["g_cp"] + 2 * (qb + 1))
    for dc in range(16):
        ai = qb * 16 + dc
        v.wait_ge(S["g_zl"].h, base["g_zl"] + 16 * (ai + 1))
        if ai >= 2:
            v.wait_ge(S["g_st"].h, base["g_st"] + 16 * (ai - 1))
        v.tensor_tensor(otp[0], ozl[dc % 2], oRR, op=ALU.mult)
        S["g_ap"].inc(v.tensor_tensor(oot[dc % 2], otp[0], oMM,
                                      op=ALU.subtract), 2)


def _emit_out_store(ctx, g, qb, oot, base):
    S, t = ctx.S, ctx.t
    for dc in range(16):
        ai = qb * 16 + dc
        g.wait_ge(S["g_ap"].h, base["g_ap"] + 2 * (ai + 1))
        S["g_st"].inc(g.dma_start(
            out=t["outT"][dc * 128:(dc + 1) * 128, qb * 512:(qb + 1) * 512],
            in_=oot[dc % 2]), 16)


def _ph6_out(ctx, block):
    return


def kernel(**inputs):
    hc = _host_consts()
    in_maps = [_core_in_map(c, inputs, hc) for c in range(8)]
    nc, _ = _build()
    res = run_bass_kernel_spmd(nc, in_maps, core_ids=list(range(8)))
    out = np.zeros((B, S, D), np.float32)
    for c in range(8):
        b, half = c // 2, c % 2
        out[b, half * TOK:(half + 1) * TOK, :] = np.asarray(res.results[c]["outT"]).T
    return out
